# version 03
# baseline (speedup 1.0000x reference)
import math
import numpy as np
from contextlib import ExitStack

import concourse.bass as bass
import concourse.mybir as mybir
from concourse.bass_utils import run_bass_kernel_spmd

F32 = mybir.dt.float32
BF16 = mybir.dt.bfloat16
I32 = mybir.dt.int32
AF = mybir.ActivationFunctionType
ALU = mybir.AluOpType

T, CD, NH, KVH, DH, IFF = 2048, 2048, 16, 8, 128, 8192
NT, NCC, NI = 16, 16, 64
GATE_OFF = 6.906768
SCALE = 1.0 / math.sqrt(128.0)
OWN = [[0, 3], [1, 2]]
DBG = False
P1_PARTS = ("k", "g", "v", "q")


def BOF(par, oc):
    return 4 * (OWN[par][oc] + 1)


def build_nc(stages=6):
    nc = bass.Bass("TRN2", target_bir_lowering=False,
                   detect_race_conditions=False)

    d_rank = nc.dram_tensor("rank", [1, 1], I32, kind="ExternalInput").ap()
    d_x = nc.dram_tensor("x_nat", [T, CD], F32, kind="ExternalInput").ap()
    d_xT = nc.dram_tensor("xT_own", [CD, 1024], F32, kind="ExternalInput").ap()
    d_wq = nc.dram_tensor("wq_t", [16, 16, 128, 128], F32, kind="ExternalInput").ap()
    d_wk = nc.dram_tensor("wk_t", [8, 16, 128, 128], F32, kind="ExternalInput").ap()
    d_wv = nc.dram_tensor("wv_r", [16, 128, 1024], F32, kind="ExternalInput").ap()
    d_wg = nc.dram_tensor("wg_r", [16, 128, 8], F32, kind="ExternalInput").ap()
    d_wc = nc.dram_tensor("wc_t", [16, 16, 128, 128], F32, kind="ExternalInput").ap()
    d_bc = nc.dram_tensor("bc_sb", [128, 16], F32, kind="ExternalInput").ap()
    d_gw = nc.dram_tensor("gwt", [64, 16, 128, 128], I32, kind="ExternalInput").ap()
    d_uw = nc.dram_tensor("uwt", [64, 16, 128, 128], I32, kind="ExternalInput").ap()
    d_dw = nc.dram_tensor("dwt", [16, 64, 128, 128], I32, kind="ExternalInput").ap()
    d_sg = nc.dram_tensor("sg_sb", [128, 64], F32, kind="ExternalInput").ap()
    d_su = nc.dram_tensor("su_sb", [128, 64], F32, kind="ExternalInput").ap()
    d_sd = nc.dram_tensor("sd_sb", [128, 16], F32, kind="ExternalInput").ap()
    d_out = nc.dram_tensor("out", [CD, 1024], F32, kind="ExternalOutput").ap()

    sk = {"kind": "ExternalOutput"} if DBG else {}
    d_xnh = nc.dram_tensor("xn_sp_h", [16, 128, 1024], BF16, **sk).ap()
    d_xnl = nc.dram_tensor("xn_sp_l", [16, 128, 1024], BF16, **sk).ap()
    d_qsp = nc.dram_tensor("q_sp", [16, 128, 1024], F32, **sk).ap()
    d_ksp = nc.dram_tensor("k_sp", [8, 128, 2048], F32, **sk).ap()
    d_vsp = nc.dram_tensor("v_sp", [16, 128, 1024], F32, **sk).ap()
    d_ssp = nc.dram_tensor("s_sp", [16, 128, 1024], F32, **sk).ap()
    d_hh = nc.dram_tensor("h_sp_h", [2, 64, 128, 512], BF16, **sk).ap()
    d_hl = nc.dram_tensor("h_sp_l", [2, 64, 128, 512], BF16, **sk).ap()
    d_dbg = (nc.dram_tensor("dbg_gate", [160, 2048], F32, kind="ExternalOutput").ap()
             if DBG else None)

    st0 = ExitStack()
    with st0:
        e = st0.enter_context
        rk_s = e(nc.sbuf_tensor("rk_s", [1, 1], I32))
        id_bf = e(nc.sbuf_tensor("id_bf", [128, 128], BF16))
        id_f = e(nc.sbuf_tensor("id_f", [128, 128], F32))
        ones_col = e(nc.sbuf_tensor("ones_col", [128, 1], F32))
        ones_row = e(nc.sbuf_tensor("ones_row", [1, 128], F32))
        pid = e(nc.sbuf_tensor("pid", [128, 1], F32))
        iot = e(nc.sbuf_tensor("iot", [128, 512], F32))
        pplus = e(nc.sbuf_tensor("pplus", [128, 4], F32))
        gproj = e(nc.sbuf_tensor("gproj", [8, 2048], F32))
        sg_s = e(nc.sbuf_tensor("sg_s", [128, 64], F32))
        su_s = e(nc.sbuf_tensor("su_s", [128, 64], F32))
        sd_s = e(nc.sbuf_tensor("sd_s", [128, 16], F32))
        bc_s = e(nc.sbuf_tensor("bc_s", [128, 16], F32))
        constk = e(nc.sbuf_tensor("constk", [128, 8], F32))
        CONSTS = [1e-5, -GATE_OFF, -1.0, SCALE, -0.5]
        for ci, cv in enumerate(CONSTS):
            nc.const_aps.register(F32, cv, constk[:, ci:ci + 1])

        sem = {}
        for name in ("dRk dXa dXb dWk0 dWk1 dWv dWg dWq0 dWq1 dWc0 dWc1 dCst "
                     "dSk dSv dSq dSxn dSs0 dSs1 dSh0 dSh1 dReK0 dReK1 dReV0 "
                     "dReV1 dReQ0 dReQ1 dReE0 dReE1 dXn3 dC30 dC31 dHb dDw "
                     "dSx0 dSx1 dO0 dO1 sG sA sB sC sD sE "
                     "sTh sTl sFh sFl sXF sPK sPGt sPV sKE sGE sVE sWH sWL sPQ "
                     "sQE sGA sGT sPS sSQ sMK sPY sNE sPD sRC sYN sP2 sSE2 "
                     "sDQ sPF sS3 sHE sPD3 sS4 sOC sDQ4").split():
            sem[name] = e(nc.semaphore(name))
        S = type("S", (), sem)
        dXs = [S.dXa, S.dXb]
        dWks = [S.dWk0, S.dWk1]
        dWqs = [S.dWq0, S.dWq1]
        dWcs = [S.dWc0, S.dWc1]
        dSss = [S.dSs0, S.dSs1]
        dShs = [S.dSh0, S.dSh1]
        dReKs = [S.dReK0, S.dReK1]
        dReVs = [S.dReV0, S.dReV1]
        dReQs = [S.dReQ0, S.dReQ1]
        dReEs = [S.dReE0, S.dReE1]
        dC3s = [S.dC30, S.dC31]
        dSxs = [S.dSx0, S.dSx1]
        dOs = [S.dO0, S.dO1]

        block = e(nc.Block())
        regs = {}

        @block.sync
        def _(sync):
            sync.dma_start(out=rk_s[:], in_=d_rank[:]).then_inc(S.dRk, 16)

        for eng_name in ("tensor", "scalar", "vector", "gpsimd"):
            @getattr(block, eng_name)
            def _(en, _n=eng_name):
                r = en.alloc_register("rk_" + _n)
                regs[_n] = r
                en.wait_ge(S.dRk, 16)
                en.reg_load(r, rk_s[0:1, 0:1])

        @block.gpsimd
        def _(g):
            g.iota(iot[:, :], [[1, 512]], channel_multiplier=0,
                   allow_small_or_imprecise_dtypes=True)
            g.iota(pid[:, :], [[1, 1]], channel_multiplier=1,
                   allow_small_or_imprecise_dtypes=True)
            g.memset(ones_col[:, :], 1.0)
            g.memset(ones_row[:, :], 1.0)
            g.dma_start(out=sg_s[:], in_=d_sg[:]).then_inc(S.dCst, 16)
            g.dma_start(out=su_s[:], in_=d_su[:]).then_inc(S.dCst, 16)
            g.dma_start(out=sd_s[:], in_=d_sd[:]).then_inc(S.dCst, 16)
            g.dma_start(out=bc_s[:], in_=d_bc[:]).then_inc(S.dCst, 16)
            for ci, cv in enumerate(CONSTS):
                g.memset(constk[:, ci:ci + 1], cv)
            g.memset(pplus[:, :], 0.0).then_inc(S.sG, 1)

        @block.vector
        def _(v):
            v.wait_ge(S.sG, 1)
            for r in range(4):
                v.tensor_scalar(pplus[:, r:r + 1], pid[:, 0:1], float(128 * r),
                                None, ALU.add)
            v.tensor_scalar(id_f[:, :], iot[:, 0:128], pid[:, 0:1], None,
                            ALU.is_equal).then_inc(S.sG, 1)
            v.wait_ge(S.sG, 2)
            v.tensor_copy(id_bf[:, :], id_f[:, :]).then_inc(S.sC, 1)

        p0 = ExitStack()
        with p0:
            e0 = p0.enter_context
            xnT_h = e0(nc.sbuf_tensor("xnT_h", [128, 16 * 2048], BF16))
            xnT_l = e0(nc.sbuf_tensor("xnT_l", [128, 16 * 2048], BF16))

            p0t = ExitStack()
            with p0t:
                e0t = p0t.enter_context
                xtok = [e0t(nc.sbuf_tensor(f"xtok{i}", [128, 2048], F32)) for i in range(2)]
                stat6 = e0t(nc.sbuf_tensor("stat6", [128, 24], F32))
                mv = e0t(nc.sbuf_tensor("mv", [128, 2], F32))
                lnv = e0t(nc.sbuf_tensor("lnv", [128, 1], F32))
                rln = e0t(nc.sbuf_tensor("rln", [128, 1], F32))
                nmr = e0t(nc.sbuf_tensor("nmr", [128, 1], F32))
                xh_tok = e0t(nc.sbuf_tensor("xh_tok", [128, 2048], BF16))
                xl_tok = e0t(nc.sbuf_tensor("xl_tok", [128, 2048], BF16))
                xf_tok = e0t(nc.sbuf_tensor("xf_tok", [128, 2048], F32))
                pTh = [e0t(nc.psum_tensor(f"pTh{i}", [128, 1024], BF16)) for i in range(2)]
                pTl = [e0t(nc.psum_tensor(f"pTl{i}", [128, 1024], BF16)) for i in range(2)]

                @block.sync
                def _(sync):
                    for tt in range(NT):
                        if tt >= 2:
                            sync.wait_ge(S.sA, tt - 1)
                            sync.wait_ge(S.sE, tt - 1)
                        sync.dma_start(out=xtok[tt % 2][:],
                                       in_=d_x[tt * 128:(tt + 1) * 128, :]
                                       ).then_inc(dXs[tt % 2], 16)

                @block.vector
                def _(v):
                    for tt in range(NT):
                        v.wait_ge(dXs[tt % 2], 16 * (tt // 2 + 1))
                        for j in range(4):
                            ins = v.bn_stats(stat6[:, 6 * j:6 * j + 6],
                                             xtok[tt % 2][:, 512 * j:512 * (j + 1)])
                        ins.then_inc(S.sA, 1)
                        v.wait_ge(S.sA, 2 * tt + 1)
                        if tt >= 1:
                            v.wait_ge(S.sB, tt)
                        v.bn_aggr(mv[:, :], stat6[:, :]).then_inc(S.sA, 1)
                        v.wait_ge(S.sB, tt + 1)
                        v.tensor_scalar(nmr[:, :], mv[:, 0:1], rln[:, 0:1], -1.0,
                                        ALU.mult, ALU.mult).then_inc(S.sC, 1)
                        v.wait_ge(S.sC, tt + 2)
                        v.wait_ge(S.sD, tt + 1)
                        if tt >= 1:
                            v.wait_ge(S.sTl, 16 * tt)
                        v.tensor_scalar(xf_tok[:, :], xtok[tt % 2][:, :], rln[:, 0:1],
                                        nmr[:, 0:1], ALU.mult, ALU.add)
                        v.tensor_sub(xl_tok[:, :], xf_tok[:, :],
                                     xh_tok[:, :]).then_inc(S.sE, 1)
                        for cc in range(NCC):
                            v.wait_ge(S.sTl, 16 * tt + cc + 1)
                            v.tensor_copy(
                                xnT_l[:, cc * 2048 + tt * 128: cc * 2048 + (tt + 1) * 128],
                                pTl[cc % 2][:, 0:128]).then_inc(S.sFl, 1)

                @block.scalar
                def _(s):
                    s.wait_ge(S.sG, 1)
                    for tt in range(NT):
                        s.wait_ge(S.sA, 2 * (tt + 1))
                        s.activation(lnv[:, :], mv[:, 1:2], AF.Ln, bias=1e-5)
                        s.activation(rln[:, :], lnv[:, :], AF.Exp,
                                     scale=-0.5).then_inc(S.sB, 1)
                        s.wait_ge(S.sC, tt + 2)
                        if tt >= 1:
                            s.wait_ge(S.sTh, 16 * tt)
                        s.activation(xh_tok[:, :], xtok[tt % 2][:, :], AF.Identity,
                                     scale=rln[:, 0:1], bias=nmr[:, 0:1]).then_inc(S.sD, 1)
                        for cc in range(NCC):
                            s.wait_ge(S.sTh, 16 * tt + cc + 1)
                            s.activation(
                                xnT_h[:, cc * 2048 + tt * 128: cc * 2048 + (tt + 1) * 128],
                                pTh[cc % 2][:, 0:128], AF.Copy).then_inc(S.sFh, 1)

                @block.tensor
                def _(t):
                    t.wait_ge(S.sC, 1)
                    for tt in range(NT):
                        t.wait_ge(S.sD, tt + 1)
                        for cc in range(NCC):
                            if 16 * tt + cc >= 2:
                                t.wait_ge(S.sFh, 16 * tt + cc - 1)
                            t.transpose(pTh[cc % 2][:, 0:128],
                                        xh_tok[:, cc * 128:(cc + 1) * 128],
                                        id_bf[:, :]).then_inc(S.sTh, 1)
                        t.wait_ge(S.sE, tt + 1)
                        for cc in range(NCC):
                            if 16 * tt + cc >= 2:
                                t.wait_ge(S.sFl, 16 * tt + cc - 1)
                            t.transpose(pTl[cc % 2][:, 0:128],
                                        xl_tok[:, cc * 128:(cc + 1) * 128],
                                        id_bf[:, :]).then_inc(S.sTl, 1)

                @block.gpsimd
                def _(g):
                    g.wait_ge(S.sFh, 256)
                    g.wait_ge(S.sFl, 256)
                    for par in range(2):
                        ctx = g.If_eq(regs["gpsimd"], 0) if par == 0 else g.Else()
                        with ctx:
                            for oc in range(2):
                                gg = OWN[par][oc]
                                for (src, dst) in ((xnT_h, d_xnh), (xnT_l, d_xnl)):
                                    sap = bass.AP(src, gg * 512,
                                                  [[32768, 128], [2048, 16], [1, 512]])
                                    dap = bass.AP(dst.tensor, oc * 512,
                                                  [[1024, 128], [131072, 16], [1, 512]])
                                    g.dma_start(out=dap, in_=sap).then_inc(S.dSxn, 16)

            if stages < 1:
                _finish(block, nc, d_out, S, [(S.dSxn, 64)], ones_row)
                return nc

            p1 = ExitStack()
            with p1:
                e1 = p1.enter_context
                pb = [e1(nc.psum_tensor(f"pb{i}", [128, 512], F32)) for i in range(8)]
                p1a = ExitStack()
                e1a = p1a.enter_context
                xnf = e1a(nc.sbuf_tensor("xnf", [128, 16 * 512], F32))
                wkb = [e1a(nc.sbuf_tensor("wkb0", [128, 2048], F32))] * 2
                wvb = e1a(nc.sbuf_tensor("wvb", [128, 1024], F32))
                wgb = e1a(nc.sbuf_tensor("wgb", [128, 2048], F32))
                gstage = e1a(nc.sbuf_tensor("gstage", [128, 512], F32))
                ktile = e1a(nc.sbuf_tensor("ktile", [128, 512], F32))
                vtile = e1a(nc.sbuf_tensor("vtile", [128, 1024], F32))

                @block.sync
                def _(sync):
                    sync.wait_ge(S.sTh, 256)
                    sync.wait_ge(S.sTl, 256)
                    sync.wait_ge(S.sE, 16)
                    sync.wait_ge(S.sFh, 256)
                    sync.wait_ge(S.sFl, 256)
                    for g in range(4):
                        if "k" in P1_PARTS:
                            for kvh in range(8):
                                idx = g * 8 + kvh
                                if idx >= 1:
                                    sync.wait_ge(S.sPK, idx)
                                sync.dma_start(
                                    out=bass.AP(wkb[idx % 2], 0, [[2048, 128], [128, 16], [1, 128]]),
                                    in_=bass.AP(d_wk.tensor, kvh * 262144,
                                                [[128, 128], [16384, 16], [1, 128]]),
                                ).then_inc(dWks[idx % 2], 16)
                        if g == 0 and "g" in P1_PARTS:
                            sync.wait_ge(S.sG, 3)
                            sync.dma_start(
                                out=bass.AP(wgb, 0, [[2048, 128], [128, 16], [1, 8]]),
                                in_=bass.AP(d_wg.tensor, 0, [[8, 128], [1024, 16], [1, 8]]),
                            ).then_inc(S.dWg, 16)
                        if "v" in P1_PARTS:
                            for cc in range(NCC):
                                n = 16 * g + cc
                                if n >= 1:
                                    sync.wait_ge(S.sPV, n)
                                sync.dma_start(out=wvb[:],
                                               in_=d_wv[cc, :, :]).then_inc(S.dWv, 16)

                @block.vector
                def _(v):
                    for g in range(4):
                        if g == 0:
                            v.wait_ge(S.sFh, 256)
                            v.wait_ge(S.sFl, 256)
                        else:
                            if "k" in P1_PARTS:
                                v.wait_ge(S.sPK, 8 * g)
                            if "g" in P1_PARTS:
                                v.wait_ge(S.sPGt, g)
                            if "v" in P1_PARTS:
                                v.wait_ge(S.sPV, 16 * g)
                        for cc in range(NCC):
                            v.tensor_add(
                                xnf[:, cc * 512:(cc + 1) * 512],
                                xnT_h[:, cc * 2048 + g * 512: cc * 2048 + g * 512 + 512],
                                xnT_l[:, cc * 2048 + g * 512: cc * 2048 + g * 512 + 512],
                            ).then_inc(S.sXF, 1)
                        for tl in range(4 if "v" in P1_PARTS else 0):
                            iv = 4 * g + tl
                            v.wait_ge(S.sPV, 16 * (g + 1))
                            if iv >= 1:
                                v.wait_ge(S.dSv, 16 * iv)
                            v.tensor_copy(vtile[:, 0:512], pb[2 * tl][:, :])
                            v.tensor_copy(vtile[:, 512:1024],
                                          pb[2 * tl + 1][:, :]).then_inc(S.sVE, 2)

                @block.tensor
                def _(t):
                    for g in range(4):
                        t.wait_ge(S.sXF, 16 * (g + 1))
                        for kvh in range(8 if "k" in P1_PARTS else 0):
                            idx = g * 8 + kvh
                            t.wait_ge(dWks[idx % 2], 16 * (idx // 2 + 1))
                            if idx >= 2:
                                t.wait_ge(S.sKE, idx - 1)
                            for cc in range(NCC):
                                ins = t.matmul(pb[idx % 2][:, :],
                                               wkb[idx % 2][:, cc * 128:(cc + 1) * 128],
                                               xnf[:, cc * 512:(cc + 1) * 512],
                                               start=(cc == 0), stop=(cc == 15))
                            ins.then_inc(S.sPK, 1)
                        if "g" in P1_PARTS:
                            t.wait_ge(S.dWg, 16)
                            t.wait_ge(S.sGE, g)
                            for cc in range(NCC):
                                ins = t.matmul(pb[2][:, :], wgb[:, cc * 128:(cc + 1) * 128],
                                               xnf[:, cc * 512:(cc + 1) * 512],
                                               start=(cc == 0), stop=(cc == 15))
                            ins.then_inc(S.sPGt, 1)
                        if "v" not in P1_PARTS:
                            continue
                        if "k" in P1_PARTS:
                            t.wait_ge(S.sKE, 8 * (g + 1))
                        if "g" in P1_PARTS:
                            t.wait_ge(S.sGE, g + 1)
                        t.wait_ge(S.sVE, 8 * g)
                        for cc in range(NCC):
                            t.wait_ge(S.dWv, 16 * (16 * g + cc + 1))
                            for tl in range(4):
                                lhs = xnf[:, cc * 512 + tl * 128: cc * 512 + (tl + 1) * 128]
                                t.matmul(pb[2 * tl][:, :], lhs, wvb[:, 0:512],
                                         start=(cc == 0), stop=(cc == 15))
                                ins = t.matmul(pb[2 * tl + 1][:, :], lhs, wvb[:, 512:1024],
                                               start=(cc == 0), stop=(cc == 15))
                            ins.then_inc(S.sPV, 1)

                @block.scalar
                def _(s):
                    for g in range(4):
                        for kvh in range(8 if "k" in P1_PARTS else 0):
                            idx = g * 8 + kvh
                            s.wait_ge(S.sPK, idx + 1)
                            if idx >= 1:
                                s.wait_ge(S.dSk, 16 * idx)
                            s.activation(ktile[:, :], pb[idx % 2][:, :],
                                         AF.Copy).then_inc(S.sKE, 1)
                        if "g" in P1_PARTS:
                            s.wait_ge(S.sPGt, g + 1)
                            s.activation(gstage[:, :], pb[2][:, :], AF.Copy)
                            s.activation(gproj[:, g * 512:(g + 1) * 512], gstage[0:8, :],
                                         AF.Copy).then_inc(S.sGE, 1)

                @block.gpsimd
                def _(g_):
                    g_.memset(wgb[:, :], 0.0).then_inc(S.sG, 1)
                    for g in range(4):
                        for kvh in range(8 if "k" in P1_PARTS else 0):
                            idx = g * 8 + kvh
                            g_.wait_ge(S.sKE, idx + 1)
                            g_.dma_start(out=d_ksp[kvh, :, g * 512:(g + 1) * 512],
                                         in_=ktile[:, :]).then_inc(S.dSk, 16)
                        for tl in range(4 if "v" in P1_PARTS else 0):
                            iv = 4 * g + tl
                            g_.wait_ge(S.sVE, 2 * (iv + 1))
                            g_.dma_start(out=d_vsp[iv, :, :],
                                         in_=vtile[:, :]).then_inc(S.dSv, 16)

                p1a.close()

                q1 = ExitStack()
                with q1 if "q" in P1_PARTS else ExitStack():
                  if "q" in P1_PARTS:
                    eq = q1.enter_context
                    wqb = [eq(nc.sbuf_tensor(f"wqb{i}", [128, 2048], F32)) for i in range(2)]
                    wqh = eq(nc.sbuf_tensor("wqh", [128, 2048], BF16))
                    wql = eq(nc.sbuf_tensor("wql", [128, 2048], BF16))
                    qt2 = eq(nc.sbuf_tensor("qt2", [128, 1024], F32))

                    @block.sync
                    def _(sync):
                        if "v" in P1_PARTS:
                            sync.wait_ge(S.sPV, 64)
                        if "k" in P1_PARTS:
                            sync.wait_ge(S.dSk, 512)
                        if "v" in P1_PARTS:
                            sync.wait_ge(S.dSv, 256)
                        for qc in range(16):
                            if qc >= 2:
                                sync.wait_ge(S.sWH, qc - 1)
                                sync.wait_ge(S.sWL, qc - 1)
                            sync.dma_start(
                                out=bass.AP(wqb[qc % 2], 0, [[2048, 128], [128, 16], [1, 128]]),
                                in_=bass.AP(d_wq.tensor, qc * 262144,
                                            [[128, 128], [16384, 16], [1, 128]]),
                            ).then_inc(dWqs[qc % 2], 16)

                    @block.scalar
                    def _(s):
                        for qc in range(16):
                            s.wait_ge(dWqs[qc % 2], 16 * (qc // 2 + 1))
                            if qc >= 1:
                                s.wait_ge(S.sPQ, 2 * qc)
                            s.activation(wqh[:, :], wqb[qc % 2][:, :],
                                         AF.Copy).then_inc(S.sWH, 1)
                            for oc in range(2):
                                s.wait_ge(S.sPQ, 2 * qc + oc + 1)
                                if qc >= 1 and oc == 0:
                                    s.wait_ge(S.dSq, 16 * qc)
                                s.activation(qt2[:, oc * 512:(oc + 1) * 512],
                                             pb[oc][:, :], AF.Copy,
                                             scale=SCALE).then_inc(S.sQE, 1)

                    @block.vector
                    def _(v):
                        for qc in range(16):
                            v.wait_ge(S.sWH, qc + 1)
                            if qc >= 1:
                                v.wait_ge(S.sPQ, 2 * qc)
                            v.tensor_sub(wql[:, :], wqb[qc % 2][:, :],
                                         wqh[:, :]).then_inc(S.sWL, 1)

                    @block.tensor
                    def _(t):
                        t.wait_ge(S.sVE, 32)
                        t.wait_ge(S.sKE, 32)
                        for qc in range(16):
                            t.wait_ge(S.sWH, qc + 1)
                            t.wait_ge(S.sWL, qc + 1)
                            for par in range(2):
                                ctx = t.If_eq(regs["tensor"], 0) if par == 0 else t.Else()
                                with ctx:
                                    for oc in range(2):
                                        if qc >= 1:
                                            t.wait_ge(S.sQE, 2 * (qc - 1) + oc + 1)
                                        gg = OWN[par][oc]
                                        for cc in range(NCC):
                                            co = cc * 2048 + gg * 512
                                            t.matmul(pb[oc][:, :],
                                                     wqh[:, cc * 128:(cc + 1) * 128],
                                                     xnT_h[:, co:co + 512],
                                                     start=(cc == 0), stop=False)
                                            t.matmul(pb[oc][:, :],
                                                     wqh[:, cc * 128:(cc + 1) * 128],
                                                     xnT_l[:, co:co + 512],
                                                     start=False, stop=False)
                                            ins = t.matmul(pb[oc][:, :],
                                                           wql[:, cc * 128:(cc + 1) * 128],
                                                           xnT_h[:, co:co + 512],
                                                           start=False, stop=(cc == 15))
                                        ins.then_inc(S.sPQ, 1)

                    @block.gpsimd
                    def _(g_):
                        for qc in range(16):
                            g_.wait_ge(S.sQE, 2 * (qc + 1))
                            g_.dma_start(out=d_qsp[qc, :, :],
                                         in_=qt2[:, :]).then_inc(S.dSq, 16)

        if stages < 2:
            w = [(S.dSxn, 64)]
            if "k" in P1_PARTS:
                w.append((S.dSk, 512))
            if "v" in P1_PARTS:
                w.append((S.dSv, 256))
            if "q" in P1_PARTS:
                w.append((S.dSq, 256))
            _finish(block, nc, d_out, S, w, ones_row)
            return nc

        s2 = ExitStack()
        with s2:
            e2 = s2.enter_context
            cmask = e2(nc.sbuf_tensor("cmask", [128, 4 * 512], F32))
            allone = e2(nc.sbuf_tensor("allone", [128, 512], F32))
            eps16 = e2(nc.sbuf_tensor("eps16", [16, 2048], F32))
            b2tok = e2(nc.sbuf_tensor("b2tok", [128, 256], F32))
            sg2 = ExitStack()
            eg2 = sg2.enter_context
            ubuf = eg2(nc.sbuf_tensor("ubuf", [8, 2048], F32))
            acc = eg2(nc.sbuf_tensor("acc", [8, 2048], F32))
            tmp8 = eg2(nc.sbuf_tensor("tmp8", [8, 2048], F32))
            lgtokP = eg2(nc.sbuf_tensor("lgtokP", [128, 2048], F32))
            estage2 = eg2(nc.sbuf_tensor("estage2", [128, 512], F32))
            lstage = eg2(nc.sbuf_tensor("lstage", [128, 512], F32))
            logg16 = eg2(nc.sbuf_tensor("logg16", [16, 2048], F32))
            pgx = eg2(nc.psum_tensor("pgx", [128, 512], F32))
            pgc = eg2(nc.psum_tensor("pgc", [128, 512], F32))
            s2.callback(sg2.close)

            @block.gpsimd
            def _(g):
                g.wait_ge(S.sPQ, 32)
                g.wait_ge(S.dSq, 256)
                g.memset(allone[:, :], 1.0)
                g.memset(lgtokP[:, :], 0.0).then_inc(S.sG, 1)

            @block.vector
            def _(v):
                v.wait_ge(S.sG, 4)
                for r in range(4):
                    v.tensor_scalar(cmask[:, r * 512:(r + 1) * 512], iot[:, :],
                                    pplus[:, r:r + 1], None, ALU.is_ge).then_inc(S.sMK, 1)

            if stages < 3:
                _finish(block, nc, d_out, S,
                        [(S.dSxn, 64), (S.dSk, 512), (S.dSv, 256), (S.dSq, 256)],
                        ones_row)
                return nc

            @block.scalar
            def _(s):
                s.wait_ge(S.sGE, 4)
                s.wait_ge(S.sPQ, 32)
                s.wait_ge(S.dSq, 256)
                s.activation(ubuf[:, :], gproj[:, :], AF.Exp, scale=-1.0,
                             bias=-GATE_OFF).then_inc(S.sGA, 1)
                for g in range(4):
                    s.wait_ge(S.sGT, 16 + g + 1)
                    s.activation(estage2[:, :], pgc[:, :], AF.Exp,
                                 scale=-1.0).then_inc(S.sGA, 1)
                for tt in range(16):
                    s.wait_ge(S.sGT, 20 + tt + 1)
                    s.activation(b2tok[:, tt * 16:(tt + 1) * 16], pgx[:, 0:16],
                                 AF.Exp, scale=-0.5).then_inc(S.sGA, 1)

            @block.vector
            def _(v):
                v.wait_ge(S.sGA, 1)
                v.tensor_scalar(acc[:, :], ubuf[:, :], -1.0 / 6.0, 1.0 / 5.0,
                                ALU.mult, ALU.add)
                for k in (4, 3, 2):
                    v.tensor_tensor(tmp8[:, :], ubuf[:, :], acc[:, :], ALU.mult)
                    v.tensor_scalar(acc[:, :], tmp8[:, :], -1.0, 1.0 / k,
                                    ALU.mult, ALU.add)
                v.tensor_tensor(tmp8[:, :], ubuf[:, :], acc[:, :], ALU.mult)
                v.tensor_scalar(acc[:, :], tmp8[:, :], -1.0, 1.0, ALU.mult, ALU.add)
                v.tensor_tensor(tmp8[:, :], ubuf[:, :], acc[:, :], ALU.mult)
                v.tensor_scalar(acc[:, :], tmp8[:, :], -1.0, None,
                                ALU.mult).then_inc(S.sGA, 2)
                for tt in range(16):
                    v.wait_ge(S.sGT, tt + 1)
                    v.tensor_copy(
                        bass.AP(lgtokP, tt * 128, [[2048, 128], [2, 8]]),
                        pgx[:, 0:8]
                    )
                    v.tensor_copy(
                        bass.AP(lgtokP, tt * 128 + 1, [[2048, 128], [2, 8]]),
                        pgx[:, 0:8]
                    ).then_inc(S.sGA, 1)
                for g in range(4):
                    v.wait_ge(S.sGA, 20 + 3 * g)
                    v.tensor_scalar(eps16[:, g * 512:(g + 1) * 512], estage2[0:16, :],
                                    1e-6, None, ALU.mult)
                    v.wait_ge(S.sGT, 16 + g + 1)
                    v.tensor_copy(lstage[:, :], pgc[:, :])
                    v.tensor_copy(logg16[:, g * 512:(g + 1) * 512],
                                  lstage[0:16, :]).then_inc(S.sGA, 2)

            @block.tensor
            def _(t):
                t.wait_ge(S.sPQ, 32)
                for tt in range(16):
                    t.wait_ge(S.sGA, 3)
                    if tt >= 1:
                        t.wait_ge(S.sGA, 3 + tt)
                    t.transpose(pgx[:, 0:8], acc[:, tt * 128:(tt + 1) * 128],
                                id_f[0:8, 0:8]).then_inc(S.sGT, 1)
                for g in range(4):
                    t.wait_ge(S.sGA, 19)
                    if g >= 1:
                        t.wait_ge(S.sGA, 19 + 3 * g)
                    for st in range(4 * g + 4):
                        rhs = (allone[:, :] if st < 4 * g
                               else cmask[:, (st - 4 * g) * 512:(st - 4 * g + 1) * 512])
                        ins = t.matmul(pgc[:, :], lgtokP[:, st * 128:(st + 1) * 128], rhs,
                                       start=(st == 0), stop=(st == 4 * g + 3))
                    ins.then_inc(S.sGT, 1)
                for tt in range(16):
                    t.wait_ge(S.sGA, 31)
                    if tt >= 1:
                        t.wait_ge(S.sGA, 31 + tt)
                    t.transpose(pgx[:, 0:16], logg16[:, tt * 128:(tt + 1) * 128],
                                id_f[0:16, 0:16]).then_inc(S.sGT, 1)

            if DBG:
                @block.gpsimd
                def _(g):
                    g.wait_ge(S.sGA, 47)
                    g.dma_start(out=d_dbg[0:16, :], in_=eps16[:, :]).then_inc(S.dCst, 16)
                    g.dma_start(out=d_dbg[16:144, 0:256], in_=b2tok[:, :]).then_inc(S.dCst, 16)
                    g.dma_start(out=d_dbg[144:152, :], in_=gproj[:, :]).then_inc(S.dCst, 16)
                    g.dma_start(out=d_dbg[152:160, :], in_=acc[:, :]).then_inc(S.dCst, 16)

            sg2.close()

            kbuf = [e2(nc.sbuf_tensor(f"kbuf{i}", [128, 2048], F32)) for i in range(2)]
            vbuf = [e2(nc.sbuf_tensor(f"vbuf{i}", [128, 2048], F32)) for i in range(2)]
            qbuf = [e2(nc.sbuf_tensor(f"qbuf{i}", [128, 1024], F32)) for i in range(2)]
            stA = [e2(nc.sbuf_tensor(f"stA{i}", [128, 512], F32)) for i in range(2)]
            stB = [e2(nc.sbuf_tensor(f"stB{i}", [128, 512], F32)) for i in range(2)]
            epsrow = [e2(nc.sbuf_tensor(f"epsrow{i}", [1, 512], F32)) for i in range(2)]
            rb = e2(nc.sbuf_tensor("rb", [128, 512], F32))
            y_all = e2(nc.sbuf_tensor("y_all", [128, 16 * 1024], F32))
            pst = [e2(nc.psum_tensor(f"pst{i}", [128, 512], F32)) for i in range(2)]
            py = e2(nc.psum_tensor("py", [128, 512], F32))
            pn = e2(nc.psum_tensor("pn", [128, 512], F32))

            @block.gpsimd
            def _(g_):
                g_.wait_ge(S.dSk, 512)
                g_.wait_ge(S.dSv, 256)
                g_.wait_ge(S.dSq, 256)
                g_.wait_ge(S.sGA, 47)
                for kvh in range(KVH):
                    if kvh >= 2:
                        g_.wait_ge(S.sYN, 4 * (kvh - 1))
                    g_.dma_start(out=kbuf[kvh % 2][:],
                                 in_=d_ksp[kvh, :, :]).then_inc(dReKs[kvh % 2], 16)
                    vap = bass.AP(d_vsp.tensor, kvh * 128,
                                  [[1024, 128], [131072, 16], [1, 128]])
                    sap = bass.AP(vbuf[kvh % 2], 0, [[2048, 128], [128, 16], [1, 128]])
                    g_.dma_start(out=sap, in_=vap).then_inc(dReVs[kvh % 2], 16)
                    for j in range(2):
                        qh = 2 * kvh + j
                        if qh >= 2:
                            g_.wait_ge(S.sYN, 2 * (qh - 1))
                        g_.dma_start(out=qbuf[qh % 2][:],
                                     in_=d_qsp[qh, :, :]).then_inc(dReQs[qh % 2], 16)
                        for par in range(2):
                            ctx = g_.If_eq(regs["gpsimd"], 0) if par == 0 else g_.Else()
                            with ctx:
                                for oc in range(2):
                                    gg = OWN[par][oc]
                                    if qh >= 1:
                                        g_.wait_ge(S.sPD, 2 * (qh - 1) + oc + 1)
                                    g_.dma_start(
                                        out=epsrow[oc][:],
                                        in_=eps16[qh:qh + 1, gg * 512:(gg + 1) * 512]
                                    ).then_inc(dReEs[oc], 16)

            def ps_base(par, qh, oc):
                return qh * 20 + (BOF(par, 0) if oc == 1 else 0)

            @block.tensor
            def _(t):
                t.wait_ge(S.sGA, 47)
                for kvh in range(KVH):
                    for j in range(2):
                        qh = 2 * kvh + j
                        for par in range(2):
                            ctx = t.If_eq(regs["tensor"], 0) if par == 0 else t.Else()
                            with ctx:
                                if j == 0:
                                    t.wait_ge(dReKs[kvh % 2], 16 * (kvh // 2 + 1))
                                    t.wait_ge(dReVs[kvh % 2], 16 * (kvh // 2 + 1))
                                t.wait_ge(dReQs[qh % 2], 16 * (qh // 2 + 1))
                                for oc in range(2):
                                    B = BOF(par, oc)
                                    base = ps_base(par, qh, oc)
                                    ocn = 2 * qh + oc

                                    def avn(st):
                                        sel = stB if st >= B - 4 else stA
                                        if st >= B - 4:
                                            t.wait_ge(S.sMK, 4 + 4 * ocn + (st - (B - 4)) + 1)
                                        else:
                                            t.wait_ge(S.sSQ, base + st + 1)
                                        t.matmul(py[:, :],
                                                 vbuf[kvh % 2][:, st * 128:(st + 1) * 128],
                                                 sel[st % 2][:, :],
                                                 start=(st == 0), stop=(st == B - 1))
                                        t.matmul(pn[:, :], allone[:, 0:128],
                                                 sel[st % 2][:, :],
                                                 start=(st == 0), stop=False
                                                 ).then_inc(S.sPY, 1)

                                    if ocn >= 1:
                                        t.wait_ge(S.sYN, ocn)
                                        t.wait_ge(S.sRC, ocn)
                                    for st in range(B):
                                        if base + st >= 2:
                                            t.wait_ge(S.sSQ, base + st - 1)
                                        t.matmul(pst[st % 2][:, :],
                                                 kbuf[kvh % 2][:, st * 128:(st + 1) * 128],
                                                 qbuf[qh % 2][:, oc * 512:(oc + 1) * 512],
                                                 start=True, stop=True
                                                 ).then_inc(S.sPS, 1)
                                        if st >= 1:
                                            avn(st - 1)
                                    avn(B - 1)
                                    t.wait_ge(dReEs[oc], 16 * (qh + 1))
                                    t.matmul(pn[:, :], ones_row[:, :], epsrow[oc][0:1, :],
                                             start=False, stop=True).then_inc(S.sPD, 1)

            @block.scalar
            def _(s):
                for kvh in range(KVH):
                    for j in range(2):
                        qh = 2 * kvh + j
                        for par in range(2):
                            ctx = s.If_eq(regs["scalar"], 0) if par == 0 else s.Else()
                            with ctx:
                                for oc in range(2):
                                    B = BOF(par, oc)
                                    base = ps_base(par, qh, oc)
                                    for st in range(B):
                                        s.wait_ge(S.sPS, base + st + 1)
                                        if base + st >= 2:
                                            s.wait_ge(S.sPY, base + st - 1)
                                        s.activation(
                                            stA[st % 2][:, :], pst[st % 2][:, :],
                                            AF.Square,
                                            scale=b2tok[:, st * 16 + qh: st * 16 + qh + 1],
                                        ).then_inc(S.sSQ, 1)

            @block.vector
            def _(v):
                for kvh in range(KVH):
                    for j in range(2):
                        qh = 2 * kvh + j
                        for par in range(2):
                            ctx = v.If_eq(regs["vector"], 0) if par == 0 else v.Else()
                            with ctx:
                                for oc in range(2):
                                    B = BOF(par, oc)
                                    base = ps_base(par, qh, oc)
                                    ocn = 2 * qh + oc
                                    for st in range(B - 4, B):
                                        r = st - (B - 4)
                                        v.wait_ge(S.sSQ, base + st + 1)
                                        if base + st >= 2:
                                            v.wait_ge(S.sPY, base + st - 1)
                                        v.tensor_tensor(
                                            stB[st % 2][:, :], stA[st % 2][:, :],
                                            cmask[:, r * 512:(r + 1) * 512],
                                            ALU.mult).then_inc(S.sMK, 1)
                                    v.wait_ge(S.sPD, ocn + 1)
                                    v.reciprocal(rb[:, :], pn[:, :]).then_inc(S.sRC, 1)
                                    v.tensor_tensor(
                                        y_all[:, qh * 1024 + oc * 512: qh * 1024 + (oc + 1) * 512],
                                        py[:, :], rb[:, :], ALU.mult).then_inc(S.sYN, 1)

            if stages < 4:
                if DBG:
                    @block.gpsimd
                    def _(g_):
                        g_.wait_ge(S.sYN, 32)
                        for qc in range(16):
                            g_.dma_start(out=d_qsp[qc, :, :],
                                         in_=y_all[:, qc * 1024:(qc + 1) * 1024]
                                         ).then_inc(S.dCst, 16)
                _finish(block, nc, d_out, S, [(S.sYN, 32)], ones_row)
                return nc

            wcb = [e2(nc.sbuf_tensor(f"wcb{i}", [128, 2048], F32)) for i in range(2)]
            stile = [e2(nc.sbuf_tensor(f"stile{i}", [128, 512], F32)) for i in range(2)]
            ps2 = [e2(nc.psum_tensor(f"ps2{i}", [128, 512], F32)) for i in range(2)]

            @block.sync
            def _(sync):
                sync.wait_ge(S.sYN, 32)
                for co in range(16):
                    if co >= 2:
                        sync.wait_ge(S.sP2, 2 * (co - 1))
                    sync.dma_start(
                        out=bass.AP(wcb[co % 2], 0, [[2048, 128], [128, 16], [1, 128]]),
                        in_=bass.AP(d_wc.tensor, co * 262144,
                                    [[128, 128], [16384, 16], [1, 128]]),
                    ).then_inc(dWcs[co % 2], 16)

            @block.tensor
            def _(t):
                t.wait_ge(S.sYN, 32)
                for co in range(16):
                    t.wait_ge(dWcs[co % 2], 16 * (co // 2 + 1))
                    for oc in range(2):
                        n2 = 2 * co + oc
                        if n2 >= 2:
                            t.wait_ge(S.sSE2, n2 - 1)
                        for qc in range(16):
                            ins = t.matmul(ps2[oc][:, :],
                                           wcb[co % 2][:, qc * 128:(qc + 1) * 128],
                                           y_all[:, qc * 1024 + oc * 512: qc * 1024 + (oc + 1) * 512],
                                           start=(qc == 0), stop=(qc == 15))
                        ins.then_inc(S.sP2, 1)

            @block.scalar
            def _(s):
                s.wait_ge(S.dCst, 64)
                for co in range(16):
                    for oc in range(2):
                        n2 = 2 * co + oc
                        s.wait_ge(S.sP2, n2 + 1)
                        if n2 >= 2:
                            s.wait_ge(dSss[n2 % 2], 16 * ((n2 - 2) // 2 + 1))
                        s.activation(stile[n2 % 2][:, :], ps2[oc][:, :], AF.Identity,
                                     bias=bc_s[:, co:co + 1]).then_inc(S.sSE2, 1)

            @block.gpsimd
            def _(g_):
                for co in range(16):
                    for oc in range(2):
                        n2 = 2 * co + oc
                        g_.wait_ge(S.sSE2, n2 + 1)
                        g_.dma_start(out=d_ssp[co, :, oc * 512:(oc + 1) * 512],
                                     in_=stile[n2 % 2][:, :]).then_inc(dSss[n2 % 2], 16)

        if stages < 5:
            _finish(block, nc, d_out, S, [(S.dSs0, 256), (S.dSs1, 256)], ones_row)
            return nc

        s3 = ExitStack()
        with s3:
            s3a = ExitStack()
            s3.callback(s3a.close)
            e3 = s3a.enter_context
            xnoh = e3(nc.sbuf_tensor("xnoh", [128, 16 * 1024], BF16))
            xnol = e3(nc.sbuf_tensor("xnol", [128, 16 * 1024], BF16))
            gwb = [e3(nc.sbuf_tensor(f"gwb{i}", [128, 2048], BF16)) for i in range(2)]
            uwb = [e3(nc.sbuf_tensor(f"uwb{i}", [128, 2048], BF16)) for i in range(2)]
            gwu = [e3(nc.sbuf_tensor(f"gwu{i}", [128, 2048], BF16)) for i in range(2)]
            uwu = [e3(nc.sbuf_tensor(f"uwu{i}", [128, 2048], BF16)) for i in range(2)]
            sgl = e3(nc.sbuf_tensor("sgl", [128, 512], F32))
            uu = e3(nc.sbuf_tensor("uu", [128, 512], F32))
            hh = e3(nc.sbuf_tensor("hh", [128, 512], F32))
            hhi = [e3(nc.sbuf_tensor(f"hhi{i}", [128, 512], BF16)) for i in range(2)]
            hlo = [e3(nc.sbuf_tensor(f"hlo{i}", [128, 512], BF16)) for i in range(2)]
            pA = [e3(nc.psum_tensor(f"pA{i}", [128, 512], F32)) for i in range(2)]
            pB = [e3(nc.psum_tensor(f"pB{i}", [128, 512], F32)) for i in range(2)]

            dre2 = [0]

            @block.gpsimd
            def _(g_):
                g_.wait_ge(S.dSxn, 64)
                g_.wait_ge(S.sP2, 32)
                g_.wait_ge(S.dSs0, 256)
                g_.wait_ge(S.dSs1, 256)
                xap = bass.AP(xnoh, 0, [[16384, 128], [1024, 16], [1, 1024]])
                dap = bass.AP(d_xnh.tensor, 0, [[1024, 128], [131072, 16], [1, 1024]])
                g_.dma_start(out=xap, in_=dap).then_inc(S.dXn3, 16)
                xap = bass.AP(xnol, 0, [[16384, 128], [1024, 16], [1, 1024]])
                dap = bass.AP(d_xnl.tensor, 0, [[1024, 128], [131072, 16], [1, 1024]])
                g_.dma_start(out=xap, in_=dap).then_inc(S.dXn3, 16)
                def h_spill(ii):
                    for oc in range(2):
                        n3 = 2 * ii + oc
                        g_.wait_ge(S.sHE, n3 + 1)
                        g_.dma_start(out=d_hh[oc, ii, :, :],
                                     in_=hhi[n3 % 2][:, :]).then_inc(dShs[n3 % 2], 16)
                        g_.dma_start(out=d_hl[oc, ii, :, :],
                                     in_=hlo[n3 % 2][:, :]).then_inc(dShs[n3 % 2], 16)

                for i in range(NI):
                    if i >= 2:
                        g_.wait_ge(S.sDQ, 2 * (i - 1))
                    g_.dma_start(
                        out=bass.AP(gwb[i % 2], 0, [[2048, 128], [128, 16], [1, 128]]),
                        in_=bass.AP(d_gw.tensor, i * 262144,
                                    [[128, 128], [16384, 16], [1, 128]]),
                    ).then_inc(dC3s[i % 2], 16)
                    g_.dma_start(
                        out=bass.AP(uwb[i % 2], 0, [[2048, 128], [128, 16], [1, 128]]),
                        in_=bass.AP(d_uw.tensor, i * 262144,
                                    [[128, 128], [16384, 16], [1, 128]]),
                    ).then_inc(dC3s[i % 2], 16)
                    if i >= 1:
                        h_spill(i - 1)
                h_spill(NI - 1)

            @block.vector
            def _(v):
                v.wait_ge(S.dCst, 64)
                for i in range(NI):
                    v.wait_ge(dC3s[i % 2], 32 * (i // 2 + 1))
                    if i >= 2:
                        v.wait_ge(S.sPF, 2 * (i - 1))
                    v.tensor_scalar(gwu[i % 2][:, :], gwb[i % 2][:, :], -128.0,
                                    None, ALU.add)
                    v.tensor_scalar(uwu[i % 2][:, :], uwb[i % 2][:, :], -128.0,
                                    None, ALU.add).then_inc(S.sDQ, 2)
                    for oc in range(2):
                        n3 = 2 * i + oc
                        v.wait_ge(S.sS3, n3 + 1)
                        v.tensor_scalar(uu[:, :], pB[oc][:, :], su_s[:, i:i + 1],
                                        None, ALU.mult)
                        v.tensor_tensor(hh[:, :], sgl[:, :], uu[:, :], ALU.mult)
                        if n3 >= 2:
                            v.wait_ge(dShs[n3 % 2], 32 * ((n3 - 2) // 2 + 1))
                        v.tensor_copy(hhi[n3 % 2][:, :], hh[:, :])
                        v.tensor_sub(hlo[n3 % 2][:, :], hh[:, :],
                                     hhi[n3 % 2][:, :]).then_inc(S.sHE, 1)

            @block.tensor
            def _(t):
                t.wait_ge(S.dXn3, 32)
                t.wait_ge(S.sSE2, 32)
                for i in range(NI):
                    t.wait_ge(S.sDQ, 2 * (i + 1))
                    for oc in range(2):
                        n3 = 2 * i + oc
                        if i >= 1:
                            t.wait_ge(S.sS3, 2 * (i - 1) + oc + 1)
                            t.wait_ge(S.sHE, 2 * (i - 1) + oc + 1)
                        for cc in range(NCC):
                            xo = cc * 1024 + oc * 512
                            t.matmul(pA[oc][:, :], gwu[i % 2][:, cc * 128:(cc + 1) * 128],
                                     xnoh[:, xo:xo + 512], start=(cc == 0), stop=False)
                            t.matmul(pA[oc][:, :], gwu[i % 2][:, cc * 128:(cc + 1) * 128],
                                     xnol[:, xo:xo + 512], start=False, stop=(cc == 15))
                        for cc in range(NCC):
                            xo = cc * 1024 + oc * 512
                            t.matmul(pB[oc][:, :], uwu[i % 2][:, cc * 128:(cc + 1) * 128],
                                     xnoh[:, xo:xo + 512], start=(cc == 0), stop=False)
                            ins = t.matmul(pB[oc][:, :], uwu[i % 2][:, cc * 128:(cc + 1) * 128],
                                           xnol[:, xo:xo + 512], start=False, stop=(cc == 15))
                        ins.then_inc(S.sPF, 1)

            @block.scalar
            def _(s):
                s.wait_ge(S.dCst, 64)
                for i in range(NI):
                    for oc in range(2):
                        n3 = 2 * i + oc
                        s.wait_ge(S.sPF, n3 + 1)
                        if n3 >= 1:
                            s.wait_ge(S.sHE, n3)
                        s.activation(sgl[:, :], pA[oc][:, :], AF.Silu,
                                     scale=sg_s[:, i:i + 1]).then_inc(S.sS3, 1)

            if stages < 6:
                _finish(block, nc, d_out, S, [(S.dSh0, 2048), (S.dSh1, 2048)], ones_row)
                return nc

            s3a.close()

            s3b = ExitStack()
            with s3b:
                e4 = s3b.enter_context
                hbh = e4(nc.sbuf_tensor("hbh", [128, 64 * 512], BF16))
                hbl = e4(nc.sbuf_tensor("hbl", [128, 64 * 512], BF16))
                dwb = e4(nc.sbuf_tensor("dwb", [128, 8192], BF16))
                dwu = [e4(nc.sbuf_tensor(f"dwu{i}", [128, 8192], BF16)) for i in range(2)]
                go = e4(nc.sbuf_tensor("go", [128, 512], F32))
                st2 = [e4(nc.sbuf_tensor(f"st2{i}", [128, 512], F32)) for i in range(2)]
                xt2 = [e4(nc.sbuf_tensor(f"xt2{i}", [128, 512], F32)) for i in range(2)]
                ot2 = [e4(nc.sbuf_tensor(f"ot2{i}", [128, 512], F32)) for i in range(2)]
                pD = [e4(nc.psum_tensor(f"pD{i}", [128, 512], F32)) for i in range(2)]

                @block.gpsimd
                def _(g_):
                    g_.wait_ge(S.sPF, 128)
                    for oc in range(2):
                        g_.wait_ge(S.dSh0, 2048)
                        g_.wait_ge(S.dSh1, 2048)
                        if oc == 1:
                            g_.wait_ge(S.sPD3, 16)
                        for (dst, src) in ((hbh, d_hh), (hbl, d_hl)):
                            sap = bass.AP(dst, 0, [[32768, 128], [512, 64], [1, 512]])
                            dap = bass.AP(src.tensor, oc * 64 * 128 * 512,
                                          [[512, 128], [65536, 64], [1, 512]])
                            g_.dma_start(out=sap, in_=dap).then_inc(S.dHb, 16)
                        for co in range(16):
                            n4 = 16 * oc + co
                            if n4 >= 1:
                                g_.wait_ge(S.sDQ4, n4)
                            g_.dma_start(
                                out=bass.AP(dwb, 0, [[8192, 128], [128, 64], [1, 128]]),
                                in_=bass.AP(d_dw.tensor, co * 1048576,
                                            [[128, 128], [16384, 64], [1, 128]]),
                            ).then_inc(S.dDw, 16)
                            if n4 >= 2:
                                g_.wait_ge(S.sOC, n4 - 1)
                            g_.dma_start(out=st2[n4 % 2][:, :],
                                         in_=d_ssp[co, :, oc * 512:(oc + 1) * 512]
                                         ).then_inc(dSxs[n4 % 2], 16)
                            g_.dma_start(
                                out=xt2[n4 % 2][:, :],
                                in_=bass.AP(d_xT.tensor, co * 128 * 1024 + oc * 512,
                                            [[1024, 128], [1, 512]]),
                            ).then_inc(dSxs[n4 % 2], 16)

                @block.vector
                def _(v):
                    v.wait_ge(S.dSs0, 256)
                    v.wait_ge(S.dSs1, 256)
                    for oc in range(2):
                        for co in range(16):
                            n4 = 16 * oc + co
                            v.wait_ge(S.dDw, 16 * (n4 + 1))
                            if n4 >= 2:
                                v.wait_ge(S.sPD3, n4 - 1)
                            v.tensor_scalar(dwu[n4 % 2][:, :], dwb[:, :], -128.0,
                                            None, ALU.add).then_inc(S.sDQ4, 1)
                            v.wait_ge(S.sS4, n4 + 1)
                            v.wait_ge(dSxs[n4 % 2], 32 * (n4 // 2 + 1))
                            if n4 >= 2:
                                v.wait_ge(dOs[n4 % 2], 16 * ((n4 - 2) // 2 + 1))
                            v.tensor_tensor(ot2[n4 % 2][:, :], st2[n4 % 2][:, :],
                                            go[:, :], ALU.mult)
                            v.tensor_add(ot2[n4 % 2][:, :], ot2[n4 % 2][:, :],
                                         xt2[n4 % 2][:, :]).then_inc(S.sOC, 1)

                @block.tensor
                def _(t):
                    t.wait_ge(S.sS3, 128)
                    t.wait_ge(S.sHE, 128)
                    for oc in range(2):
                        t.wait_ge(S.dHb, 32 * (oc + 1))
                        for co in range(16):
                            n4 = 16 * oc + co
                            t.wait_ge(S.sDQ4, n4 + 1)
                            if n4 >= 2:
                                t.wait_ge(S.sS4, n4 - 1)
                            for i in range(NI):
                                t.matmul(pD[n4 % 2][:, :],
                                         dwu[n4 % 2][:, i * 128:(i + 1) * 128],
                                         hbh[:, i * 512:(i + 1) * 512],
                                         start=(i == 0), stop=False)
                                ins = t.matmul(pD[n4 % 2][:, :],
                                               dwu[n4 % 2][:, i * 128:(i + 1) * 128],
                                               hbl[:, i * 512:(i + 1) * 512],
                                               start=False, stop=(i == 63))
                            ins.then_inc(S.sPD3, 1)

                @block.scalar
                def _(s):
                    for oc in range(2):
                        for co in range(16):
                            n4 = 16 * oc + co
                            s.wait_ge(S.sPD3, n4 + 1)
                            if n4 >= 1:
                                s.wait_ge(S.sOC, n4)
                            s.activation(go[:, :], pD[n4 % 2][:, :], AF.Silu,
                                         scale=sd_s[:, co:co + 1]).then_inc(S.sS4, 1)

                @block.sync
                def _(sync):
                    for oc in range(2):
                        for co in range(16):
                            n4 = 16 * oc + co
                            sync.wait_ge(S.sOC, n4 + 1)
                            sync.dma_start(out=d_out[co * 128:(co + 1) * 128,
                                                     oc * 512:(oc + 1) * 512],
                                           in_=ot2[n4 % 2][:, :]).then_inc(dOs[n4 % 2], 16)
                    sync.wait_ge(S.dO0, 256)
                    sync.wait_ge(S.dO1, 256)

    return nc


def _finish(block, nc, d_out, S, waits, src):
    @block.sync
    def _(sync):
        for (sm, v) in waits:
            sync.wait_ge(sm, v)
        sync.dma_start(out=d_out[0:1, 0:128], in_=src[0:1, 0:128]).then_inc(S.dO0, 16)
        sync.wait_ge(S.dO0, 16)


_NC_CACHE = {}


def _get_nc():
    if "nc" not in _NC_CACHE:
        _NC_CACHE["nc"] = build_nc()
    return _NC_CACHE["nc"]


def make_inputs(core, x, Wq, Wk, Wv, Wg, Wc, bc, gw, gs, uw, us, dw, ds):
    b, par = core // 2, core % 2
    own_cols = np.concatenate([np.arange(g * 512, (g + 1) * 512) for g in OWN[par]])
    xb = np.ascontiguousarray(x[b])
    return {
        "rank": np.array([[par]], dtype=np.int32),
        "x_nat": xb,
        "xT_own": np.ascontiguousarray(xb.T[:, own_cols]),
        "wq_t": np.ascontiguousarray(Wq.reshape(16, 128, 16, 128).transpose(2, 0, 1, 3)),
        "wk_t": np.ascontiguousarray(Wk.reshape(16, 128, 8, 128).transpose(2, 0, 1, 3)),
        "wv_r": np.ascontiguousarray(Wv.reshape(16, 128, 1024)),
        "wg_r": np.ascontiguousarray(Wg.reshape(16, 128, 8)),
        "wc_t": np.ascontiguousarray(Wc.reshape(16, 128, 16, 128).transpose(2, 0, 1, 3)),
        "bc_sb": np.ascontiguousarray(bc.reshape(16, 128).T),
        "gwt": np.ascontiguousarray(gw.T.reshape(16, 128, 64, 128).transpose(2, 0, 1, 3)),
        "uwt": np.ascontiguousarray(uw.T.reshape(16, 128, 64, 128).transpose(2, 0, 1, 3)),
        "dwt": np.ascontiguousarray(dw.T.reshape(64, 128, 16, 128).transpose(2, 0, 1, 3)),
        "sg_sb": np.ascontiguousarray(gs.reshape(64, 128).T),
        "su_sb": np.ascontiguousarray(us.reshape(64, 128).T),
        "sd_sb": np.ascontiguousarray(ds.reshape(16, 128).T),
    }


def kernel(L0_attn_hidden_states, L0_attn_Wq, L0_attn_Wk, L0_attn_Wv,
           L0_attn_Wg, L0_attn_Wc, L0_attn_bc, L0_ffn_gate_w,
           L0_ffn_gate_scale, L0_ffn_up_w, L0_ffn_up_scale, L0_ffn_down_w,
           L0_ffn_down_scale):
    x = np.asarray(L0_attn_hidden_states, dtype=np.float32)
    args = (x, np.asarray(L0_attn_Wq, np.float32), np.asarray(L0_attn_Wk, np.float32),
            np.asarray(L0_attn_Wv, np.float32), np.asarray(L0_attn_Wg, np.float32),
            np.asarray(L0_attn_Wc, np.float32), np.asarray(L0_attn_bc, np.float32),
            np.asarray(L0_ffn_gate_w, np.int32), np.asarray(L0_ffn_gate_scale, np.float32),
            np.asarray(L0_ffn_up_w, np.int32), np.asarray(L0_ffn_up_scale, np.float32),
            np.asarray(L0_ffn_down_w, np.int32), np.asarray(L0_ffn_down_scale, np.float32))
    in_maps = [make_inputs(c, *args) for c in range(8)]
    nc = _get_nc()
    res = run_bass_kernel_spmd(nc, in_maps, core_ids=list(range(8)))
    out = np.empty((4, T, CD), dtype=np.float32)
    for c in range(8):
        b, par = c // 2, c % 2
        o = res.results[c]["out"]
        for oc, g in enumerate(OWN[par]):
            out[b, g * 512:(g + 1) * 512, :] = o[:, oc * 512:(oc + 1) * 512].T
    return out


# version 05
# speedup vs baseline: 1.1032x; 1.1032x over previous
import math
import numpy as np
from contextlib import ExitStack

import concourse.bass as bass
import concourse.mybir as mybir
from concourse.bass_utils import run_bass_kernel_spmd

F32 = mybir.dt.float32
BF16 = mybir.dt.bfloat16
I32 = mybir.dt.int32
U8 = mybir.dt.uint8
AF = mybir.ActivationFunctionType
ALU = mybir.AluOpType

T, CD, NH, KVH, DH, IFF = 2048, 2048, 16, 8, 128, 8192
NT, NCC, NI = 16, 16, 64
GATE_OFF = 6.906768
GO_HI = float(np.float32(GATE_OFF))
GO_LO = float(np.float32(GATE_OFF - GO_HI))
SCALE = 1.0 / math.sqrt(128.0)
OWN = [[0, 3], [1, 2]]
DBG = False
P1_PARTS = ("k", "g", "v", "q")


def BOF(par, oc):
    return 4 * (OWN[par][oc] + 1)


def build_nc(stages=6):
    nc = bass.Bass("TRN2", target_bir_lowering=False,
                   detect_race_conditions=False)

    d_rank = nc.dram_tensor("rank", [1, 1], I32, kind="ExternalInput").ap()
    d_x = nc.dram_tensor("x_nat", [T, CD], F32, kind="ExternalInput").ap()
    d_xT = nc.dram_tensor("xT_own", [CD, 1024], F32, kind="ExternalInput").ap()
    d_wq = nc.dram_tensor("wq_t", [16, 16, 128, 128], F32, kind="ExternalInput").ap()
    d_wk = nc.dram_tensor("wk_t", [8, 16, 128, 128], F32, kind="ExternalInput").ap()
    d_wv = nc.dram_tensor("wv_r", [16, 128, 1024], F32, kind="ExternalInput").ap()
    d_wg = nc.dram_tensor("wg_r", [16, 128, 8], F32, kind="ExternalInput").ap()
    d_wc = nc.dram_tensor("wc_t", [16, 16, 128, 128], F32, kind="ExternalInput").ap()
    d_bc = nc.dram_tensor("bc_sb", [128, 16], F32, kind="ExternalInput").ap()
    d_gw = nc.dram_tensor("gwt", [64, 16, 128, 128], U8, kind="ExternalInput").ap()
    d_uw = nc.dram_tensor("uwt", [64, 16, 128, 128], U8, kind="ExternalInput").ap()
    d_dw = nc.dram_tensor("dwt", [16, 64, 128, 128], U8, kind="ExternalInput").ap()
    d_sg = nc.dram_tensor("sg_sb", [128, 64], F32, kind="ExternalInput").ap()
    d_su = nc.dram_tensor("su_sb", [128, 64], F32, kind="ExternalInput").ap()
    d_sd = nc.dram_tensor("sd_sb", [128, 16], F32, kind="ExternalInput").ap()
    d_out = nc.dram_tensor("out", [CD, 1024], F32, kind="ExternalOutput").ap()

    sk = {"kind": "ExternalOutput"} if DBG else {}
    d_xnh = nc.dram_tensor("xn_sp_h", [16, 128, 1024], BF16, **sk).ap()
    d_xnl = nc.dram_tensor("xn_sp_l", [16, 128, 1024], BF16, **sk).ap()
    d_qsp = nc.dram_tensor("q_sp", [16, 128, 1024], F32, **sk).ap()
    d_ksp = nc.dram_tensor("k_sp", [8, 128, 2048], F32, **sk).ap()
    d_vsp = nc.dram_tensor("v_sp", [16, 128, 1024], F32, **sk).ap()
    d_ssp = nc.dram_tensor("s_sp", [16, 128, 1024], F32, **sk).ap()
    d_hh = nc.dram_tensor("h_sp_h", [2, 64, 128, 512], BF16, **sk).ap()
    d_hl = nc.dram_tensor("h_sp_l", [2, 64, 128, 512], BF16, **sk).ap()
    d_dbg = (nc.dram_tensor("dbg_gate", [160, 2048], F32, kind="ExternalOutput").ap()
             if DBG else None)

    st0 = ExitStack()
    with st0:
        e = st0.enter_context
        rk_s = e(nc.sbuf_tensor("rk_s", [1, 1], I32))
        id_bf = e(nc.sbuf_tensor("id_bf", [128, 128], BF16))
        ones_col = e(nc.sbuf_tensor("ones_col", [128, 1], F32))
        ones_row = e(nc.sbuf_tensor("ones_row", [1, 128], F32))
        pid = e(nc.sbuf_tensor("pid", [128, 1], F32))
        sg_s = e(nc.sbuf_tensor("sg_s", [128, 64], F32))
        su_s = e(nc.sbuf_tensor("su_s", [128, 64], F32))
        sd_s = e(nc.sbuf_tensor("sd_s", [128, 16], F32))
        bc_s = e(nc.sbuf_tensor("bc_s", [128, 16], F32))
        constk = e(nc.sbuf_tensor("constk", [128, 8], F32))
        g0 = ExitStack()
        eg0 = g0.enter_context
        id_f = eg0(nc.sbuf_tensor("id_f", [128, 128], F32))
        iot = eg0(nc.sbuf_tensor("iot", [128, 512], F32))
        pplus = eg0(nc.sbuf_tensor("pplus", [128, 4], F32))
        gproj = eg0(nc.sbuf_tensor("gproj", [8, 2048], F32))
        st0.callback(g0.close)
        CONSTS = [1e-5, -GATE_OFF, -1.0, SCALE, -0.5]
        for ci, cv in enumerate(CONSTS):
            nc.const_aps.register(F32, cv, constk[:, ci:ci + 1])

        sem = {}
        for name in ("dRk dXa dXb dWk0 dWk1 dWv dWg dWq0 dWq1 dWc0 dWc1 dCst "
                     "dSk dSv dSq dSxn dSs0 dSs1 dSh0 dSh1 dReK0 dReK1 dReV0 "
                     "dReV1 dReQ0 dReQ1 dReE0 dReE1 dXn3 dC30 dC31 dHb dDw "
                     "dSx0 dSx1 dO0 dO1 sG sA sB sC sD sE "
                     "sTh sTl sFh sFl sXF sPK sPGt sPV sKE sGE sVE sWH sWL sPQ "
                     "sQE sGA sGT sPS sSQ sMK sPY sNE sPD sRC sYN sP2 sSE2 "
                     "sDQ sPF sS3 sHE sPD3 sS4 sOC sDQ4 sGU dDwB sPDh").split():
            sem[name] = e(nc.semaphore(name))
        S = type("S", (), sem)
        dXs = [S.dXa, S.dXb]
        dWks = [S.dWk0, S.dWk1]
        dWqs = [S.dWq0, S.dWq1]
        dWcs = [S.dWc0, S.dWc1]
        dSss = [S.dSs0, S.dSs1]
        dShs = [S.dSh0, S.dSh1]
        dReKs = [S.dReK0, S.dReK1]
        dReVs = [S.dReV0, S.dReV1]
        dReQs = [S.dReQ0, S.dReQ1]
        dReEs = [S.dReE0, S.dReE1]
        dC3s = [S.dC30, S.dC31]
        dSxs = [S.dSx0, S.dSx1]
        dOs = [S.dO0, S.dO1]

        block = e(nc.Block())
        regs = {}

        @block.sync
        def _(sync):
            sync.dma_start(out=rk_s[:], in_=d_rank[:]).then_inc(S.dRk, 16)

        for eng_name in ("tensor", "scalar", "vector", "gpsimd"):
            @getattr(block, eng_name)
            def _(en, _n=eng_name):
                r = en.alloc_register("rk_" + _n)
                regs[_n] = r
                en.wait_ge(S.dRk, 16)
                en.reg_load(r, rk_s[0:1, 0:1])

        @block.gpsimd
        def _(g):
            g.iota(iot[:, :], [[1, 512]], channel_multiplier=0,
                   allow_small_or_imprecise_dtypes=True)
            g.iota(pid[:, :], [[1, 1]], channel_multiplier=1,
                   allow_small_or_imprecise_dtypes=True)
            g.memset(ones_col[:, :], 1.0)
            g.memset(ones_row[:, :], 1.0)
            g.dma_start(out=sg_s[:], in_=d_sg[:]).then_inc(S.dCst, 16)
            g.dma_start(out=su_s[:], in_=d_su[:]).then_inc(S.dCst, 16)
            g.dma_start(out=sd_s[:], in_=d_sd[:]).then_inc(S.dCst, 16)
            g.dma_start(out=bc_s[:], in_=d_bc[:]).then_inc(S.dCst, 16)
            for ci, cv in enumerate(CONSTS):
                g.memset(constk[:, ci:ci + 1], cv)
            g.memset(pplus[:, :], 0.0).then_inc(S.sG, 1)

        @block.vector
        def _(v):
            v.wait_ge(S.sG, 1)
            for r in range(4):
                v.tensor_scalar(pplus[:, r:r + 1], pid[:, 0:1], float(128 * r),
                                None, ALU.add)
            v.tensor_scalar(id_f[:, :], iot[:, 0:128], pid[:, 0:1], None,
                            ALU.is_equal).then_inc(S.sG, 1)
            v.wait_ge(S.sG, 2)
            v.tensor_copy(id_bf[:, :], id_f[:, :]).then_inc(S.sC, 1)

        p0 = ExitStack()
        with p0:
            e0 = p0.enter_context
            xnT_h = e0(nc.sbuf_tensor("xnT_h", [128, 16 * 2048], BF16))
            xnT_l = e0(nc.sbuf_tensor("xnT_l", [128, 16 * 2048], BF16))

            p0t = ExitStack()
            with p0t:
                e0t = p0t.enter_context
                xtok = [e0t(nc.sbuf_tensor(f"xtok{i}", [128, 2048], F32)) for i in range(2)]
                stat6 = e0t(nc.sbuf_tensor("stat6", [128, 24], F32))
                mv = e0t(nc.sbuf_tensor("mv", [128, 2], F32))
                lnv = e0t(nc.sbuf_tensor("lnv", [128, 1], F32))
                rln = e0t(nc.sbuf_tensor("rln", [128, 1], F32))
                nmr = e0t(nc.sbuf_tensor("nmr", [128, 1], F32))
                xh_tok = e0t(nc.sbuf_tensor("xh_tok", [128, 2048], BF16))
                xl_tok = e0t(nc.sbuf_tensor("xl_tok", [128, 2048], BF16))
                xf_tok = e0t(nc.sbuf_tensor("xf_tok", [128, 2048], F32))
                pTh = [e0t(nc.psum_tensor(f"pTh{i}", [128, 1024], BF16)) for i in range(2)]
                pTl = [e0t(nc.psum_tensor(f"pTl{i}", [128, 1024], BF16)) for i in range(2)]

                @block.sync
                def _(sync):
                    for tt in range(NT):
                        if tt >= 2:
                            sync.wait_ge(S.sA, tt - 1)
                            sync.wait_ge(S.sE, tt - 1)
                        sync.dma_start(out=xtok[tt % 2][:],
                                       in_=d_x[tt * 128:(tt + 1) * 128, :]
                                       ).then_inc(dXs[tt % 2], 16)

                @block.vector
                def _(v):
                    for tt in range(NT):
                        v.wait_ge(dXs[tt % 2], 16 * (tt // 2 + 1))
                        for j in range(4):
                            ins = v.bn_stats(stat6[:, 6 * j:6 * j + 6],
                                             xtok[tt % 2][:, 512 * j:512 * (j + 1)])
                        ins.then_inc(S.sA, 1)
                        v.wait_ge(S.sA, 2 * tt + 1)
                        if tt >= 1:
                            v.wait_ge(S.sB, tt)
                        v.bn_aggr(mv[:, :], stat6[:, :]).then_inc(S.sA, 1)
                        v.wait_ge(S.sB, tt + 1)
                        v.tensor_scalar(nmr[:, :], mv[:, 0:1], rln[:, 0:1], -1.0,
                                        ALU.mult, ALU.mult).then_inc(S.sC, 1)
                        v.wait_ge(S.sC, tt + 2)
                        v.wait_ge(S.sD, tt + 1)
                        if tt >= 1:
                            v.wait_ge(S.sTl, 16 * tt)
                        v.tensor_scalar(xf_tok[:, :], xtok[tt % 2][:, :], rln[:, 0:1],
                                        nmr[:, 0:1], ALU.mult, ALU.add)
                        v.tensor_sub(xl_tok[:, :], xf_tok[:, :],
                                     xh_tok[:, :]).then_inc(S.sE, 1)
                        for cc in range(NCC):
                            v.wait_ge(S.sTl, 16 * tt + cc + 1)
                            v.tensor_copy(
                                xnT_l[:, cc * 2048 + tt * 128: cc * 2048 + (tt + 1) * 128],
                                pTl[cc % 2][:, 0:128]).then_inc(S.sFl, 1)

                @block.scalar
                def _(s):
                    s.wait_ge(S.sG, 1)
                    for tt in range(NT):
                        s.wait_ge(S.sA, 2 * (tt + 1))
                        s.activation(lnv[:, :], mv[:, 1:2], AF.Ln, bias=1e-5)
                        s.activation(rln[:, :], lnv[:, :], AF.Exp,
                                     scale=-0.5).then_inc(S.sB, 1)
                        s.wait_ge(S.sC, tt + 2)
                        if tt >= 1:
                            s.wait_ge(S.sTh, 16 * tt)
                        s.activation(xh_tok[:, :], xtok[tt % 2][:, :], AF.Identity,
                                     scale=rln[:, 0:1], bias=nmr[:, 0:1]).then_inc(S.sD, 1)
                        for cc in range(NCC):
                            s.wait_ge(S.sTh, 16 * tt + cc + 1)
                            s.activation(
                                xnT_h[:, cc * 2048 + tt * 128: cc * 2048 + (tt + 1) * 128],
                                pTh[cc % 2][:, 0:128], AF.Copy).then_inc(S.sFh, 1)

                @block.tensor
                def _(t):
                    t.wait_ge(S.sC, 1)
                    for tt in range(NT):
                        t.wait_ge(S.sD, tt + 1)
                        for cc in range(NCC):
                            if 16 * tt + cc >= 2:
                                t.wait_ge(S.sFh, 16 * tt + cc - 1)
                            t.transpose(pTh[cc % 2][:, 0:128],
                                        xh_tok[:, cc * 128:(cc + 1) * 128],
                                        id_bf[:, :]).then_inc(S.sTh, 1)
                        t.wait_ge(S.sE, tt + 1)
                        for cc in range(NCC):
                            if 16 * tt + cc >= 2:
                                t.wait_ge(S.sFl, 16 * tt + cc - 1)
                            t.transpose(pTl[cc % 2][:, 0:128],
                                        xl_tok[:, cc * 128:(cc + 1) * 128],
                                        id_bf[:, :]).then_inc(S.sTl, 1)

                @block.gpsimd
                def _(g):
                    g.wait_ge(S.sFh, 256)
                    g.wait_ge(S.sFl, 256)
                    for par in range(2):
                        ctx = g.If_eq(regs["gpsimd"], 0) if par == 0 else g.Else()
                        with ctx:
                            for oc in range(2):
                                gg = OWN[par][oc]
                                for (src, dst) in ((xnT_h, d_xnh), (xnT_l, d_xnl)):
                                    sap = bass.AP(src, gg * 512,
                                                  [[32768, 128], [2048, 16], [1, 512]])
                                    dap = bass.AP(dst.tensor, oc * 512,
                                                  [[1024, 128], [131072, 16], [1, 512]])
                                    g.dma_start(out=dap, in_=sap).then_inc(S.dSxn, 16)

            if stages < 1:
                _finish(block, nc, d_out, S, [(S.dSxn, 64)], ones_row)
                return nc

            p1 = ExitStack()
            with p1:
                e1 = p1.enter_context
                pb = [e1(nc.psum_tensor(f"pb{i}", [128, 512], F32)) for i in range(8)]
                p1a = ExitStack()
                e1a = p1a.enter_context
                xnf = e1a(nc.sbuf_tensor("xnf", [128, 16 * 512], F32))
                wkb = [e1a(nc.sbuf_tensor("wkb0", [128, 2048], F32))] * 2
                wvb = e1a(nc.sbuf_tensor("wvb", [128, 1024], F32))
                wgb = e1a(nc.sbuf_tensor("wgb", [128, 2048], F32))
                gstage = e1a(nc.sbuf_tensor("gstage", [128, 512], F32))
                ktile = e1a(nc.sbuf_tensor("ktile", [128, 512], F32))
                vtile = e1a(nc.sbuf_tensor("vtile", [128, 1024], F32))

                @block.sync
                def _(sync):
                    sync.wait_ge(S.sTh, 256)
                    sync.wait_ge(S.sTl, 256)
                    sync.wait_ge(S.sE, 16)
                    sync.wait_ge(S.sFh, 256)
                    sync.wait_ge(S.sFl, 256)
                    for g in range(4):
                        if "k" in P1_PARTS:
                            for kvh in range(8):
                                idx = g * 8 + kvh
                                if idx >= 1:
                                    sync.wait_ge(S.sPK, idx)
                                sync.dma_start(
                                    out=bass.AP(wkb[idx % 2], 0, [[2048, 128], [128, 16], [1, 128]]),
                                    in_=bass.AP(d_wk.tensor, kvh * 262144,
                                                [[128, 128], [16384, 16], [1, 128]]),
                                ).then_inc(dWks[idx % 2], 16)
                        if g == 0 and "g" in P1_PARTS:
                            sync.wait_ge(S.sG, 3)
                            sync.dma_start(
                                out=bass.AP(wgb, 0, [[2048, 128], [128, 16], [1, 8]]),
                                in_=bass.AP(d_wg.tensor, 0, [[8, 128], [1024, 16], [1, 8]]),
                            ).then_inc(S.dWg, 16)
                        if "v" in P1_PARTS:
                            for cc in range(NCC):
                                n = 16 * g + cc
                                if n >= 1:
                                    sync.wait_ge(S.sPV, n)
                                sync.dma_start(out=wvb[:],
                                               in_=d_wv[cc, :, :]).then_inc(S.dWv, 16)

                @block.vector
                def _(v):
                    for g in range(4):
                        if g == 0:
                            v.wait_ge(S.sFh, 256)
                            v.wait_ge(S.sFl, 256)
                        else:
                            if "k" in P1_PARTS:
                                v.wait_ge(S.sPK, 8 * g)
                            if "g" in P1_PARTS:
                                v.wait_ge(S.sPGt, g)
                            if "v" in P1_PARTS:
                                v.wait_ge(S.sPV, 16 * g)
                        for cc in range(NCC):
                            v.tensor_add(
                                xnf[:, cc * 512:(cc + 1) * 512],
                                xnT_h[:, cc * 2048 + g * 512: cc * 2048 + g * 512 + 512],
                                xnT_l[:, cc * 2048 + g * 512: cc * 2048 + g * 512 + 512],
                            ).then_inc(S.sXF, 1)
                        for tl in range(4 if "v" in P1_PARTS else 0):
                            iv = 4 * g + tl
                            v.wait_ge(S.sPV, 16 * (g + 1))
                            if iv >= 1:
                                v.wait_ge(S.dSv, 16 * iv)
                            v.tensor_copy(vtile[:, 0:512], pb[2 * tl][:, :])
                            v.tensor_copy(vtile[:, 512:1024],
                                          pb[2 * tl + 1][:, :]).then_inc(S.sVE, 2)

                @block.tensor
                def _(t):
                    for g in range(4):
                        t.wait_ge(S.sXF, 16 * (g + 1))
                        for kvh in range(8 if "k" in P1_PARTS else 0):
                            idx = g * 8 + kvh
                            t.wait_ge(dWks[idx % 2], 16 * (idx // 2 + 1))
                            if idx >= 2:
                                t.wait_ge(S.sKE, idx - 1)
                            for cc in range(NCC):
                                ins = t.matmul(pb[idx % 2][:, :],
                                               wkb[idx % 2][:, cc * 128:(cc + 1) * 128],
                                               xnf[:, cc * 512:(cc + 1) * 512],
                                               start=(cc == 0), stop=(cc == 15))
                            ins.then_inc(S.sPK, 1)
                        if "g" in P1_PARTS:
                            t.wait_ge(S.dWg, 16)
                            t.wait_ge(S.sGE, g)
                            for cc in range(NCC):
                                ins = t.matmul(pb[2][:, :], wgb[:, cc * 128:(cc + 1) * 128],
                                               xnf[:, cc * 512:(cc + 1) * 512],
                                               start=(cc == 0), stop=(cc == 15))
                            ins.then_inc(S.sPGt, 1)
                        if "v" not in P1_PARTS:
                            continue
                        if "k" in P1_PARTS:
                            t.wait_ge(S.sKE, 8 * (g + 1))
                        if "g" in P1_PARTS:
                            t.wait_ge(S.sGE, g + 1)
                        t.wait_ge(S.sVE, 8 * g)
                        for cc in range(NCC):
                            t.wait_ge(S.dWv, 16 * (16 * g + cc + 1))
                            for tl in range(4):
                                lhs = xnf[:, cc * 512 + tl * 128: cc * 512 + (tl + 1) * 128]
                                t.matmul(pb[2 * tl][:, :], lhs, wvb[:, 0:512],
                                         start=(cc == 0), stop=(cc == 15))
                                ins = t.matmul(pb[2 * tl + 1][:, :], lhs, wvb[:, 512:1024],
                                               start=(cc == 0), stop=(cc == 15))
                            ins.then_inc(S.sPV, 1)

                @block.scalar
                def _(s):
                    for g in range(4):
                        for kvh in range(8 if "k" in P1_PARTS else 0):
                            idx = g * 8 + kvh
                            s.wait_ge(S.sPK, idx + 1)
                            if idx >= 1:
                                s.wait_ge(S.dSk, 16 * idx)
                            s.activation(ktile[:, :], pb[idx % 2][:, :],
                                         AF.Copy).then_inc(S.sKE, 1)
                        if "g" in P1_PARTS:
                            s.wait_ge(S.sPGt, g + 1)
                            s.activation(gstage[:, :], pb[2][:, :], AF.Copy)
                            s.activation(gproj[:, g * 512:(g + 1) * 512], gstage[0:8, :],
                                         AF.Copy).then_inc(S.sGE, 1)

                @block.gpsimd
                def _(g_):
                    g_.memset(wgb[:, :], 0.0).then_inc(S.sG, 1)
                    for g in range(4):
                        for kvh in range(8 if "k" in P1_PARTS else 0):
                            idx = g * 8 + kvh
                            g_.wait_ge(S.sKE, idx + 1)
                            g_.dma_start(out=d_ksp[kvh, :, g * 512:(g + 1) * 512],
                                         in_=ktile[:, :]).then_inc(S.dSk, 16)
                        for tl in range(4 if "v" in P1_PARTS else 0):
                            iv = 4 * g + tl
                            g_.wait_ge(S.sVE, 2 * (iv + 1))
                            g_.dma_start(out=d_vsp[iv, :, :],
                                         in_=vtile[:, :]).then_inc(S.dSv, 16)

                p1a.close()

                q1 = ExitStack()
                with q1 if "q" in P1_PARTS else ExitStack():
                  if "q" in P1_PARTS:
                    eq = q1.enter_context
                    wqb = [eq(nc.sbuf_tensor(f"wqb{i}", [128, 2048], F32)) for i in range(2)]
                    wqh = eq(nc.sbuf_tensor("wqh", [128, 2048], BF16))
                    wql = eq(nc.sbuf_tensor("wql", [128, 2048], BF16))
                    qt2 = eq(nc.sbuf_tensor("qt2", [128, 1024], F32))

                    @block.sync
                    def _(sync):
                        if "v" in P1_PARTS:
                            sync.wait_ge(S.sPV, 64)
                        if "k" in P1_PARTS:
                            sync.wait_ge(S.dSk, 512)
                        if "v" in P1_PARTS:
                            sync.wait_ge(S.dSv, 256)
                        for qc in range(16):
                            if qc >= 2:
                                sync.wait_ge(S.sWH, qc - 1)
                                sync.wait_ge(S.sWL, qc - 1)
                            sync.dma_start(
                                out=bass.AP(wqb[qc % 2], 0, [[2048, 128], [128, 16], [1, 128]]),
                                in_=bass.AP(d_wq.tensor, qc * 262144,
                                            [[128, 128], [16384, 16], [1, 128]]),
                            ).then_inc(dWqs[qc % 2], 16)

                    @block.scalar
                    def _(s):
                        for qc in range(16):
                            s.wait_ge(dWqs[qc % 2], 16 * (qc // 2 + 1))
                            if qc >= 1:
                                s.wait_ge(S.sPQ, 2 * qc)
                            s.activation(wqh[:, :], wqb[qc % 2][:, :],
                                         AF.Copy).then_inc(S.sWH, 1)
                            for oc in range(2):
                                s.wait_ge(S.sPQ, 2 * qc + oc + 1)
                                if qc >= 1 and oc == 0:
                                    s.wait_ge(S.dSq, 16 * qc)
                                s.activation(qt2[:, oc * 512:(oc + 1) * 512],
                                             pb[oc][:, :], AF.Copy,
                                             scale=SCALE).then_inc(S.sQE, 1)

                    @block.vector
                    def _(v):
                        for qc in range(16):
                            v.wait_ge(S.sWH, qc + 1)
                            if qc >= 1:
                                v.wait_ge(S.sPQ, 2 * qc)
                            v.tensor_sub(wql[:, :], wqb[qc % 2][:, :],
                                         wqh[:, :]).then_inc(S.sWL, 1)

                    @block.tensor
                    def _(t):
                        t.wait_ge(S.sVE, 32)
                        t.wait_ge(S.sKE, 32)
                        for qc in range(16):
                            t.wait_ge(S.sWH, qc + 1)
                            t.wait_ge(S.sWL, qc + 1)
                            for par in range(2):
                                ctx = t.If_eq(regs["tensor"], 0) if par == 0 else t.Else()
                                with ctx:
                                    for oc in range(2):
                                        if qc >= 1:
                                            t.wait_ge(S.sQE, 2 * (qc - 1) + oc + 1)
                                        gg = OWN[par][oc]
                                        for cc in range(NCC):
                                            co = cc * 2048 + gg * 512
                                            t.matmul(pb[oc][:, :],
                                                     wqh[:, cc * 128:(cc + 1) * 128],
                                                     xnT_h[:, co:co + 512],
                                                     start=(cc == 0), stop=False)
                                            t.matmul(pb[oc][:, :],
                                                     wqh[:, cc * 128:(cc + 1) * 128],
                                                     xnT_l[:, co:co + 512],
                                                     start=False, stop=False)
                                            ins = t.matmul(pb[oc][:, :],
                                                           wql[:, cc * 128:(cc + 1) * 128],
                                                           xnT_h[:, co:co + 512],
                                                           start=False, stop=(cc == 15))
                                        ins.then_inc(S.sPQ, 1)

                    @block.gpsimd
                    def _(g_):
                        for qc in range(16):
                            g_.wait_ge(S.sQE, 2 * (qc + 1))
                            g_.dma_start(out=d_qsp[qc, :, :],
                                         in_=qt2[:, :]).then_inc(S.dSq, 16)

        if stages < 2:
            w = [(S.dSxn, 64)]
            if "k" in P1_PARTS:
                w.append((S.dSk, 512))
            if "v" in P1_PARTS:
                w.append((S.dSv, 256))
            if "q" in P1_PARTS:
                w.append((S.dSq, 256))
            _finish(block, nc, d_out, S, w, ones_row)
            return nc

        s2 = ExitStack()
        with s2:
            e2 = s2.enter_context
            cmask = e2(nc.sbuf_tensor("cmask", [128, 4 * 512], F32))
            allone = e2(nc.sbuf_tensor("allone", [128, 512], F32))
            eps16 = e2(nc.sbuf_tensor("eps16", [16, 2048], F32))
            b2tok = e2(nc.sbuf_tensor("b2tok", [128, 256], F32))
            sg2 = ExitStack()
            eg2 = sg2.enter_context
            ubuf = eg2(nc.sbuf_tensor("ubuf", [8, 2048], F32))
            gbuf = eg2(nc.sbuf_tensor("gbuf", [8, 2048], F32))
            acc = eg2(nc.sbuf_tensor("acc", [8, 2048], F32))
            tmp8 = eg2(nc.sbuf_tensor("tmp8", [8, 2048], F32))
            lgtokP = eg2(nc.sbuf_tensor("lgtokP", [128, 2048], F32))
            estage2 = eg2(nc.sbuf_tensor("estage2", [128, 512], F32))
            lstage = eg2(nc.sbuf_tensor("lstage", [128, 512], F32))
            logg16 = eg2(nc.sbuf_tensor("logg16", [16, 2048], F32))
            pgx = eg2(nc.psum_tensor("pgx", [128, 512], F32))
            pgc = eg2(nc.psum_tensor("pgc", [128, 512], F32))
            s2.callback(sg2.close)

            @block.gpsimd
            def _(g):
                g.wait_ge(S.sPQ, 32)
                g.wait_ge(S.dSq, 256)
                g.memset(allone[:, :], 1.0)
                g.memset(lgtokP[:, :], 0.0).then_inc(S.sG, 1)

            @block.vector
            def _(v):
                v.wait_ge(S.sG, 4)
                for r in range(4):
                    v.tensor_scalar(cmask[:, r * 512:(r + 1) * 512], iot[:, :],
                                    pplus[:, r:r + 1], None, ALU.is_ge).then_inc(S.sMK, 1)

            if stages < 3:
                _finish(block, nc, d_out, S,
                        [(S.dSxn, 64), (S.dSk, 512), (S.dSv, 256), (S.dSq, 256)],
                        ones_row)
                return nc

            @block.scalar
            def _(s):
                s.wait_ge(S.sGA, 1)
                s.activation(ubuf[:, :], gbuf[:, :], AF.Exp,
                             scale=-1.0).then_inc(S.sGA, 1)
                for g in range(4):
                    s.wait_ge(S.sGT, 16 + g + 1)
                    s.activation(estage2[:, :], pgc[:, :], AF.Exp,
                                 scale=-1.0).then_inc(S.sGA, 1)
                for tt in range(16):
                    s.wait_ge(S.sGT, 20 + tt + 1)
                    s.activation(b2tok[:, tt * 16:(tt + 1) * 16], pgx[:, 0:16],
                                 AF.Exp, scale=-0.5).then_inc(S.sGA, 1)

            @block.vector
            def _(v):
                v.wait_ge(S.sGE, 4)
                v.wait_ge(S.sPQ, 32)
                v.wait_ge(S.dSq, 256)
                v.tensor_scalar(gbuf[:, :], gproj[:, :], GO_HI, GO_LO,
                                ALU.add, ALU.add).then_inc(S.sGA, 1)
                v.wait_ge(S.sGA, 2)
                v.tensor_scalar(acc[:, :], ubuf[:, :], -1.0 / 6.0, 1.0 / 5.0,
                                ALU.mult, ALU.add)
                for k in (4, 3, 2):
                    v.tensor_tensor(tmp8[:, :], ubuf[:, :], acc[:, :], ALU.mult)
                    v.tensor_scalar(acc[:, :], tmp8[:, :], -1.0, 1.0 / k,
                                    ALU.mult, ALU.add)
                v.tensor_tensor(tmp8[:, :], ubuf[:, :], acc[:, :], ALU.mult)
                v.tensor_scalar(acc[:, :], tmp8[:, :], -1.0, 1.0, ALU.mult, ALU.add)
                v.tensor_tensor(tmp8[:, :], ubuf[:, :], acc[:, :], ALU.mult)
                v.tensor_scalar(acc[:, :], tmp8[:, :], -1.0, None,
                                ALU.mult).then_inc(S.sGA, 2)
                for tt in range(16):
                    v.wait_ge(S.sGT, tt + 1)
                    v.tensor_copy(
                        bass.AP(lgtokP, tt * 128, [[2048, 128], [2, 8]]),
                        pgx[:, 0:8]
                    )
                    v.tensor_copy(
                        bass.AP(lgtokP, tt * 128 + 1, [[2048, 128], [2, 8]]),
                        pgx[:, 0:8]
                    ).then_inc(S.sGA, 1)
                for g in range(4):
                    v.wait_ge(S.sGA, 21 + 3 * g)
                    v.tensor_scalar(eps16[:, g * 512:(g + 1) * 512], estage2[0:16, :],
                                    1e-6, None, ALU.mult)
                    v.wait_ge(S.sGT, 16 + g + 1)
                    v.tensor_copy(lstage[:, :], pgc[:, :])
                    v.tensor_copy(logg16[:, g * 512:(g + 1) * 512],
                                  lstage[0:16, :]).then_inc(S.sGA, 2)

            @block.tensor
            def _(t):
                t.wait_ge(S.sPQ, 32)
                for tt in range(16):
                    t.wait_ge(S.sGA, 4)
                    if tt >= 1:
                        t.wait_ge(S.sGA, 4 + tt)
                    t.transpose(pgx[:, 0:8], acc[:, tt * 128:(tt + 1) * 128],
                                id_f[0:8, 0:8]).then_inc(S.sGT, 1)
                for g in range(4):
                    t.wait_ge(S.sGA, 20)
                    if g >= 1:
                        t.wait_ge(S.sGA, 20 + 3 * g)
                    for st in range(4 * g + 4):
                        rhs = (allone[:, :] if st < 4 * g
                               else cmask[:, (st - 4 * g) * 512:(st - 4 * g + 1) * 512])
                        ins = t.matmul(pgc[:, :], lgtokP[:, st * 128:(st + 1) * 128], rhs,
                                       start=(st == 0), stop=(st == 4 * g + 3))
                    ins.then_inc(S.sGT, 1)
                for tt in range(16):
                    t.wait_ge(S.sGA, 32)
                    if tt >= 1:
                        t.wait_ge(S.sGA, 32 + tt)
                    t.transpose(pgx[:, 0:16], logg16[:, tt * 128:(tt + 1) * 128],
                                id_f[0:16, 0:16]).then_inc(S.sGT, 1)

            if DBG:
                @block.gpsimd
                def _(g):
                    g.wait_ge(S.sGA, 48)
                    g.dma_start(out=d_dbg[0:16, :], in_=eps16[:, :]).then_inc(S.dCst, 16)
                    g.dma_start(out=d_dbg[16:144, 0:256], in_=b2tok[:, :]).then_inc(S.dCst, 16)
                    g.dma_start(out=d_dbg[144:152, :], in_=gproj[:, :]).then_inc(S.dCst, 16)
                    g.dma_start(out=d_dbg[152:160, :], in_=acc[:, :]).then_inc(S.dCst, 16)

            sg2.close()

            kbuf = [e2(nc.sbuf_tensor(f"kbuf{i}", [128, 2048], F32)) for i in range(2)]
            vbuf = [e2(nc.sbuf_tensor(f"vbuf{i}", [128, 2048], F32)) for i in range(2)]
            qbuf = [e2(nc.sbuf_tensor(f"qbuf{i}", [128, 1024], F32)) for i in range(2)]
            stA = [e2(nc.sbuf_tensor(f"stA{i}", [128, 512], F32)) for i in range(2)]
            stB = [e2(nc.sbuf_tensor(f"stB{i}", [128, 512], F32)) for i in range(2)]
            epsrow = [e2(nc.sbuf_tensor(f"epsrow{i}", [1, 512], F32)) for i in range(2)]
            rb = e2(nc.sbuf_tensor("rb", [128, 512], F32))
            y_all = e2(nc.sbuf_tensor("y_all", [128, 16 * 1024], F32))
            pst = [e2(nc.psum_tensor(f"pst{i}", [128, 512], F32)) for i in range(2)]
            py = e2(nc.psum_tensor("py", [128, 512], F32))
            pn = e2(nc.psum_tensor("pn", [128, 512], F32))

            @block.gpsimd
            def _(g_):
                g_.wait_ge(S.dSk, 512)
                g_.wait_ge(S.dSv, 256)
                g_.wait_ge(S.dSq, 256)
                g_.wait_ge(S.sGA, 48)
                for kvh in range(KVH):
                    if kvh >= 2:
                        g_.wait_ge(S.sYN, 4 * (kvh - 1))
                    g_.dma_start(out=kbuf[kvh % 2][:],
                                 in_=d_ksp[kvh, :, :]).then_inc(dReKs[kvh % 2], 16)
                    vap = bass.AP(d_vsp.tensor, kvh * 128,
                                  [[1024, 128], [131072, 16], [1, 128]])
                    sap = bass.AP(vbuf[kvh % 2], 0, [[2048, 128], [128, 16], [1, 128]])
                    g_.dma_start(out=sap, in_=vap).then_inc(dReVs[kvh % 2], 16)
                    for j in range(2):
                        qh = 2 * kvh + j
                        if qh >= 2:
                            g_.wait_ge(S.sYN, 2 * (qh - 1))
                        g_.dma_start(out=qbuf[qh % 2][:],
                                     in_=d_qsp[qh, :, :]).then_inc(dReQs[qh % 2], 16)
                        for par in range(2):
                            ctx = g_.If_eq(regs["gpsimd"], 0) if par == 0 else g_.Else()
                            with ctx:
                                for oc in range(2):
                                    gg = OWN[par][oc]
                                    if qh >= 1:
                                        g_.wait_ge(S.sPD, 2 * (qh - 1) + oc + 1)
                                    g_.dma_start(
                                        out=epsrow[oc][:],
                                        in_=eps16[qh:qh + 1, gg * 512:(gg + 1) * 512]
                                    ).then_inc(dReEs[oc], 16)

            def ps_base(par, qh, oc):
                return qh * 20 + (BOF(par, 0) if oc == 1 else 0)

            @block.tensor
            def _(t):
                t.wait_ge(S.sGA, 48)
                for kvh in range(KVH):
                    for j in range(2):
                        qh = 2 * kvh + j
                        for par in range(2):
                            ctx = t.If_eq(regs["tensor"], 0) if par == 0 else t.Else()
                            with ctx:
                                if j == 0:
                                    t.wait_ge(dReKs[kvh % 2], 16 * (kvh // 2 + 1))
                                    t.wait_ge(dReVs[kvh % 2], 16 * (kvh // 2 + 1))
                                t.wait_ge(dReQs[qh % 2], 16 * (qh // 2 + 1))
                                for oc in range(2):
                                    B = BOF(par, oc)
                                    base = ps_base(par, qh, oc)
                                    ocn = 2 * qh + oc

                                    def avn(st):
                                        sel = stB if st >= B - 4 else stA
                                        if st >= B - 4:
                                            t.wait_ge(S.sMK, 4 + 4 * ocn + (st - (B - 4)) + 1)
                                        else:
                                            t.wait_ge(S.sSQ, base + st + 1)
                                        t.matmul(py[:, :],
                                                 vbuf[kvh % 2][:, st * 128:(st + 1) * 128],
                                                 sel[st % 2][:, :],
                                                 start=(st == 0), stop=(st == B - 1))
                                        t.matmul(pn[:, :], allone[:, 0:128],
                                                 sel[st % 2][:, :],
                                                 start=(st == 0), stop=False
                                                 ).then_inc(S.sPY, 1)

                                    if ocn >= 1:
                                        t.wait_ge(S.sYN, ocn)
                                        t.wait_ge(S.sRC, ocn)
                                    for st in range(B):
                                        if base + st >= 2:
                                            t.wait_ge(S.sSQ, base + st - 1)
                                        t.matmul(pst[st % 2][:, :],
                                                 kbuf[kvh % 2][:, st * 128:(st + 1) * 128],
                                                 qbuf[qh % 2][:, oc * 512:(oc + 1) * 512],
                                                 start=True, stop=True
                                                 ).then_inc(S.sPS, 1)
                                        if st >= 1:
                                            avn(st - 1)
                                    avn(B - 1)
                                    t.wait_ge(dReEs[oc], 16 * (qh + 1))
                                    t.matmul(pn[:, :], ones_row[:, :], epsrow[oc][0:1, :],
                                             start=False, stop=True).then_inc(S.sPD, 1)

            @block.scalar
            def _(s):
                for kvh in range(KVH):
                    for j in range(2):
                        qh = 2 * kvh + j
                        for par in range(2):
                            ctx = s.If_eq(regs["scalar"], 0) if par == 0 else s.Else()
                            with ctx:
                                for oc in range(2):
                                    B = BOF(par, oc)
                                    base = ps_base(par, qh, oc)
                                    for st in range(B):
                                        s.wait_ge(S.sPS, base + st + 1)
                                        if base + st >= 2:
                                            s.wait_ge(S.sPY, base + st - 1)
                                        s.activation(
                                            stA[st % 2][:, :], pst[st % 2][:, :],
                                            AF.Square,
                                            scale=b2tok[:, st * 16 + qh: st * 16 + qh + 1],
                                        ).then_inc(S.sSQ, 1)

            @block.vector
            def _(v):
                for kvh in range(KVH):
                    for j in range(2):
                        qh = 2 * kvh + j
                        for par in range(2):
                            ctx = v.If_eq(regs["vector"], 0) if par == 0 else v.Else()
                            with ctx:
                                for oc in range(2):
                                    B = BOF(par, oc)
                                    base = ps_base(par, qh, oc)
                                    ocn = 2 * qh + oc
                                    for st in range(B - 4, B):
                                        r = st - (B - 4)
                                        v.wait_ge(S.sSQ, base + st + 1)
                                        if base + st >= 2:
                                            v.wait_ge(S.sPY, base + st - 1)
                                        v.tensor_tensor(
                                            stB[st % 2][:, :], stA[st % 2][:, :],
                                            cmask[:, r * 512:(r + 1) * 512],
                                            ALU.mult).then_inc(S.sMK, 1)
                                    v.wait_ge(S.sPD, ocn + 1)
                                    v.reciprocal(rb[:, :], pn[:, :]).then_inc(S.sRC, 1)
                                    v.tensor_tensor(
                                        y_all[:, qh * 1024 + oc * 512: qh * 1024 + (oc + 1) * 512],
                                        py[:, :], rb[:, :], ALU.mult).then_inc(S.sYN, 1)

            if stages < 4:
                if DBG:
                    @block.gpsimd
                    def _(g_):
                        g_.wait_ge(S.sYN, 32)
                        for qc in range(16):
                            g_.dma_start(out=d_qsp[qc, :, :],
                                         in_=y_all[:, qc * 1024:(qc + 1) * 1024]
                                         ).then_inc(S.dCst, 16)
                _finish(block, nc, d_out, S, [(S.sYN, 32)], ones_row)
                return nc

            wcb = [e2(nc.sbuf_tensor(f"wcb{i}", [128, 2048], F32)) for i in range(2)]
            stile = [e2(nc.sbuf_tensor(f"stile{i}", [128, 512], F32)) for i in range(2)]
            ps2 = [e2(nc.psum_tensor(f"ps2{i}", [128, 512], F32)) for i in range(2)]

            @block.sync
            def _(sync):
                sync.wait_ge(S.sYN, 32)
                for co in range(16):
                    if co >= 2:
                        sync.wait_ge(S.sP2, 2 * (co - 1))
                    sync.dma_start(
                        out=bass.AP(wcb[co % 2], 0, [[2048, 128], [128, 16], [1, 128]]),
                        in_=bass.AP(d_wc.tensor, co * 262144,
                                    [[128, 128], [16384, 16], [1, 128]]),
                    ).then_inc(dWcs[co % 2], 16)

            @block.tensor
            def _(t):
                t.wait_ge(S.sYN, 32)
                for co in range(16):
                    t.wait_ge(dWcs[co % 2], 16 * (co // 2 + 1))
                    for oc in range(2):
                        n2 = 2 * co + oc
                        if n2 >= 2:
                            t.wait_ge(S.sSE2, n2 - 1)
                        for qc in range(16):
                            ins = t.matmul(ps2[oc][:, :],
                                           wcb[co % 2][:, qc * 128:(qc + 1) * 128],
                                           y_all[:, qc * 1024 + oc * 512: qc * 1024 + (oc + 1) * 512],
                                           start=(qc == 0), stop=(qc == 15))
                        ins.then_inc(S.sP2, 1)

            @block.scalar
            def _(s):
                s.wait_ge(S.dCst, 64)
                for co in range(16):
                    for oc in range(2):
                        n2 = 2 * co + oc
                        s.wait_ge(S.sP2, n2 + 1)
                        if n2 >= 2:
                            s.wait_ge(dSss[n2 % 2], 16 * ((n2 - 2) // 2 + 1))
                        s.activation(stile[n2 % 2][:, :], ps2[oc][:, :], AF.Identity,
                                     bias=bc_s[:, co:co + 1]).then_inc(S.sSE2, 1)

            @block.gpsimd
            def _(g_):
                for co in range(16):
                    for oc in range(2):
                        n2 = 2 * co + oc
                        g_.wait_ge(S.sSE2, n2 + 1)
                        g_.dma_start(out=d_ssp[co, :, oc * 512:(oc + 1) * 512],
                                     in_=stile[n2 % 2][:, :]).then_inc(dSss[n2 % 2], 16)

        if stages < 5:
            _finish(block, nc, d_out, S, [(S.dSs0, 256), (S.dSs1, 256)], ones_row)
            return nc

        g0.close()
        s3 = ExitStack()
        with s3:
            e3 = s3.enter_context
            xno_h = e3(nc.sbuf_tensor("xno_h", [128, 16 * 512], BF16))
            xno_l = e3(nc.sbuf_tensor("xno_l", [128, 16 * 512], BF16))
            gwb = [e3(nc.sbuf_tensor(f"gwb{i}", [128, 2048], BF16)) for i in range(2)]
            uwb = [e3(nc.sbuf_tensor(f"uwb{i}", [128, 2048], BF16)) for i in range(2)]
            hbh = e3(nc.sbuf_tensor("hbh", [128, 64 * 512], BF16))
            hbl = e3(nc.sbuf_tensor("hbl", [128, 64 * 512], BF16))
            dwb = [e3(nc.sbuf_tensor(f"dwb{i}", [128, 2048], BF16)) for i in range(3)]
            obf = e3(nc.sbuf_tensor("obf", [128, 128], BF16))
            cs128 = e3(nc.sbuf_tensor("cs128", [128, 512], F32))
            gtmp = e3(nc.sbuf_tensor("gtmp", [128, 512], F32))
            sgl = e3(nc.sbuf_tensor("sgl", [128, 512], F32))
            uu = e3(nc.sbuf_tensor("uu", [128, 512], F32))
            go = e3(nc.sbuf_tensor("go", [128, 512], F32))
            st2 = e3(nc.sbuf_tensor("st2", [128, 512], F32))
            xt2 = e3(nc.sbuf_tensor("xt2", [128, 512], F32))
            ot2 = [e3(nc.sbuf_tensor(f"ot2{i}", [128, 512], F32)) for i in range(2)]
            pA = [e3(nc.psum_tensor(f"pA{i}", [128, 512], F32)) for i in range(2)]
            pB = [e3(nc.psum_tensor(f"pB{i}", [128, 512], F32)) for i in range(2)]
            pD = [e3(nc.psum_tensor(f"pD{i}", [128, 512], F32)) for i in range(2)]
            pC = [e3(nc.psum_tensor(f"pC{i}", [128, 512], F32)) for i in range(2)]
            dDws = [S.dDw, S.dDwB, S.dHb]

            @block.gpsimd
            def _(g_):
                g_.wait_ge(S.dSxn, 64)
                g_.wait_ge(S.sP2, 32)
                g_.wait_ge(S.dSs0, 256)
                g_.wait_ge(S.dSs1, 256)
                g_.memset(obf[:, :], 1.0).then_inc(S.dXn3, 1)
                for oc in range(2):
                    if oc == 1:
                        g_.wait_ge(S.sPF, NI)
                    for (dst, src) in ((xno_h, d_xnh), (xno_l, d_xnl)):
                        g_.dma_start(
                            out=bass.AP(dst, 0, [[8192, 128], [512, 16], [1, 512]]),
                            in_=bass.AP(src.tensor, oc * 512,
                                        [[1024, 128], [131072, 16], [1, 512]]),
                        ).then_inc(S.dXn3, 16)
                    for i in range(NI):
                        n = oc * NI + i
                        if n >= 2:
                            g_.wait_ge(S.sPF, n - 1)
                        g_.dma_start(
                            out=bass.AP(gwb[n % 2], 0, [[2048, 128], [128, 16], [1, 128]]),
                            in_=bass.AP(d_gw.tensor, i * 262144,
                                        [[128, 128], [16384, 16], [1, 128]]),
                        ).then_inc(dC3s[n % 2], 16)
                        g_.dma_start(
                            out=bass.AP(uwb[n % 2], 0, [[2048, 128], [128, 16], [1, 128]]),
                            in_=bass.AP(d_uw.tensor, i * 262144,
                                        [[128, 128], [16384, 16], [1, 128]]),
                        ).then_inc(dC3s[n % 2], 16)
                    if DBG:
                        g_.wait_ge(S.sHE, NI * (oc + 1))
                        g_.dma_start(
                            out=bass.AP(d_hh.tensor, oc * 4194304,
                                        [[512, 128], [65536, 64], [1, 512]]),
                            in_=bass.AP(hbh, 0, [[32768, 128], [512, 64], [1, 512]]),
                        ).then_inc(dShs[0], 16)
                        g_.dma_start(
                            out=bass.AP(d_hl.tensor, oc * 4194304,
                                        [[512, 128], [65536, 64], [1, 512]]),
                            in_=bass.AP(hbl, 0, [[32768, 128], [512, 64], [1, 512]]),
                        ).then_inc(dShs[1], 16)
                    for co in range(NT):
                        m = NT * oc + co
                        for qf in range(4):
                            nh = 4 * m + qf
                            if nh >= 3:
                                g_.wait_ge(S.sPDh, nh - 2)
                            g_.dma_start(
                                out=bass.AP(dwb[nh % 3], 0,
                                            [[2048, 128], [128, 16], [1, 128]]),
                                in_=bass.AP(d_dw.tensor,
                                            co * 1048576 + qf * 262144,
                                            [[128, 128], [16384, 16], [1, 128]]),
                            ).then_inc(dDws[nh % 3], 16)
                        if m >= 1:
                            g_.wait_ge(S.sOC, m)
                        g_.dma_start(out=st2[:, :],
                                     in_=d_ssp[co, :, oc * 512:(oc + 1) * 512]
                                     ).then_inc(dSxs[m % 2], 16)
                        g_.dma_start(
                            out=xt2[:, :],
                            in_=bass.AP(d_xT.tensor, co * 128 * 1024 + oc * 512,
                                        [[1024, 128], [1, 512]]),
                        ).then_inc(dSxs[m % 2], 16)

            @block.tensor
            def _(t):
                t.wait_ge(S.sSE2, 32)
                for oc in range(2):
                    t.wait_ge(S.dXn3, 1 + 32 * (oc + 1))
                    for cc in range(NCC):
                        t.matmul(pC[oc][:, :], obf[:, :],
                                 xno_h[:, cc * 512:(cc + 1) * 512],
                                 start=(cc == 0), stop=False)
                        ins = t.matmul(pC[oc][:, :], obf[:, :],
                                       xno_l[:, cc * 512:(cc + 1) * 512],
                                       start=False, stop=(cc == NCC - 1))
                    ins.then_inc(S.sDQ, 1)
                    for i in range(NI):
                        n = oc * NI + i
                        t.wait_ge(dC3s[n % 2], 32 * (n // 2 + 1))
                        if n >= 2:
                            t.wait_ge(S.sGU, n - 1)
                            t.wait_ge(S.sHE, n - 1)
                        for cc in range(NCC):
                            xo = cc * 512
                            t.matmul(pA[n % 2][:, :],
                                     gwb[n % 2][:, cc * 128:(cc + 1) * 128],
                                     xno_h[:, xo:xo + 512],
                                     start=(cc == 0), stop=False)
                            t.matmul(pA[n % 2][:, :],
                                     gwb[n % 2][:, cc * 128:(cc + 1) * 128],
                                     xno_l[:, xo:xo + 512], start=False,
                                     stop=(cc == NCC - 1))
                        for cc in range(NCC):
                            xo = cc * 512
                            t.matmul(pB[n % 2][:, :],
                                     uwb[n % 2][:, cc * 128:(cc + 1) * 128],
                                     xno_h[:, xo:xo + 512],
                                     start=(cc == 0), stop=False)
                            ins = t.matmul(pB[n % 2][:, :],
                                           uwb[n % 2][:, cc * 128:(cc + 1) * 128],
                                           xno_l[:, xo:xo + 512], start=False,
                                           stop=(cc == NCC - 1))
                        ins.then_inc(S.sPF, 1)
                    t.wait_ge(S.sHE, NI * (oc + 1))
                    for co in range(NT):
                        m = NT * oc + co
                        if m >= 2:
                            t.wait_ge(S.sS4, m - 1)
                        for i in range(NI):
                            qf, io = i // 16, i % 16
                            nh = 4 * m + qf
                            if io == 0:
                                t.wait_ge(S.sDQ4, nh + 1)
                            t.matmul(pD[m % 2][:, :],
                                     dwb[nh % 3][:, io * 128:(io + 1) * 128],
                                     hbh[:, i * 512:(i + 1) * 512],
                                     start=(i == 0), stop=False)
                            ins = t.matmul(pD[m % 2][:, :],
                                           dwb[nh % 3][:, io * 128:(io + 1) * 128],
                                           hbl[:, i * 512:(i + 1) * 512],
                                           start=False, stop=(i == NI - 1))
                            if io == 15:
                                ins.then_inc(S.sPDh, 1)

            def _combine(v, m):
                v.wait_ge(S.sS4, m + 1)
                v.wait_ge(dSxs[m % 2], 32 * (m // 2 + 1))
                if m >= 2:
                    v.wait_ge(dOs[m % 2], 16 * ((m - 2) // 2 + 1))
                v.tensor_tensor(ot2[m % 2][:, :], st2[:, :], go[:, :],
                                ALU.mult)
                v.tensor_add(ot2[m % 2][:, :], ot2[m % 2][:, :],
                             xt2[:, :]).then_inc(S.sOC, 1)

            @block.vector
            def _(v):
                v.wait_ge(S.dCst, 64)
                for oc in range(2):
                    v.wait_ge(S.sDQ, oc + 1)
                    v.tensor_scalar(cs128[:, :], pC[oc][:, :], -128.0, None,
                                    ALU.mult)
                    for i in range(NI):
                        n = oc * NI + i
                        v.wait_ge(S.sPF, n + 1)
                        if n >= 1:
                            v.wait_ge(S.sS3, n)
                        v.tensor_add(gtmp[:, :], pA[n % 2][:, :],
                                     cs128[:, :]).then_inc(S.sGU, 1)
                        v.tensor_add(uu[:, :], pB[n % 2][:, :], cs128[:, :])
                        v.tensor_scalar(uu[:, :], uu[:, :], su_s[:, i:i + 1],
                                        None, ALU.mult)
                        if oc == 1 and i == 0:
                            v.wait_ge(S.sPDh, 4 * NT)
                            if DBG:
                                v.wait_ge(dShs[0], 16)
                                v.wait_ge(dShs[1], 16)
                        v.wait_ge(S.sS3, n + 1)
                        v.tensor_tensor(uu[:, :], sgl[:, :], uu[:, :],
                                        ALU.mult)
                        v.tensor_copy(hbh[:, i * 512:(i + 1) * 512], uu[:, :])
                        v.tensor_sub(hbl[:, i * 512:(i + 1) * 512], uu[:, :],
                                     hbh[:, i * 512:(i + 1) * 512]
                                     ).then_inc(S.sHE, 1)
                    for co in range(NT):
                        m = NT * oc + co
                        for qf in range(4):
                            nh = 4 * m + qf
                            v.wait_ge(dDws[nh % 3], 16 * (nh // 3 + 1))
                            if nh >= 3:
                                v.wait_ge(S.sPDh, nh - 2)
                            v.tensor_scalar(dwb[nh % 3][:, :], dwb[nh % 3][:, :],
                                            -128.0, None, ALU.add
                                            ).then_inc(S.sDQ4, 1)
                        if co >= 1:
                            _combine(v, m - 1)
                    _combine(v, NT * oc + NT - 1)

            @block.scalar
            def _(s):
                s.wait_ge(S.dCst, 64)
                for oc in range(2):
                    for i in range(NI):
                        n = oc * NI + i
                        s.wait_ge(S.sGU, n + 1)
                        if n >= 1:
                            s.wait_ge(S.sHE, n)
                        s.activation(sgl[:, :], gtmp[:, :],
                                     AF.Silu, scale=sg_s[:, i:i + 1]
                                     ).then_inc(S.sS3, 1)
                    for co in range(NT):
                        m = NT * oc + co
                        s.wait_ge(S.sPDh, 4 * (m + 1))
                        if m >= 1:
                            s.wait_ge(S.sOC, m)
                        s.activation(go[:, :], pD[m % 2][:, :], AF.Silu,
                                     scale=sd_s[:, co:co + 1]).then_inc(S.sS4, 1)

            @block.sync
            def _(sync):
                for oc in range(2):
                    for co in range(NT):
                        m = NT * oc + co
                        sync.wait_ge(S.sOC, m + 1)
                        sync.dma_start(out=d_out[co * 128:(co + 1) * 128,
                                                 oc * 512:(oc + 1) * 512],
                                       in_=ot2[m % 2][:, :]).then_inc(dOs[m % 2], 16)
                sync.wait_ge(S.dO0, 256)
                sync.wait_ge(S.dO1, 256)


    return nc


def _finish(block, nc, d_out, S, waits, src):
    @block.sync
    def _(sync):
        for (sm, v) in waits:
            sync.wait_ge(sm, v)
        sync.dma_start(out=d_out[0:1, 0:128], in_=src[0:1, 0:128]).then_inc(S.dO0, 16)
        sync.wait_ge(S.dO0, 16)


_NC_CACHE = {}


def _get_nc():
    if "nc" not in _NC_CACHE:
        _NC_CACHE["nc"] = build_nc()
    return _NC_CACHE["nc"]


def make_inputs(core, x, Wq, Wk, Wv, Wg, Wc, bc, gw, gs, uw, us, dw, ds):
    b, par = core // 2, core % 2
    own_cols = np.concatenate([np.arange(g * 512, (g + 1) * 512) for g in OWN[par]])
    xb = np.ascontiguousarray(x[b])
    return {
        "rank": np.array([[par]], dtype=np.int32),
        "x_nat": xb,
        "xT_own": np.ascontiguousarray(xb.T[:, own_cols]),
        "wq_t": np.ascontiguousarray(Wq.reshape(16, 128, 16, 128).transpose(2, 0, 1, 3)),
        "wk_t": np.ascontiguousarray(Wk.reshape(16, 128, 8, 128).transpose(2, 0, 1, 3)),
        "wv_r": np.ascontiguousarray(Wv.reshape(16, 128, 1024)),
        "wg_r": np.ascontiguousarray(Wg.reshape(16, 128, 8)),
        "wc_t": np.ascontiguousarray(Wc.reshape(16, 128, 16, 128).transpose(2, 0, 1, 3)),
        "bc_sb": np.ascontiguousarray(bc.reshape(16, 128).T),
        "gwt": np.ascontiguousarray(
            gw.T.reshape(16, 128, 64, 128).transpose(2, 0, 1, 3)).astype(np.uint8),
        "uwt": np.ascontiguousarray(
            uw.T.reshape(16, 128, 64, 128).transpose(2, 0, 1, 3)).astype(np.uint8),
        "dwt": np.ascontiguousarray(
            dw.T.reshape(64, 128, 16, 128).transpose(2, 0, 1, 3)).astype(np.uint8),
        "sg_sb": np.ascontiguousarray(gs.reshape(64, 128).T),
        "su_sb": np.ascontiguousarray(us.reshape(64, 128).T),
        "sd_sb": np.ascontiguousarray(ds.reshape(16, 128).T),
    }


def kernel(L0_attn_hidden_states, L0_attn_Wq, L0_attn_Wk, L0_attn_Wv,
           L0_attn_Wg, L0_attn_Wc, L0_attn_bc, L0_ffn_gate_w,
           L0_ffn_gate_scale, L0_ffn_up_w, L0_ffn_up_scale, L0_ffn_down_w,
           L0_ffn_down_scale):
    x = np.asarray(L0_attn_hidden_states, dtype=np.float32)
    args = (x, np.asarray(L0_attn_Wq, np.float32), np.asarray(L0_attn_Wk, np.float32),
            np.asarray(L0_attn_Wv, np.float32), np.asarray(L0_attn_Wg, np.float32),
            np.asarray(L0_attn_Wc, np.float32), np.asarray(L0_attn_bc, np.float32),
            np.asarray(L0_ffn_gate_w, np.int32), np.asarray(L0_ffn_gate_scale, np.float32),
            np.asarray(L0_ffn_up_w, np.int32), np.asarray(L0_ffn_up_scale, np.float32),
            np.asarray(L0_ffn_down_w, np.int32), np.asarray(L0_ffn_down_scale, np.float32))
    in_maps = [make_inputs(c, *args) for c in range(8)]
    nc = _get_nc()
    res = run_bass_kernel_spmd(nc, in_maps, core_ids=list(range(8)))
    out = np.empty((4, T, CD), dtype=np.float32)
    for c in range(8):
        b, par = c // 2, c % 2
        o = res.results[c]["out"]
        for oc, g in enumerate(OWN[par]):
            out[b, g * 512:(g + 1) * 512, :] = o[:, oc * 512:(oc + 1) * 512].T
    return out


# version 06 (final)
# speedup vs baseline: 1.1699x; 1.0605x over previous
import math
import numpy as np
from contextlib import ExitStack

import concourse.bass as bass
import concourse.mybir as mybir
from concourse.bass_utils import run_bass_kernel_spmd

F32 = mybir.dt.float32
BF16 = mybir.dt.bfloat16
I32 = mybir.dt.int32
U8 = mybir.dt.uint8
AF = mybir.ActivationFunctionType
ALU = mybir.AluOpType

T, CD, NH, KVH, DH, IFF = 2048, 2048, 16, 8, 128, 8192
NT, NCC, NI = 16, 16, 64
GATE_OFF = 6.906768
GO_HI = float(np.float32(GATE_OFF))
GO_LO = float(np.float32(GATE_OFF - GO_HI))
SCALE = 1.0 / math.sqrt(128.0)
OWN = [[0, 3], [1, 2]]
DBG = False
P1_PARTS = ("k", "g", "v", "q")


def BOF(par, oc):
    return 4 * (OWN[par][oc] + 1)


def build_nc(stages=6):
    nc = bass.Bass("TRN2", target_bir_lowering=False,
                   detect_race_conditions=False)

    d_rank = nc.dram_tensor("rank", [1, 1], I32, kind="ExternalInput").ap()
    d_x = nc.dram_tensor("x_nat", [T, CD], F32, kind="ExternalInput").ap()
    d_xT = nc.dram_tensor("xT_own", [CD, 1024], F32, kind="ExternalInput").ap()
    d_wq = nc.dram_tensor("wq_t", [16, 16, 128, 128], F32, kind="ExternalInput").ap()
    d_wk = nc.dram_tensor("wk_t", [8, 16, 128, 128], F32, kind="ExternalInput").ap()
    d_wv = nc.dram_tensor("wv_r", [16, 128, 1024], F32, kind="ExternalInput").ap()
    d_wg = nc.dram_tensor("wg_r", [16, 128, 8], F32, kind="ExternalInput").ap()
    d_wc = nc.dram_tensor("wc_t", [16, 16, 128, 128], F32, kind="ExternalInput").ap()
    d_bc = nc.dram_tensor("bc_sb", [128, 16], F32, kind="ExternalInput").ap()
    d_gw = nc.dram_tensor("gwt", [64, 16, 128, 128], U8, kind="ExternalInput").ap()
    d_uw = nc.dram_tensor("uwt", [64, 16, 128, 128], U8, kind="ExternalInput").ap()
    d_dw = nc.dram_tensor("dwt", [16, 64, 128, 128], U8, kind="ExternalInput").ap()
    d_sg = nc.dram_tensor("sg_sb", [128, 64], F32, kind="ExternalInput").ap()
    d_su = nc.dram_tensor("su_sb", [128, 64], F32, kind="ExternalInput").ap()
    d_sd = nc.dram_tensor("sd_sb", [128, 16], F32, kind="ExternalInput").ap()
    d_out = nc.dram_tensor("out", [CD, 1024], F32, kind="ExternalOutput").ap()

    sk = {"kind": "ExternalOutput"} if DBG else {}
    d_xnh = nc.dram_tensor("xn_sp_h", [16, 128, 1024], BF16, **sk).ap()
    d_xnl = nc.dram_tensor("xn_sp_l", [16, 128, 1024], BF16, **sk).ap()
    d_qsp = nc.dram_tensor("q_sp", [16, 128, 1024], F32, **sk).ap()
    d_ksp = nc.dram_tensor("k_sp", [8, 128, 2048], F32, **sk).ap()
    d_vsp = nc.dram_tensor("v_sp", [16, 128, 1024], F32, **sk).ap()
    d_ssp = nc.dram_tensor("s_sp", [16, 128, 1024], F32, **sk).ap()
    d_hh = nc.dram_tensor("h_sp_h", [2, 64, 128, 512], BF16, **sk).ap()
    d_hl = nc.dram_tensor("h_sp_l", [2, 64, 128, 512], BF16, **sk).ap()
    d_dbg = (nc.dram_tensor("dbg_gate", [160, 2048], F32, kind="ExternalOutput").ap()
             if DBG else None)

    st0 = ExitStack()
    with st0:
        e = st0.enter_context
        rk_s = e(nc.sbuf_tensor("rk_s", [1, 1], I32))
        id_bf = e(nc.sbuf_tensor("id_bf", [128, 128], BF16))
        ones_col = e(nc.sbuf_tensor("ones_col", [128, 1], F32))
        ones_row = e(nc.sbuf_tensor("ones_row", [1, 128], F32))
        pid = e(nc.sbuf_tensor("pid", [128, 1], F32))
        sg_s = e(nc.sbuf_tensor("sg_s", [128, 64], F32))
        su_s = e(nc.sbuf_tensor("su_s", [128, 64], F32))
        sd_s = e(nc.sbuf_tensor("sd_s", [128, 16], F32))
        bc_s = e(nc.sbuf_tensor("bc_s", [128, 16], F32))
        constk = e(nc.sbuf_tensor("constk", [128, 8], F32))
        g0 = ExitStack()
        eg0 = g0.enter_context
        id_f = eg0(nc.sbuf_tensor("id_f", [128, 128], F32))
        iot = eg0(nc.sbuf_tensor("iot", [128, 512], F32))
        pplus = eg0(nc.sbuf_tensor("pplus", [128, 4], F32))
        gproj = eg0(nc.sbuf_tensor("gproj", [8, 2048], F32))
        st0.callback(g0.close)
        CONSTS = [1e-5, -GATE_OFF, -1.0, SCALE, -0.5]
        for ci, cv in enumerate(CONSTS):
            nc.const_aps.register(F32, cv, constk[:, ci:ci + 1])

        sem = {}
        for name in ("dRk dXa dXb dWk0 dWk1 dWv dWg dWq0 dWq1 dWc0 dWc1 dCst "
                     "dSk dSv dSq dSxn dSs0 dSs1 dSh0 dSh1 dReK0 dReK1 dReV0 "
                     "dReV1 dReQ0 dReQ1 dReE0 dReE1 dXn3 dC30 dC31 dHb dDw "
                     "dSx0 dSx1 dO0 dO1 sG sA sB sC sD sE "
                     "sTh sTl sFh sFl sXF sPK sPGt sPV sKE sGE sVE sWH sWL sPQ "
                     "sQE sGA sGT sPS sSQ sMK sPY sNE sPD sRC sYN sP2 sSE2 "
                     "sDQ sPF sS3 sHE sPD3 sS4 sOC sDQ4 sGU dDwB sPDh dWvB dWk2 dWv2").split():
            sem[name] = e(nc.semaphore(name))
        S = type("S", (), sem)
        dXs = [S.dXa, S.dXb]
        dWks = [S.dWk0, S.dWk1]
        dWqs = [S.dWq0, S.dWq1]
        dWcs = [S.dWc0, S.dWc1]
        dSss = [S.dSs0, S.dSs1]
        dShs = [S.dSh0, S.dSh1]
        dReKs = [S.dReK0, S.dReK1]
        dReVs = [S.dReV0, S.dReV1]
        dReQs = [S.dReQ0, S.dReQ1]
        dReEs = [S.dReE0, S.dReE1]
        dC3s = [S.dC30, S.dC31]
        dSxs = [S.dSx0, S.dSx1]
        dWvs = [S.dWv, S.dWvB, S.dWv2]
        dWkhs = [S.dWk0, S.dWk1, S.dWk2]
        dOs = [S.dO0, S.dO1]

        block = e(nc.Block())
        regs = {}

        @block.sync
        def _(sync):
            sync.dma_start(out=rk_s[:], in_=d_rank[:]).then_inc(S.dRk, 16)

        for eng_name in ("tensor", "scalar", "vector", "gpsimd"):
            @getattr(block, eng_name)
            def _(en, _n=eng_name):
                r = en.alloc_register("rk_" + _n)
                regs[_n] = r
                en.wait_ge(S.dRk, 16)
                en.reg_load(r, rk_s[0:1, 0:1])

        @block.gpsimd
        def _(g):
            g.iota(iot[:, :], [[1, 512]], channel_multiplier=0,
                   allow_small_or_imprecise_dtypes=True)
            g.iota(pid[:, :], [[1, 1]], channel_multiplier=1,
                   allow_small_or_imprecise_dtypes=True)
            g.memset(ones_col[:, :], 1.0)
            g.memset(ones_row[:, :], 1.0)
            g.dma_start(out=sg_s[:], in_=d_sg[:]).then_inc(S.dCst, 16)
            g.dma_start(out=su_s[:], in_=d_su[:]).then_inc(S.dCst, 16)
            g.dma_start(out=sd_s[:], in_=d_sd[:]).then_inc(S.dCst, 16)
            g.dma_start(out=bc_s[:], in_=d_bc[:]).then_inc(S.dCst, 16)
            for ci, cv in enumerate(CONSTS):
                g.memset(constk[:, ci:ci + 1], cv)
            g.memset(pplus[:, :], 0.0).then_inc(S.sG, 1)

        @block.vector
        def _(v):
            v.wait_ge(S.sG, 1)
            for r in range(4):
                v.tensor_scalar(pplus[:, r:r + 1], pid[:, 0:1], float(128 * r),
                                None, ALU.add)
            v.tensor_scalar(id_f[:, :], iot[:, 0:128], pid[:, 0:1], None,
                            ALU.is_equal).then_inc(S.sG, 1)
            v.wait_ge(S.sG, 2)
            v.tensor_copy(id_bf[:, :], id_f[:, :]).then_inc(S.sC, 1)

        p0 = ExitStack()
        with p0:
            e0 = p0.enter_context
            xnT_h = e0(nc.sbuf_tensor("xnT_h", [128, 16 * 2048], BF16))
            xnT_l = e0(nc.sbuf_tensor("xnT_l", [128, 16 * 2048], BF16))

            p0t = ExitStack()
            with p0t:
                e0t = p0t.enter_context
                xtok = [e0t(nc.sbuf_tensor(f"xtok{i}", [128, 2048], F32)) for i in range(2)]
                stat6 = e0t(nc.sbuf_tensor("stat6", [128, 24], F32))
                mv = e0t(nc.sbuf_tensor("mv", [128, 2], F32))
                lnv = e0t(nc.sbuf_tensor("lnv", [128, 1], F32))
                rln = e0t(nc.sbuf_tensor("rln", [128, 1], F32))
                nmr = e0t(nc.sbuf_tensor("nmr", [128, 1], F32))
                xh_tok = e0t(nc.sbuf_tensor("xh_tok", [128, 2048], BF16))
                xl_tok = e0t(nc.sbuf_tensor("xl_tok", [128, 2048], BF16))
                xf_tok = e0t(nc.sbuf_tensor("xf_tok", [128, 2048], F32))
                pTh = [e0t(nc.psum_tensor(f"pTh{i}", [128, 1024], BF16)) for i in range(2)]
                pTl = [e0t(nc.psum_tensor(f"pTl{i}", [128, 1024], BF16)) for i in range(2)]

                @block.sync
                def _(sync):
                    for tt in range(NT):
                        if tt >= 2:
                            sync.wait_ge(S.sA, tt - 1)
                            sync.wait_ge(S.sE, tt - 1)
                        sync.dma_start(out=xtok[tt % 2][:],
                                       in_=d_x[tt * 128:(tt + 1) * 128, :]
                                       ).then_inc(dXs[tt % 2], 16)

                @block.vector
                def _(v):
                    for tt in range(NT):
                        v.wait_ge(dXs[tt % 2], 16 * (tt // 2 + 1))
                        for j in range(4):
                            ins = v.bn_stats(stat6[:, 6 * j:6 * j + 6],
                                             xtok[tt % 2][:, 512 * j:512 * (j + 1)])
                        ins.then_inc(S.sA, 1)
                        v.wait_ge(S.sA, 2 * tt + 1)
                        if tt >= 1:
                            v.wait_ge(S.sB, tt)
                        v.bn_aggr(mv[:, :], stat6[:, :]).then_inc(S.sA, 1)
                        v.wait_ge(S.sB, tt + 1)
                        v.tensor_scalar(nmr[:, :], mv[:, 0:1], rln[:, 0:1], -1.0,
                                        ALU.mult, ALU.mult).then_inc(S.sC, 1)
                        v.wait_ge(S.sC, tt + 2)
                        v.wait_ge(S.sD, tt + 1)
                        if tt >= 1:
                            v.wait_ge(S.sTl, 16 * tt)
                        v.tensor_scalar(xf_tok[:, :], xtok[tt % 2][:, :], rln[:, 0:1],
                                        nmr[:, 0:1], ALU.mult, ALU.add)
                        v.tensor_sub(xl_tok[:, :], xf_tok[:, :],
                                     xh_tok[:, :]).then_inc(S.sE, 1)
                        for cc in range(NCC):
                            v.wait_ge(S.sTl, 16 * tt + cc + 1)
                            v.tensor_copy(
                                xnT_l[:, cc * 2048 + tt * 128: cc * 2048 + (tt + 1) * 128],
                                pTl[cc % 2][:, 0:128]).then_inc(S.sFl, 1)

                @block.scalar
                def _(s):
                    s.wait_ge(S.sG, 1)
                    for tt in range(NT):
                        s.wait_ge(S.sA, 2 * (tt + 1))
                        s.activation(lnv[:, :], mv[:, 1:2], AF.Ln, bias=1e-5)
                        s.activation(rln[:, :], lnv[:, :], AF.Exp,
                                     scale=-0.5).then_inc(S.sB, 1)
                        s.wait_ge(S.sC, tt + 2)
                        if tt >= 1:
                            s.wait_ge(S.sTh, 16 * tt)
                        s.activation(xh_tok[:, :], xtok[tt % 2][:, :], AF.Identity,
                                     scale=rln[:, 0:1], bias=nmr[:, 0:1]).then_inc(S.sD, 1)
                        for cc in range(NCC):
                            s.wait_ge(S.sTh, 16 * tt + cc + 1)
                            s.activation(
                                xnT_h[:, cc * 2048 + tt * 128: cc * 2048 + (tt + 1) * 128],
                                pTh[cc % 2][:, 0:128], AF.Copy).then_inc(S.sFh, 1)

                @block.tensor
                def _(t):
                    t.wait_ge(S.sC, 1)
                    for tt in range(NT):
                        t.wait_ge(S.sD, tt + 1)
                        for cc in range(NCC):
                            if 16 * tt + cc >= 2:
                                t.wait_ge(S.sFh, 16 * tt + cc - 1)
                            t.transpose(pTh[cc % 2][:, 0:128],
                                        xh_tok[:, cc * 128:(cc + 1) * 128],
                                        id_bf[:, :]).then_inc(S.sTh, 1)
                        t.wait_ge(S.sE, tt + 1)
                        for cc in range(NCC):
                            if 16 * tt + cc >= 2:
                                t.wait_ge(S.sFl, 16 * tt + cc - 1)
                            t.transpose(pTl[cc % 2][:, 0:128],
                                        xl_tok[:, cc * 128:(cc + 1) * 128],
                                        id_bf[:, :]).then_inc(S.sTl, 1)

                @block.gpsimd
                def _(g):
                    g.wait_ge(S.sFh, 256)
                    g.wait_ge(S.sFl, 256)
                    for par in range(2):
                        ctx = g.If_eq(regs["gpsimd"], 0) if par == 0 else g.Else()
                        with ctx:
                            for oc in range(2):
                                gg = OWN[par][oc]
                                for (src, dst) in ((xnT_h, d_xnh), (xnT_l, d_xnl)):
                                    sap = bass.AP(src, gg * 512,
                                                  [[32768, 128], [2048, 16], [1, 512]])
                                    dap = bass.AP(dst.tensor, oc * 512,
                                                  [[1024, 128], [131072, 16], [1, 512]])
                                    g.dma_start(out=dap, in_=sap).then_inc(S.dSxn, 16)

            if stages < 1:
                _finish(block, nc, d_out, S, [(S.dSxn, 64)], ones_row)
                return nc

            p1 = ExitStack()
            with p1:
                e1 = p1.enter_context
                pb = [e1(nc.psum_tensor(f"pb{i}", [128, 512], F32)) for i in range(8)]
                p1a = ExitStack()
                e1a = p1a.enter_context
                xnf = e1a(nc.sbuf_tensor("xnf", [128, 16 * 512], F32))
                wkb = [e1a(nc.sbuf_tensor(f"wkb{i}", [128, 1024], F32)) for i in range(3)]
                wvb = [e1a(nc.sbuf_tensor(f"wvb{i}", [128, 512], F32)) for i in range(3)]
                wgb = e1a(nc.sbuf_tensor("wgb", [128, 2048], F32))
                gstage = e1a(nc.sbuf_tensor("gstage", [128, 512], F32))
                ktile = e1a(nc.sbuf_tensor("ktile", [128, 512], F32))
                vtile = e1a(nc.sbuf_tensor("vtile", [128, 1024], F32))

                @block.sync
                def _(sync):
                    sync.wait_ge(S.sTh, 256)
                    sync.wait_ge(S.sTl, 256)
                    sync.wait_ge(S.sE, 16)
                    sync.wait_ge(S.sFh, 256)
                    sync.wait_ge(S.sFl, 256)
                    for g in range(4):
                        if "k" in P1_PARTS:
                            for kvh in range(8):
                                idx = g * 8 + kvh
                                for hf in range(2):
                                    nhk = 2 * idx + hf
                                    if nhk >= 3:
                                        sync.wait_ge(S.sPK, nhk - 2)
                                    sync.dma_start(
                                        out=bass.AP(wkb[nhk % 3], 0,
                                                    [[1024, 128], [128, 8], [1, 128]]),
                                        in_=bass.AP(d_wk.tensor,
                                                    kvh * 262144 + hf * 131072,
                                                    [[128, 128], [16384, 8], [1, 128]]),
                                    ).then_inc(dWkhs[nhk % 3], 16)
                        if g == 0 and "g" in P1_PARTS:
                            sync.wait_ge(S.sG, 3)
                            sync.dma_start(
                                out=bass.AP(wgb, 0, [[2048, 128], [128, 16], [1, 8]]),
                                in_=bass.AP(d_wg.tensor, 0, [[8, 128], [1024, 16], [1, 8]]),
                            ).then_inc(S.dWg, 16)
                        if "v" in P1_PARTS:
                            for cc in range(NCC):
                                n = 16 * g + cc
                                for hf in range(2):
                                    nhv = 2 * n + hf
                                    if nhv >= 3:
                                        sync.wait_ge(S.sPV, nhv - 2)
                                    sync.dma_start(
                                        out=wvb[nhv % 3][:],
                                        in_=d_wv[cc, :, hf * 512:(hf + 1) * 512]
                                    ).then_inc(dWvs[nhv % 3], 16)

                @block.vector
                def _(v):
                    for g in range(4):
                        if g == 0:
                            v.wait_ge(S.sFh, 256)
                            v.wait_ge(S.sFl, 256)
                        else:
                            if "k" in P1_PARTS:
                                v.wait_ge(S.sPK, 16 * g)
                            if "g" in P1_PARTS:
                                v.wait_ge(S.sPGt, g)
                            if "v" in P1_PARTS:
                                v.wait_ge(S.sPV, 32 * g)
                        for cc in range(NCC):
                            v.tensor_add(
                                xnf[:, cc * 512:(cc + 1) * 512],
                                xnT_h[:, cc * 2048 + g * 512: cc * 2048 + g * 512 + 512],
                                xnT_l[:, cc * 2048 + g * 512: cc * 2048 + g * 512 + 512],
                            ).then_inc(S.sXF, 1)
                        for tl in range(4 if "v" in P1_PARTS else 0):
                            iv = 4 * g + tl
                            v.wait_ge(S.sPV, 32 * (g + 1))
                            if iv >= 1:
                                v.wait_ge(S.dSv, 16 * iv)
                            v.tensor_copy(vtile[:, 0:512], pb[2 * tl][:, :])
                            v.tensor_copy(vtile[:, 512:1024],
                                          pb[2 * tl + 1][:, :]).then_inc(S.sVE, 2)

                @block.tensor
                def _(t):
                    for g in range(4):
                        t.wait_ge(S.sXF, 16 * (g + 1))
                        for kvh in range(8 if "k" in P1_PARTS else 0):
                            idx = g * 8 + kvh
                            if idx >= 2:
                                t.wait_ge(S.sKE, idx - 1)
                            for cc in range(NCC):
                                nhk = 2 * idx + cc // 8
                                if cc % 8 == 0:
                                    t.wait_ge(dWkhs[nhk % 3],
                                              16 * (nhk // 3 + 1))
                                ins = t.matmul(pb[idx % 2][:, :],
                                               wkb[nhk % 3][:, (cc % 8) * 128:
                                                            (cc % 8 + 1) * 128],
                                               xnf[:, cc * 512:(cc + 1) * 512],
                                               start=(cc == 0), stop=(cc == 15))
                                if cc % 8 == 7:
                                    ins.then_inc(S.sPK, 1)
                        if "g" in P1_PARTS:
                            t.wait_ge(S.dWg, 16)
                            t.wait_ge(S.sGE, g)
                            for cc in range(NCC):
                                ins = t.matmul(pb[2][:, :], wgb[:, cc * 128:(cc + 1) * 128],
                                               xnf[:, cc * 512:(cc + 1) * 512],
                                               start=(cc == 0), stop=(cc == 15))
                            ins.then_inc(S.sPGt, 1)
                        if "v" not in P1_PARTS:
                            continue
                        if "k" in P1_PARTS:
                            t.wait_ge(S.sKE, 8 * (g + 1))
                        if "g" in P1_PARTS:
                            t.wait_ge(S.sGE, g + 1)
                        t.wait_ge(S.sVE, 8 * g)
                        for cc in range(NCC):
                            n = 16 * g + cc
                            for hf in range(2):
                                nhv = 2 * n + hf
                                t.wait_ge(dWvs[nhv % 3], 16 * (nhv // 3 + 1))
                                for tl in range(4):
                                    lhs = xnf[:, cc * 512 + tl * 128:
                                              cc * 512 + (tl + 1) * 128]
                                    ins = t.matmul(pb[2 * tl + hf][:, :], lhs,
                                                   wvb[nhv % 3][:, :],
                                                   start=(cc == 0), stop=(cc == 15))
                                ins.then_inc(S.sPV, 1)

                @block.scalar
                def _(s):
                    for g in range(4):
                        for kvh in range(8 if "k" in P1_PARTS else 0):
                            idx = g * 8 + kvh
                            s.wait_ge(S.sPK, 2 * (idx + 1))
                            if idx >= 1:
                                s.wait_ge(S.dSk, 16 * idx)
                            s.activation(ktile[:, :], pb[idx % 2][:, :],
                                         AF.Copy).then_inc(S.sKE, 1)
                        if "g" in P1_PARTS:
                            s.wait_ge(S.sPGt, g + 1)
                            s.activation(gstage[:, :], pb[2][:, :], AF.Copy)
                            s.activation(gproj[:, g * 512:(g + 1) * 512], gstage[0:8, :],
                                         AF.Copy).then_inc(S.sGE, 1)

                @block.gpsimd
                def _(g_):
                    g_.memset(wgb[:, :], 0.0).then_inc(S.sG, 1)
                    for g in range(4):
                        for kvh in range(8 if "k" in P1_PARTS else 0):
                            idx = g * 8 + kvh
                            g_.wait_ge(S.sKE, idx + 1)
                            g_.dma_start(out=d_ksp[kvh, :, g * 512:(g + 1) * 512],
                                         in_=ktile[:, :]).then_inc(S.dSk, 16)
                        for tl in range(4 if "v" in P1_PARTS else 0):
                            iv = 4 * g + tl
                            g_.wait_ge(S.sVE, 2 * (iv + 1))
                            g_.dma_start(out=d_vsp[iv, :, :],
                                         in_=vtile[:, :]).then_inc(S.dSv, 16)

                p1a.close()

                q1 = ExitStack()
                with q1 if "q" in P1_PARTS else ExitStack():
                  if "q" in P1_PARTS:
                    eq = q1.enter_context
                    wqb = [eq(nc.sbuf_tensor(f"wqb{i}", [128, 2048], F32)) for i in range(2)]
                    wqh = eq(nc.sbuf_tensor("wqh", [128, 2048], BF16))
                    wql = eq(nc.sbuf_tensor("wql", [128, 2048], BF16))
                    qt2 = eq(nc.sbuf_tensor("qt2", [128, 1024], F32))

                    @block.sync
                    def _(sync):
                        if "v" in P1_PARTS:
                            sync.wait_ge(S.sPV, 128)
                        if "k" in P1_PARTS:
                            sync.wait_ge(S.dSk, 512)
                        if "v" in P1_PARTS:
                            sync.wait_ge(S.dSv, 256)
                        for qc in range(16):
                            if qc >= 2:
                                sync.wait_ge(S.sWH, qc - 1)
                                sync.wait_ge(S.sWL, qc - 1)
                            sync.dma_start(
                                out=bass.AP(wqb[qc % 2], 0, [[2048, 128], [128, 16], [1, 128]]),
                                in_=bass.AP(d_wq.tensor, qc * 262144,
                                            [[128, 128], [16384, 16], [1, 128]]),
                            ).then_inc(dWqs[qc % 2], 16)

                    @block.scalar
                    def _(s):
                        for qc in range(16):
                            s.wait_ge(dWqs[qc % 2], 16 * (qc // 2 + 1))
                            if qc >= 1:
                                s.wait_ge(S.sPQ, 2 * qc)
                            s.activation(wqh[:, :], wqb[qc % 2][:, :],
                                         AF.Copy).then_inc(S.sWH, 1)
                            for oc in range(2):
                                s.wait_ge(S.sPQ, 2 * qc + oc + 1)
                                if qc >= 1 and oc == 0:
                                    s.wait_ge(S.dSq, 16 * qc)
                                s.activation(qt2[:, oc * 512:(oc + 1) * 512],
                                             pb[oc][:, :], AF.Copy,
                                             scale=SCALE).then_inc(S.sQE, 1)

                    @block.vector
                    def _(v):
                        for qc in range(16):
                            v.wait_ge(S.sWH, qc + 1)
                            if qc >= 1:
                                v.wait_ge(S.sPQ, 2 * qc)
                            v.tensor_sub(wql[:, :], wqb[qc % 2][:, :],
                                         wqh[:, :]).then_inc(S.sWL, 1)

                    @block.tensor
                    def _(t):
                        t.wait_ge(S.sVE, 32)
                        t.wait_ge(S.sKE, 32)
                        for qc in range(16):
                            t.wait_ge(S.sWH, qc + 1)
                            t.wait_ge(S.sWL, qc + 1)
                            for par in range(2):
                                ctx = t.If_eq(regs["tensor"], 0) if par == 0 else t.Else()
                                with ctx:
                                    for oc in range(2):
                                        if qc >= 1:
                                            t.wait_ge(S.sQE, 2 * (qc - 1) + oc + 1)
                                        gg = OWN[par][oc]
                                        for cc in range(NCC):
                                            co = cc * 2048 + gg * 512
                                            t.matmul(pb[oc][:, :],
                                                     wqh[:, cc * 128:(cc + 1) * 128],
                                                     xnT_h[:, co:co + 512],
                                                     start=(cc == 0), stop=False)
                                            t.matmul(pb[oc][:, :],
                                                     wqh[:, cc * 128:(cc + 1) * 128],
                                                     xnT_l[:, co:co + 512],
                                                     start=False, stop=False)
                                            ins = t.matmul(pb[oc][:, :],
                                                           wql[:, cc * 128:(cc + 1) * 128],
                                                           xnT_h[:, co:co + 512],
                                                           start=False, stop=(cc == 15))
                                        ins.then_inc(S.sPQ, 1)

                    @block.gpsimd
                    def _(g_):
                        for qc in range(16):
                            g_.wait_ge(S.sQE, 2 * (qc + 1))
                            g_.dma_start(out=d_qsp[qc, :, :],
                                         in_=qt2[:, :]).then_inc(S.dSq, 16)

        if stages < 2:
            w = [(S.dSxn, 64)]
            if "k" in P1_PARTS:
                w.append((S.dSk, 512))
            if "v" in P1_PARTS:
                w.append((S.dSv, 256))
            if "q" in P1_PARTS:
                w.append((S.dSq, 256))
            _finish(block, nc, d_out, S, w, ones_row)
            return nc

        s2 = ExitStack()
        with s2:
            e2 = s2.enter_context
            cmask = e2(nc.sbuf_tensor("cmask", [128, 4 * 512], F32))
            allone = e2(nc.sbuf_tensor("allone", [128, 512], F32))
            eps16 = e2(nc.sbuf_tensor("eps16", [16, 2048], F32))
            b2tok = e2(nc.sbuf_tensor("b2tok", [128, 256], F32))
            sg2 = ExitStack()
            eg2 = sg2.enter_context
            ubuf = eg2(nc.sbuf_tensor("ubuf", [8, 2048], F32))
            gbuf = eg2(nc.sbuf_tensor("gbuf", [8, 2048], F32))
            acc = eg2(nc.sbuf_tensor("acc", [8, 2048], F32))
            tmp8 = eg2(nc.sbuf_tensor("tmp8", [8, 2048], F32))
            lgtokP = eg2(nc.sbuf_tensor("lgtokP", [128, 2048], F32))
            estage2 = eg2(nc.sbuf_tensor("estage2", [128, 512], F32))
            lstage = eg2(nc.sbuf_tensor("lstage", [128, 512], F32))
            logg16 = eg2(nc.sbuf_tensor("logg16", [16, 2048], F32))
            pgx = eg2(nc.psum_tensor("pgx", [128, 512], F32))
            pgc = eg2(nc.psum_tensor("pgc", [128, 512], F32))
            s2.callback(sg2.close)

            @block.gpsimd
            def _(g):
                g.wait_ge(S.sPQ, 32)
                g.wait_ge(S.dSq, 256)
                g.memset(allone[:, :], 1.0)
                g.memset(lgtokP[:, :], 0.0).then_inc(S.sG, 1)

            @block.vector
            def _(v):
                v.wait_ge(S.sG, 4)
                for r in range(4):
                    v.tensor_scalar(cmask[:, r * 512:(r + 1) * 512], iot[:, :],
                                    pplus[:, r:r + 1], None, ALU.is_ge).then_inc(S.sMK, 1)

            if stages < 3:
                _finish(block, nc, d_out, S,
                        [(S.dSxn, 64), (S.dSk, 512), (S.dSv, 256), (S.dSq, 256)],
                        ones_row)
                return nc

            @block.scalar
            def _(s):
                s.wait_ge(S.sGA, 1)
                s.activation(ubuf[:, :], gbuf[:, :], AF.Exp,
                             scale=-1.0).then_inc(S.sGA, 1)
                for g in range(4):
                    s.wait_ge(S.sGT, 16 + g + 1)
                    s.activation(estage2[:, :], pgc[:, :], AF.Exp,
                                 scale=-1.0).then_inc(S.sGA, 1)
                for tt in range(16):
                    s.wait_ge(S.sGT, 20 + tt + 1)
                    s.activation(b2tok[:, tt * 16:(tt + 1) * 16], pgx[:, 0:16],
                                 AF.Exp, scale=-0.5).then_inc(S.sGA, 1)

            @block.vector
            def _(v):
                v.wait_ge(S.sGE, 4)
                v.wait_ge(S.sPQ, 32)
                v.wait_ge(S.dSq, 256)
                v.tensor_scalar(gbuf[:, :], gproj[:, :], GO_HI, GO_LO,
                                ALU.add, ALU.add).then_inc(S.sGA, 1)
                v.wait_ge(S.sGA, 2)
                v.tensor_scalar(acc[:, :], ubuf[:, :], -1.0 / 6.0, 1.0 / 5.0,
                                ALU.mult, ALU.add)
                for k in (4, 3, 2):
                    v.tensor_tensor(tmp8[:, :], ubuf[:, :], acc[:, :], ALU.mult)
                    v.tensor_scalar(acc[:, :], tmp8[:, :], -1.0, 1.0 / k,
                                    ALU.mult, ALU.add)
                v.tensor_tensor(tmp8[:, :], ubuf[:, :], acc[:, :], ALU.mult)
                v.tensor_scalar(acc[:, :], tmp8[:, :], -1.0, 1.0, ALU.mult, ALU.add)
                v.tensor_tensor(tmp8[:, :], ubuf[:, :], acc[:, :], ALU.mult)
                v.tensor_scalar(acc[:, :], tmp8[:, :], -1.0, None,
                                ALU.mult).then_inc(S.sGA, 2)
                for tt in range(16):
                    v.wait_ge(S.sGT, tt + 1)
                    v.tensor_copy(
                        bass.AP(lgtokP, tt * 128, [[2048, 128], [2, 8]]),
                        pgx[:, 0:8]
                    )
                    v.tensor_copy(
                        bass.AP(lgtokP, tt * 128 + 1, [[2048, 128], [2, 8]]),
                        pgx[:, 0:8]
                    ).then_inc(S.sGA, 1)
                for g in range(4):
                    v.wait_ge(S.sGA, 21 + 3 * g)
                    v.tensor_scalar(eps16[:, g * 512:(g + 1) * 512], estage2[0:16, :],
                                    1e-6, None, ALU.mult)
                    v.wait_ge(S.sGT, 16 + g + 1)
                    v.tensor_copy(lstage[:, :], pgc[:, :])
                    v.tensor_copy(logg16[:, g * 512:(g + 1) * 512],
                                  lstage[0:16, :]).then_inc(S.sGA, 2)

            @block.tensor
            def _(t):
                t.wait_ge(S.sPQ, 32)
                for tt in range(16):
                    t.wait_ge(S.sGA, 4)
                    if tt >= 1:
                        t.wait_ge(S.sGA, 4 + tt)
                    t.transpose(pgx[:, 0:8], acc[:, tt * 128:(tt + 1) * 128],
                                id_f[0:8, 0:8]).then_inc(S.sGT, 1)
                for g in range(4):
                    t.wait_ge(S.sGA, 20)
                    if g >= 1:
                        t.wait_ge(S.sGA, 20 + 3 * g)
                    for st in range(4 * g + 4):
                        rhs = (allone[:, :] if st < 4 * g
                               else cmask[:, (st - 4 * g) * 512:(st - 4 * g + 1) * 512])
                        ins = t.matmul(pgc[:, :], lgtokP[:, st * 128:(st + 1) * 128], rhs,
                                       start=(st == 0), stop=(st == 4 * g + 3))
                    ins.then_inc(S.sGT, 1)
                for tt in range(16):
                    t.wait_ge(S.sGA, 32)
                    if tt >= 1:
                        t.wait_ge(S.sGA, 32 + tt)
                    t.transpose(pgx[:, 0:16], logg16[:, tt * 128:(tt + 1) * 128],
                                id_f[0:16, 0:16]).then_inc(S.sGT, 1)

            if DBG:
                @block.gpsimd
                def _(g):
                    g.wait_ge(S.sGA, 48)
                    g.dma_start(out=d_dbg[0:16, :], in_=eps16[:, :]).then_inc(S.dCst, 16)
                    g.dma_start(out=d_dbg[16:144, 0:256], in_=b2tok[:, :]).then_inc(S.dCst, 16)
                    g.dma_start(out=d_dbg[144:152, :], in_=gproj[:, :]).then_inc(S.dCst, 16)
                    g.dma_start(out=d_dbg[152:160, :], in_=acc[:, :]).then_inc(S.dCst, 16)

            sg2.close()

            kbuf = [e2(nc.sbuf_tensor(f"kbuf{i}", [128, 2048], F32)) for i in range(2)]
            vbuf = [e2(nc.sbuf_tensor(f"vbuf{i}", [128, 2048], F32)) for i in range(2)]
            qbuf = [e2(nc.sbuf_tensor(f"qbuf{i}", [128, 1024], F32)) for i in range(2)]
            stA = [e2(nc.sbuf_tensor(f"stA{i}", [128, 512], F32)) for i in range(2)]
            stB = [e2(nc.sbuf_tensor(f"stB{i}", [128, 512], F32)) for i in range(2)]
            epsrow = [e2(nc.sbuf_tensor(f"epsrow{i}", [1, 512], F32)) for i in range(2)]
            rb = e2(nc.sbuf_tensor("rb", [128, 512], F32))
            y_all = e2(nc.sbuf_tensor("y_all", [128, 16 * 1024], F32))
            pst = [e2(nc.psum_tensor(f"pst{i}", [128, 512], F32)) for i in range(2)]
            py = e2(nc.psum_tensor("py", [128, 512], F32))
            pn = e2(nc.psum_tensor("pn", [128, 512], F32))

            @block.gpsimd
            def _(g_):
                g_.wait_ge(S.dSk, 512)
                g_.wait_ge(S.dSv, 256)
                g_.wait_ge(S.dSq, 256)
                g_.wait_ge(S.sGA, 48)
                for kvh in range(KVH):
                    if kvh >= 2:
                        g_.wait_ge(S.sYN, 4 * (kvh - 1))
                    g_.dma_start(out=kbuf[kvh % 2][:],
                                 in_=d_ksp[kvh, :, :]).then_inc(dReKs[kvh % 2], 16)
                    vap = bass.AP(d_vsp.tensor, kvh * 128,
                                  [[1024, 128], [131072, 16], [1, 128]])
                    sap = bass.AP(vbuf[kvh % 2], 0, [[2048, 128], [128, 16], [1, 128]])
                    g_.dma_start(out=sap, in_=vap).then_inc(dReVs[kvh % 2], 16)
                    for j in range(2):
                        qh = 2 * kvh + j
                        if qh >= 2:
                            g_.wait_ge(S.sYN, 2 * (qh - 1))
                        g_.dma_start(out=qbuf[qh % 2][:],
                                     in_=d_qsp[qh, :, :]).then_inc(dReQs[qh % 2], 16)
                        for par in range(2):
                            ctx = g_.If_eq(regs["gpsimd"], 0) if par == 0 else g_.Else()
                            with ctx:
                                for oc in range(2):
                                    gg = OWN[par][oc]
                                    if qh >= 1:
                                        g_.wait_ge(S.sPD, 2 * (qh - 1) + oc + 1)
                                    g_.dma_start(
                                        out=epsrow[oc][:],
                                        in_=eps16[qh:qh + 1, gg * 512:(gg + 1) * 512]
                                    ).then_inc(dReEs[oc], 16)

            def ps_base(par, qh, oc):
                return qh * 20 + (BOF(par, 0) if oc == 1 else 0)

            @block.tensor
            def _(t):
                t.wait_ge(S.sGA, 48)
                for kvh in range(KVH):
                    for j in range(2):
                        qh = 2 * kvh + j
                        for par in range(2):
                            ctx = t.If_eq(regs["tensor"], 0) if par == 0 else t.Else()
                            with ctx:
                                if j == 0:
                                    t.wait_ge(dReKs[kvh % 2], 16 * (kvh // 2 + 1))
                                    t.wait_ge(dReVs[kvh % 2], 16 * (kvh // 2 + 1))
                                t.wait_ge(dReQs[qh % 2], 16 * (qh // 2 + 1))
                                for oc in range(2):
                                    B = BOF(par, oc)
                                    base = ps_base(par, qh, oc)
                                    ocn = 2 * qh + oc

                                    def avn(st):
                                        sel = stB if st >= B - 4 else stA
                                        if st >= B - 4:
                                            t.wait_ge(S.sMK, 4 + 4 * ocn + (st - (B - 4)) + 1)
                                        else:
                                            t.wait_ge(S.sSQ, base + st + 1)
                                        t.matmul(py[:, :],
                                                 vbuf[kvh % 2][:, st * 128:(st + 1) * 128],
                                                 sel[st % 2][:, :],
                                                 start=(st == 0), stop=(st == B - 1))
                                        t.matmul(pn[:, :], allone[:, 0:128],
                                                 sel[st % 2][:, :],
                                                 start=(st == 0), stop=False
                                                 ).then_inc(S.sPY, 1)

                                    if ocn >= 1:
                                        t.wait_ge(S.sYN, ocn)
                                        t.wait_ge(S.sRC, ocn)
                                    for st in range(B):
                                        if base + st >= 2:
                                            t.wait_ge(S.sSQ, base + st - 1)
                                        t.matmul(pst[st % 2][:, :],
                                                 kbuf[kvh % 2][:, st * 128:(st + 1) * 128],
                                                 qbuf[qh % 2][:, oc * 512:(oc + 1) * 512],
                                                 start=True, stop=True
                                                 ).then_inc(S.sPS, 1)
                                        if st >= 1:
                                            avn(st - 1)
                                    avn(B - 1)
                                    t.wait_ge(dReEs[oc], 16 * (qh + 1))
                                    t.matmul(pn[:, :], ones_row[:, :], epsrow[oc][0:1, :],
                                             start=False, stop=True).then_inc(S.sPD, 1)

            @block.scalar
            def _(s):
                for kvh in range(KVH):
                    for j in range(2):
                        qh = 2 * kvh + j
                        for par in range(2):
                            ctx = s.If_eq(regs["scalar"], 0) if par == 0 else s.Else()
                            with ctx:
                                for oc in range(2):
                                    B = BOF(par, oc)
                                    base = ps_base(par, qh, oc)
                                    for st in range(B):
                                        s.wait_ge(S.sPS, base + st + 1)
                                        if base + st >= 2:
                                            s.wait_ge(S.sPY, base + st - 1)
                                        s.activation(
                                            stA[st % 2][:, :], pst[st % 2][:, :],
                                            AF.Square,
                                            scale=b2tok[:, st * 16 + qh: st * 16 + qh + 1],
                                        ).then_inc(S.sSQ, 1)

            @block.vector
            def _(v):
                for kvh in range(KVH):
                    for j in range(2):
                        qh = 2 * kvh + j
                        for par in range(2):
                            ctx = v.If_eq(regs["vector"], 0) if par == 0 else v.Else()
                            with ctx:
                                for oc in range(2):
                                    B = BOF(par, oc)
                                    base = ps_base(par, qh, oc)
                                    ocn = 2 * qh + oc
                                    for st in range(B - 4, B):
                                        r = st - (B - 4)
                                        v.wait_ge(S.sSQ, base + st + 1)
                                        if base + st >= 2:
                                            v.wait_ge(S.sPY, base + st - 1)
                                        v.tensor_tensor(
                                            stB[st % 2][:, :], stA[st % 2][:, :],
                                            cmask[:, r * 512:(r + 1) * 512],
                                            ALU.mult).then_inc(S.sMK, 1)
                                    v.wait_ge(S.sPD, ocn + 1)
                                    v.reciprocal(rb[:, :], pn[:, :]).then_inc(S.sRC, 1)
                                    v.tensor_tensor(
                                        y_all[:, qh * 1024 + oc * 512: qh * 1024 + (oc + 1) * 512],
                                        py[:, :], rb[:, :], ALU.mult).then_inc(S.sYN, 1)

            if stages < 4:
                if DBG:
                    @block.gpsimd
                    def _(g_):
                        g_.wait_ge(S.sYN, 32)
                        for qc in range(16):
                            g_.dma_start(out=d_qsp[qc, :, :],
                                         in_=y_all[:, qc * 1024:(qc + 1) * 1024]
                                         ).then_inc(S.dCst, 16)
                _finish(block, nc, d_out, S, [(S.sYN, 32)], ones_row)
                return nc

            wcb = [e2(nc.sbuf_tensor(f"wcb{i}", [128, 2048], F32)) for i in range(2)]
            stile = [e2(nc.sbuf_tensor(f"stile{i}", [128, 512], F32)) for i in range(2)]
            ps2 = [e2(nc.psum_tensor(f"ps2{i}", [128, 512], F32)) for i in range(2)]

            @block.sync
            def _(sync):
                sync.wait_ge(S.sYN, 32)
                for co in range(16):
                    if co >= 2:
                        sync.wait_ge(S.sP2, 2 * (co - 1))
                    sync.dma_start(
                        out=bass.AP(wcb[co % 2], 0, [[2048, 128], [128, 16], [1, 128]]),
                        in_=bass.AP(d_wc.tensor, co * 262144,
                                    [[128, 128], [16384, 16], [1, 128]]),
                    ).then_inc(dWcs[co % 2], 16)

            @block.tensor
            def _(t):
                t.wait_ge(S.sYN, 32)
                for co in range(16):
                    t.wait_ge(dWcs[co % 2], 16 * (co // 2 + 1))
                    for oc in range(2):
                        n2 = 2 * co + oc
                        if n2 >= 2:
                            t.wait_ge(S.sSE2, n2 - 1)
                        for qc in range(16):
                            ins = t.matmul(ps2[oc][:, :],
                                           wcb[co % 2][:, qc * 128:(qc + 1) * 128],
                                           y_all[:, qc * 1024 + oc * 512: qc * 1024 + (oc + 1) * 512],
                                           start=(qc == 0), stop=(qc == 15))
                        ins.then_inc(S.sP2, 1)

            @block.scalar
            def _(s):
                s.wait_ge(S.dCst, 64)
                for co in range(16):
                    for oc in range(2):
                        n2 = 2 * co + oc
                        s.wait_ge(S.sP2, n2 + 1)
                        if n2 >= 2:
                            s.wait_ge(dSss[n2 % 2], 16 * ((n2 - 2) // 2 + 1))
                        s.activation(stile[n2 % 2][:, :], ps2[oc][:, :], AF.Identity,
                                     bias=bc_s[:, co:co + 1]).then_inc(S.sSE2, 1)

            @block.gpsimd
            def _(g_):
                for co in range(16):
                    for oc in range(2):
                        n2 = 2 * co + oc
                        g_.wait_ge(S.sSE2, n2 + 1)
                        g_.dma_start(out=d_ssp[co, :, oc * 512:(oc + 1) * 512],
                                     in_=stile[n2 % 2][:, :]).then_inc(dSss[n2 % 2], 16)

        if stages < 5:
            _finish(block, nc, d_out, S, [(S.dSs0, 256), (S.dSs1, 256)], ones_row)
            return nc

        g0.close()
        s3 = ExitStack()
        with s3:
            e3 = s3.enter_context
            xno_h = e3(nc.sbuf_tensor("xno_h", [128, 16 * 512], BF16))
            xno_l = e3(nc.sbuf_tensor("xno_l", [128, 16 * 512], BF16))
            gwb = [e3(nc.sbuf_tensor(f"gwb{i}", [128, 2048], BF16)) for i in range(2)]
            uwb = [e3(nc.sbuf_tensor(f"uwb{i}", [128, 2048], BF16)) for i in range(2)]
            hbh = e3(nc.sbuf_tensor("hbh", [128, 64 * 512], BF16))
            hbl = e3(nc.sbuf_tensor("hbl", [128, 64 * 512], BF16))
            dwb = [e3(nc.sbuf_tensor(f"dwb{i}", [128, 2048], BF16)) for i in range(3)]
            obf = e3(nc.sbuf_tensor("obf", [128, 128], BF16))
            cs128 = e3(nc.sbuf_tensor("cs128", [128, 512], F32))
            gtmp = e3(nc.sbuf_tensor("gtmp", [128, 512], F32))
            sgl = e3(nc.sbuf_tensor("sgl", [128, 512], F32))
            uu = e3(nc.sbuf_tensor("uu", [128, 512], F32))
            go = e3(nc.sbuf_tensor("go", [128, 512], F32))
            st2 = e3(nc.sbuf_tensor("st2", [128, 512], F32))
            xt2 = e3(nc.sbuf_tensor("xt2", [128, 512], F32))
            ot2 = [e3(nc.sbuf_tensor(f"ot2{i}", [128, 512], F32)) for i in range(2)]
            pA = [e3(nc.psum_tensor(f"pA{i}", [128, 512], F32)) for i in range(2)]
            pB = [e3(nc.psum_tensor(f"pB{i}", [128, 512], F32)) for i in range(2)]
            pD = [e3(nc.psum_tensor(f"pD{i}", [128, 512], F32)) for i in range(2)]
            pC = [e3(nc.psum_tensor(f"pC{i}", [128, 512], F32)) for i in range(2)]
            dDws = [S.dDw, S.dDwB, S.dHb]

            @block.gpsimd
            def _(g_):
                g_.wait_ge(S.dSxn, 64)
                g_.wait_ge(S.sP2, 32)
                g_.wait_ge(S.dSs0, 256)
                g_.wait_ge(S.dSs1, 256)
                g_.memset(obf[:, :], 1.0).then_inc(S.dXn3, 1)
                for oc in range(2):
                    if oc == 1:
                        g_.wait_ge(S.sPF, NI)
                    for (dst, src) in ((xno_h, d_xnh), (xno_l, d_xnl)):
                        g_.dma_start(
                            out=bass.AP(dst, 0, [[8192, 128], [512, 16], [1, 512]]),
                            in_=bass.AP(src.tensor, oc * 512,
                                        [[1024, 128], [131072, 16], [1, 512]]),
                        ).then_inc(S.dXn3, 16)
                    for i in range(NI):
                        n = oc * NI + i
                        if n >= 2:
                            g_.wait_ge(S.sPF, n - 1)
                        g_.dma_start(
                            out=bass.AP(gwb[n % 2], 0, [[2048, 128], [128, 16], [1, 128]]),
                            in_=bass.AP(d_gw.tensor, i * 262144,
                                        [[128, 128], [16384, 16], [1, 128]]),
                        ).then_inc(dC3s[n % 2], 16)
                        g_.dma_start(
                            out=bass.AP(uwb[n % 2], 0, [[2048, 128], [128, 16], [1, 128]]),
                            in_=bass.AP(d_uw.tensor, i * 262144,
                                        [[128, 128], [16384, 16], [1, 128]]),
                        ).then_inc(dC3s[n % 2], 16)
                    if DBG:
                        g_.wait_ge(S.sHE, NI * (oc + 1))
                        g_.dma_start(
                            out=bass.AP(d_hh.tensor, oc * 4194304,
                                        [[512, 128], [65536, 64], [1, 512]]),
                            in_=bass.AP(hbh, 0, [[32768, 128], [512, 64], [1, 512]]),
                        ).then_inc(dShs[0], 16)
                        g_.dma_start(
                            out=bass.AP(d_hl.tensor, oc * 4194304,
                                        [[512, 128], [65536, 64], [1, 512]]),
                            in_=bass.AP(hbl, 0, [[32768, 128], [512, 64], [1, 512]]),
                        ).then_inc(dShs[1], 16)
                    for co in range(NT):
                        m = NT * oc + co
                        for qf in range(4):
                            nh = 4 * m + qf
                            if nh >= 3:
                                g_.wait_ge(S.sPDh, nh - 2)
                            g_.dma_start(
                                out=bass.AP(dwb[nh % 3], 0,
                                            [[2048, 128], [128, 16], [1, 128]]),
                                in_=bass.AP(d_dw.tensor,
                                            co * 1048576 + qf * 262144,
                                            [[128, 128], [16384, 16], [1, 128]]),
                            ).then_inc(dDws[nh % 3], 16)
                        if m >= 1:
                            g_.wait_ge(S.sOC, m)
                        g_.dma_start(out=st2[:, :],
                                     in_=d_ssp[co, :, oc * 512:(oc + 1) * 512]
                                     ).then_inc(dSxs[m % 2], 16)
                        g_.dma_start(
                            out=xt2[:, :],
                            in_=bass.AP(d_xT.tensor, co * 128 * 1024 + oc * 512,
                                        [[1024, 128], [1, 512]]),
                        ).then_inc(dSxs[m % 2], 16)

            @block.tensor
            def _(t):
                t.wait_ge(S.sSE2, 32)
                for oc in range(2):
                    t.wait_ge(S.dXn3, 1 + 32 * (oc + 1))
                    for cc in range(NCC):
                        t.matmul(pC[oc][:, :], obf[:, :],
                                 xno_h[:, cc * 512:(cc + 1) * 512],
                                 start=(cc == 0), stop=False)
                        ins = t.matmul(pC[oc][:, :], obf[:, :],
                                       xno_l[:, cc * 512:(cc + 1) * 512],
                                       start=False, stop=(cc == NCC - 1))
                    ins.then_inc(S.sDQ, 1)
                    for i in range(NI):
                        n = oc * NI + i
                        t.wait_ge(dC3s[n % 2], 32 * (n // 2 + 1))
                        if n >= 2:
                            t.wait_ge(S.sGU, n - 1)
                            t.wait_ge(S.sHE, n - 1)
                        for cc in range(NCC):
                            xo = cc * 512
                            t.matmul(pA[n % 2][:, :],
                                     gwb[n % 2][:, cc * 128:(cc + 1) * 128],
                                     xno_h[:, xo:xo + 512],
                                     start=(cc == 0), stop=False)
                            t.matmul(pA[n % 2][:, :],
                                     gwb[n % 2][:, cc * 128:(cc + 1) * 128],
                                     xno_l[:, xo:xo + 512], start=False,
                                     stop=(cc == NCC - 1))
                        for cc in range(NCC):
                            xo = cc * 512
                            t.matmul(pB[n % 2][:, :],
                                     uwb[n % 2][:, cc * 128:(cc + 1) * 128],
                                     xno_h[:, xo:xo + 512],
                                     start=(cc == 0), stop=False)
                            ins = t.matmul(pB[n % 2][:, :],
                                           uwb[n % 2][:, cc * 128:(cc + 1) * 128],
                                           xno_l[:, xo:xo + 512], start=False,
                                           stop=(cc == NCC - 1))
                        ins.then_inc(S.sPF, 1)
                    t.wait_ge(S.sHE, NI * (oc + 1))
                    for co in range(NT):
                        m = NT * oc + co
                        if m >= 2:
                            t.wait_ge(S.sS4, m - 1)
                        for i in range(NI):
                            qf, io = i // 16, i % 16
                            nh = 4 * m + qf
                            if io == 0:
                                t.wait_ge(S.sDQ4, nh + 1)
                            t.matmul(pD[m % 2][:, :],
                                     dwb[nh % 3][:, io * 128:(io + 1) * 128],
                                     hbh[:, i * 512:(i + 1) * 512],
                                     start=(i == 0), stop=False)
                            ins = t.matmul(pD[m % 2][:, :],
                                           dwb[nh % 3][:, io * 128:(io + 1) * 128],
                                           hbl[:, i * 512:(i + 1) * 512],
                                           start=False, stop=(i == NI - 1))
                            if io == 15:
                                ins.then_inc(S.sPDh, 1)

            def _combine(v, m):
                v.wait_ge(S.sS4, m + 1)
                v.wait_ge(dSxs[m % 2], 32 * (m // 2 + 1))
                if m >= 2:
                    v.wait_ge(dOs[m % 2], 16 * ((m - 2) // 2 + 1))
                v.tensor_tensor(ot2[m % 2][:, :], st2[:, :], go[:, :],
                                ALU.mult)
                v.tensor_add(ot2[m % 2][:, :], ot2[m % 2][:, :],
                             xt2[:, :]).then_inc(S.sOC, 1)

            @block.vector
            def _(v):
                v.wait_ge(S.dCst, 64)
                for oc in range(2):
                    v.wait_ge(S.sDQ, oc + 1)
                    v.tensor_scalar(cs128[:, :], pC[oc][:, :], -128.0, None,
                                    ALU.mult)
                    for i in range(NI):
                        n = oc * NI + i
                        v.wait_ge(S.sPF, n + 1)
                        if n >= 1:
                            v.wait_ge(S.sS3, n)
                        v.tensor_add(gtmp[:, :], pA[n % 2][:, :],
                                     cs128[:, :]).then_inc(S.sGU, 1)
                        v.tensor_add(uu[:, :], pB[n % 2][:, :], cs128[:, :])
                        v.tensor_scalar(uu[:, :], uu[:, :], su_s[:, i:i + 1],
                                        None, ALU.mult)
                        if oc == 1 and i == 0:
                            v.wait_ge(S.sPDh, 4 * NT)
                            if DBG:
                                v.wait_ge(dShs[0], 16)
                                v.wait_ge(dShs[1], 16)
                        v.wait_ge(S.sS3, n + 1)
                        v.tensor_tensor(uu[:, :], sgl[:, :], uu[:, :],
                                        ALU.mult)
                        v.tensor_copy(hbh[:, i * 512:(i + 1) * 512], uu[:, :])
                        v.tensor_sub(hbl[:, i * 512:(i + 1) * 512], uu[:, :],
                                     hbh[:, i * 512:(i + 1) * 512]
                                     ).then_inc(S.sHE, 1)
                    for co in range(NT):
                        m = NT * oc + co
                        for qf in range(4):
                            nh = 4 * m + qf
                            v.wait_ge(dDws[nh % 3], 16 * (nh // 3 + 1))
                            if nh >= 3:
                                v.wait_ge(S.sPDh, nh - 2)
                            v.tensor_scalar(dwb[nh % 3][:, :], dwb[nh % 3][:, :],
                                            -128.0, None, ALU.add
                                            ).then_inc(S.sDQ4, 1)
                        if co >= 1:
                            _combine(v, m - 1)
                    _combine(v, NT * oc + NT - 1)

            @block.scalar
            def _(s):
                s.wait_ge(S.dCst, 64)
                for oc in range(2):
                    for i in range(NI):
                        n = oc * NI + i
                        s.wait_ge(S.sGU, n + 1)
                        if n >= 1:
                            s.wait_ge(S.sHE, n)
                        s.activation(sgl[:, :], gtmp[:, :],
                                     AF.Silu, scale=sg_s[:, i:i + 1]
                                     ).then_inc(S.sS3, 1)
                    for co in range(NT):
                        m = NT * oc + co
                        s.wait_ge(S.sPDh, 4 * (m + 1))
                        if m >= 1:
                            s.wait_ge(S.sOC, m)
                        s.activation(go[:, :], pD[m % 2][:, :], AF.Silu,
                                     scale=sd_s[:, co:co + 1]).then_inc(S.sS4, 1)

            @block.sync
            def _(sync):
                for oc in range(2):
                    for co in range(NT):
                        m = NT * oc + co
                        sync.wait_ge(S.sOC, m + 1)
                        sync.dma_start(out=d_out[co * 128:(co + 1) * 128,
                                                 oc * 512:(oc + 1) * 512],
                                       in_=ot2[m % 2][:, :]).then_inc(dOs[m % 2], 16)
                sync.wait_ge(S.dO0, 256)
                sync.wait_ge(S.dO1, 256)


    return nc


def _finish(block, nc, d_out, S, waits, src):
    @block.sync
    def _(sync):
        for (sm, v) in waits:
            sync.wait_ge(sm, v)
        sync.dma_start(out=d_out[0:1, 0:128], in_=src[0:1, 0:128]).then_inc(S.dO0, 16)
        sync.wait_ge(S.dO0, 16)


_NC_CACHE = {}


def _get_nc():
    if "nc" not in _NC_CACHE:
        _NC_CACHE["nc"] = build_nc()
    return _NC_CACHE["nc"]


def make_inputs(core, x, Wq, Wk, Wv, Wg, Wc, bc, gw, gs, uw, us, dw, ds):
    b, par = core // 2, core % 2
    own_cols = np.concatenate([np.arange(g * 512, (g + 1) * 512) for g in OWN[par]])
    xb = np.ascontiguousarray(x[b])
    return {
        "rank": np.array([[par]], dtype=np.int32),
        "x_nat": xb,
        "xT_own": np.ascontiguousarray(xb.T[:, own_cols]),
        "wq_t": np.ascontiguousarray(Wq.reshape(16, 128, 16, 128).transpose(2, 0, 1, 3)),
        "wk_t": np.ascontiguousarray(Wk.reshape(16, 128, 8, 128).transpose(2, 0, 1, 3)),
        "wv_r": np.ascontiguousarray(Wv.reshape(16, 128, 1024)),
        "wg_r": np.ascontiguousarray(Wg.reshape(16, 128, 8)),
        "wc_t": np.ascontiguousarray(Wc.reshape(16, 128, 16, 128).transpose(2, 0, 1, 3)),
        "bc_sb": np.ascontiguousarray(bc.reshape(16, 128).T),
        "gwt": np.ascontiguousarray(
            gw.T.reshape(16, 128, 64, 128).transpose(2, 0, 1, 3)).astype(np.uint8),
        "uwt": np.ascontiguousarray(
            uw.T.reshape(16, 128, 64, 128).transpose(2, 0, 1, 3)).astype(np.uint8),
        "dwt": np.ascontiguousarray(
            dw.T.reshape(64, 128, 16, 128).transpose(2, 0, 1, 3)).astype(np.uint8),
        "sg_sb": np.ascontiguousarray(gs.reshape(64, 128).T),
        "su_sb": np.ascontiguousarray(us.reshape(64, 128).T),
        "sd_sb": np.ascontiguousarray(ds.reshape(16, 128).T),
    }


def kernel(L0_attn_hidden_states, L0_attn_Wq, L0_attn_Wk, L0_attn_Wv,
           L0_attn_Wg, L0_attn_Wc, L0_attn_bc, L0_ffn_gate_w,
           L0_ffn_gate_scale, L0_ffn_up_w, L0_ffn_up_scale, L0_ffn_down_w,
           L0_ffn_down_scale):
    x = np.asarray(L0_attn_hidden_states, dtype=np.float32)
    args = (x, np.asarray(L0_attn_Wq, np.float32), np.asarray(L0_attn_Wk, np.float32),
            np.asarray(L0_attn_Wv, np.float32), np.asarray(L0_attn_Wg, np.float32),
            np.asarray(L0_attn_Wc, np.float32), np.asarray(L0_attn_bc, np.float32),
            np.asarray(L0_ffn_gate_w, np.int32), np.asarray(L0_ffn_gate_scale, np.float32),
            np.asarray(L0_ffn_up_w, np.int32), np.asarray(L0_ffn_up_scale, np.float32),
            np.asarray(L0_ffn_down_w, np.int32), np.asarray(L0_ffn_down_scale, np.float32))
    in_maps = [make_inputs(c, *args) for c in range(8)]
    nc = _get_nc()
    res = run_bass_kernel_spmd(nc, in_maps, core_ids=list(range(8)))
    out = np.empty((4, T, CD), dtype=np.float32)
    for c in range(8):
        b, par = c // 2, c % 2
        o = res.results[c]["out"]
        for oc, g in enumerate(OWN[par]):
            out[b, g * 512:(g + 1) * 512, :] = o[:, oc * 512:(oc + 1) * 512].T
    return out


# version 07
# speedup vs baseline: 1.2648x; 1.0811x over previous
import math
import numpy as np
from contextlib import ExitStack

import concourse.bass as bass
import concourse.mybir as mybir
from concourse.bass_utils import run_bass_kernel_spmd

F32 = mybir.dt.float32
BF16 = mybir.dt.bfloat16
I32 = mybir.dt.int32
U8 = mybir.dt.uint8
AF = mybir.ActivationFunctionType
ALU = mybir.AluOpType

T, CD, NH, KVH, DH, IFF = 2048, 2048, 16, 8, 128, 8192
NT, NCC, NI = 16, 16, 64
GATE_OFF = 6.906768
GO_HI = float(np.float32(GATE_OFF))
GO_LO = float(np.float32(GATE_OFF - GO_HI))
SCALE = 1.0 / math.sqrt(128.0)
OWN = [[0, 3], [1, 2]]
DBG = False
P1_PARTS = ("k", "g", "v", "q")


def BOF(par, oc):
    return 4 * (OWN[par][oc] + 1)


def build_nc(stages=6):
    nc = bass.Bass("TRN2", target_bir_lowering=False,
                   detect_race_conditions=False)

    d_rank = nc.dram_tensor("rank", [1, 1], I32, kind="ExternalInput").ap()
    d_x = nc.dram_tensor("x_nat", [T, CD], F32, kind="ExternalInput").ap()
    d_xT = nc.dram_tensor("xT_own", [CD, 1024], F32, kind="ExternalInput").ap()
    d_wq = nc.dram_tensor("wq_t", [16, 16, 128, 128], F32, kind="ExternalInput").ap()
    d_wk = nc.dram_tensor("wk_t", [8, 16, 128, 128], F32, kind="ExternalInput").ap()
    d_wv = nc.dram_tensor("wv_r", [16, 128, 1024], F32, kind="ExternalInput").ap()
    d_wg = nc.dram_tensor("wg_r", [16, 128, 8], F32, kind="ExternalInput").ap()
    d_wc = nc.dram_tensor("wc_t", [16, 16, 128, 128], F32, kind="ExternalInput").ap()
    d_bc = nc.dram_tensor("bc_sb", [128, 16], F32, kind="ExternalInput").ap()
    d_gw = nc.dram_tensor("gwt", [64, 16, 128, 128], U8, kind="ExternalInput").ap()
    d_uw = nc.dram_tensor("uwt", [64, 16, 128, 128], U8, kind="ExternalInput").ap()
    d_dw = nc.dram_tensor("dwt", [16, 64, 128, 128], U8, kind="ExternalInput").ap()
    d_sg = nc.dram_tensor("sg_sb", [128, 64], F32, kind="ExternalInput").ap()
    d_su = nc.dram_tensor("su_sb", [128, 64], F32, kind="ExternalInput").ap()
    d_sd = nc.dram_tensor("sd_sb", [128, 16], F32, kind="ExternalInput").ap()
    d_out = nc.dram_tensor("out", [CD, 1024], F32, kind="ExternalOutput").ap()

    sk = {"kind": "ExternalOutput"} if DBG else {}
    d_xnh = nc.dram_tensor("xn_sp_h", [16, 128, 1024], BF16, **sk).ap()
    d_xnl = nc.dram_tensor("xn_sp_l", [16, 128, 1024], BF16, **sk).ap()
    d_qsp = nc.dram_tensor("q_sp", [16, 128, 1024], F32, **sk).ap()
    d_ksp = nc.dram_tensor("k_sp", [8, 128, 2048], F32, **sk).ap()
    d_vsp = nc.dram_tensor("v_sp", [16, 128, 1024], F32, **sk).ap()
    d_ssp = nc.dram_tensor("s_sp", [16, 128, 1024], F32, **sk).ap()
    d_hh = nc.dram_tensor("h_sp_h", [2, 64, 128, 512], BF16, **sk).ap()
    d_hl = nc.dram_tensor("h_sp_l", [2, 64, 128, 512], BF16, **sk).ap()
    d_dbg = (nc.dram_tensor("dbg_gate", [160, 2048], F32, kind="ExternalOutput").ap()
             if DBG else None)

    st0 = ExitStack()
    with st0:
        e = st0.enter_context
        rk_s = e(nc.sbuf_tensor("rk_s", [1, 1], I32))
        id_bf = e(nc.sbuf_tensor("id_bf", [128, 128], BF16))
        ones_col = e(nc.sbuf_tensor("ones_col", [128, 1], F32))
        ones_row = e(nc.sbuf_tensor("ones_row", [1, 128], F32))
        pid = e(nc.sbuf_tensor("pid", [128, 1], F32))
        sg_s = e(nc.sbuf_tensor("sg_s", [128, 64], F32))
        su_s = e(nc.sbuf_tensor("su_s", [128, 64], F32))
        sd_s = e(nc.sbuf_tensor("sd_s", [128, 16], F32))
        bc_s = e(nc.sbuf_tensor("bc_s", [128, 16], F32))
        constk = e(nc.sbuf_tensor("constk", [128, 8], F32))
        g0 = ExitStack()
        eg0 = g0.enter_context
        id_f = eg0(nc.sbuf_tensor("id_f", [128, 128], F32))
        iot = eg0(nc.sbuf_tensor("iot", [128, 512], F32))
        pplus = eg0(nc.sbuf_tensor("pplus", [128, 4], F32))
        gproj = eg0(nc.sbuf_tensor("gproj", [8, 2048], F32))
        st0.callback(g0.close)
        CONSTS = [1e-5, -GATE_OFF, -1.0, SCALE, -0.5]
        for ci, cv in enumerate(CONSTS):
            nc.const_aps.register(F32, cv, constk[:, ci:ci + 1])

        sem = {}
        for name in ("dRk dXa dXb dWk0 dWk1 dWv dWg dWq0 dWq1 dWc0 dWc1 dCst "
                     "dSk dSv dSq dSxn dSs0 dSs1 dSh0 dSh1 dReK0 dReK1 dReV0 "
                     "dReV1 dReQ0 dReQ1 dReE0 dReE1 dXn3 dC30 dC31 dHb dDw "
                     "dSx0 dSx1 dO0 dO1 sG sA sB sC sD sE "
                     "sTh sTl sFh sFl sXF sPK sPGt sPV sKE sGE sVE sWH sWL sPQ "
                     "sQE sGA sGT sPS sSQ sMK sPY sNE sPD sRC sYN sP2 sSE2 "
                     "sDQ sPF sS3 sHE sPD3 sS4 sOC sDQ4 sGU dDwB sPDh dWvB dWk2 dWv2").split():
            sem[name] = e(nc.semaphore(name))
        S = type("S", (), sem)
        dXs = [S.dXa, S.dXb]
        dWks = [S.dWk0, S.dWk1]
        dWqs = [S.dWq0, S.dWq1]
        dWcs = [S.dWc0, S.dWc1]
        dSss = [S.dSs0, S.dSs1]
        dShs = [S.dSh0, S.dSh1]
        dReKs = [S.dReK0, S.dReK1]
        dReVs = [S.dReV0, S.dReV1]
        dReQs = [S.dReQ0, S.dReQ1]
        dReEs = [S.dReE0, S.dReE1]
        dC3s = [S.dC30, S.dC31]
        dSxs = [S.dSx0, S.dSx1]
        dWvs = [S.dWv, S.dWvB, S.dWv2]
        dWkhs = [S.dWk0, S.dWk1, S.dWk2]
        dOs = [S.dO0, S.dO1]

        block = e(nc.Block())
        regs = {}

        @block.sync
        def _(sync):
            sync.dma_start(out=rk_s[:], in_=d_rank[:]).then_inc(S.dRk, 16)

        for eng_name in ("tensor", "scalar", "vector", "gpsimd"):
            @getattr(block, eng_name)
            def _(en, _n=eng_name):
                r = en.alloc_register("rk_" + _n)
                regs[_n] = r
                en.wait_ge(S.dRk, 16)
                en.reg_load(r, rk_s[0:1, 0:1])

        @block.gpsimd
        def _(g):
            g.iota(iot[:, :], [[1, 512]], channel_multiplier=0,
                   allow_small_or_imprecise_dtypes=True)
            g.iota(pid[:, :], [[1, 1]], channel_multiplier=1,
                   allow_small_or_imprecise_dtypes=True)
            g.memset(ones_col[:, :], 1.0)
            g.memset(ones_row[:, :], 1.0)
            g.dma_start(out=sg_s[:], in_=d_sg[:]).then_inc(S.dCst, 16)
            g.dma_start(out=su_s[:], in_=d_su[:]).then_inc(S.dCst, 16)
            g.dma_start(out=sd_s[:], in_=d_sd[:]).then_inc(S.dCst, 16)
            g.dma_start(out=bc_s[:], in_=d_bc[:]).then_inc(S.dCst, 16)
            for ci, cv in enumerate(CONSTS):
                g.memset(constk[:, ci:ci + 1], cv)
            g.memset(pplus[:, :], 0.0).then_inc(S.sG, 1)

        @block.vector
        def _(v):
            v.wait_ge(S.sG, 1)
            for r in range(4):
                v.tensor_scalar(pplus[:, r:r + 1], pid[:, 0:1], float(128 * r),
                                None, ALU.add)
            v.tensor_scalar(id_f[:, :], iot[:, 0:128], pid[:, 0:1], None,
                            ALU.is_equal).then_inc(S.sG, 1)
            v.wait_ge(S.sG, 2)
            v.tensor_copy(id_bf[:, :], id_f[:, :]).then_inc(S.sC, 1)

        p0 = ExitStack()
        with p0:
            e0 = p0.enter_context
            xnT_h = e0(nc.sbuf_tensor("xnT_h", [128, 16 * 2048], BF16))
            xnT_l = e0(nc.sbuf_tensor("xnT_l", [128, 16 * 2048], BF16))

            p0t = ExitStack()
            with p0t:
                e0t = p0t.enter_context
                xtok = [e0t(nc.sbuf_tensor(f"xtok{i}", [128, 2048], F32)) for i in range(2)]
                stat6 = e0t(nc.sbuf_tensor("stat6", [128, 24], F32))
                mv = e0t(nc.sbuf_tensor("mv", [128, 2], F32))
                lnv = e0t(nc.sbuf_tensor("lnv", [128, 1], F32))
                rln = e0t(nc.sbuf_tensor("rln", [128, 1], F32))
                nmr = e0t(nc.sbuf_tensor("nmr", [128, 1], F32))
                xh_tok = e0t(nc.sbuf_tensor("xh_tok", [128, 2048], BF16))
                xl_tok = e0t(nc.sbuf_tensor("xl_tok", [128, 2048], BF16))
                xf_tok = e0t(nc.sbuf_tensor("xf_tok", [128, 2048], F32))
                pTh = [e0t(nc.psum_tensor(f"pTh{i}", [128, 1024], BF16)) for i in range(2)]
                pTl = [e0t(nc.psum_tensor(f"pTl{i}", [128, 1024], BF16)) for i in range(2)]

                @block.sync
                def _(sync):
                    for tt in range(NT):
                        if tt >= 2:
                            sync.wait_ge(S.sA, tt - 1)
                            sync.wait_ge(S.sE, tt - 1)
                        sync.dma_start(out=xtok[tt % 2][:],
                                       in_=d_x[tt * 128:(tt + 1) * 128, :]
                                       ).then_inc(dXs[tt % 2], 16)

                @block.vector
                def _(v):
                    for tt in range(NT):
                        v.wait_ge(dXs[tt % 2], 16 * (tt // 2 + 1))
                        for j in range(4):
                            ins = v.bn_stats(stat6[:, 6 * j:6 * j + 6],
                                             xtok[tt % 2][:, 512 * j:512 * (j + 1)])
                        ins.then_inc(S.sA, 1)
                        v.wait_ge(S.sA, 2 * tt + 1)
                        if tt >= 1:
                            v.wait_ge(S.sB, tt)
                        v.bn_aggr(mv[:, :], stat6[:, :]).then_inc(S.sA, 1)
                        v.wait_ge(S.sB, tt + 1)
                        v.tensor_scalar(nmr[:, :], mv[:, 0:1], rln[:, 0:1], -1.0,
                                        ALU.mult, ALU.mult).then_inc(S.sC, 1)
                        v.wait_ge(S.sC, tt + 2)
                        v.wait_ge(S.sD, tt + 1)
                        if tt >= 1:
                            v.wait_ge(S.sTl, 16 * tt)
                        v.tensor_scalar(xf_tok[:, :], xtok[tt % 2][:, :], rln[:, 0:1],
                                        nmr[:, 0:1], ALU.mult, ALU.add)
                        v.tensor_sub(xl_tok[:, :], xf_tok[:, :],
                                     xh_tok[:, :]).then_inc(S.sE, 1)
                        for cc in range(NCC):
                            v.wait_ge(S.sTl, 16 * tt + cc + 1)
                            v.tensor_copy(
                                xnT_l[:, cc * 2048 + tt * 128: cc * 2048 + (tt + 1) * 128],
                                pTl[cc % 2][:, 0:128]).then_inc(S.sFl, 1)

                @block.scalar
                def _(s):
                    s.wait_ge(S.sG, 1)
                    for tt in range(NT):
                        s.wait_ge(S.sA, 2 * (tt + 1))
                        s.activation(lnv[:, :], mv[:, 1:2], AF.Ln, bias=1e-5)
                        s.activation(rln[:, :], lnv[:, :], AF.Exp,
                                     scale=-0.5).then_inc(S.sB, 1)
                        s.wait_ge(S.sC, tt + 2)
                        if tt >= 1:
                            s.wait_ge(S.sTh, 16 * tt)
                        s.activation(xh_tok[:, :], xtok[tt % 2][:, :], AF.Identity,
                                     scale=rln[:, 0:1], bias=nmr[:, 0:1]).then_inc(S.sD, 1)
                        for cc in range(NCC):
                            s.wait_ge(S.sTh, 16 * tt + cc + 1)
                            s.activation(
                                xnT_h[:, cc * 2048 + tt * 128: cc * 2048 + (tt + 1) * 128],
                                pTh[cc % 2][:, 0:128], AF.Copy).then_inc(S.sFh, 1)

                @block.tensor
                def _(t):
                    t.wait_ge(S.sC, 1)
                    for tt in range(NT):
                        t.wait_ge(S.sD, tt + 1)
                        for cc in range(NCC):
                            if 16 * tt + cc >= 2:
                                t.wait_ge(S.sFh, 16 * tt + cc - 1)
                            t.transpose(pTh[cc % 2][:, 0:128],
                                        xh_tok[:, cc * 128:(cc + 1) * 128],
                                        id_bf[:, :]).then_inc(S.sTh, 1)
                        t.wait_ge(S.sE, tt + 1)
                        for cc in range(NCC):
                            if 16 * tt + cc >= 2:
                                t.wait_ge(S.sFl, 16 * tt + cc - 1)
                            t.transpose(pTl[cc % 2][:, 0:128],
                                        xl_tok[:, cc * 128:(cc + 1) * 128],
                                        id_bf[:, :]).then_inc(S.sTl, 1)

                @block.gpsimd
                def _(g):
                    g.wait_ge(S.sFh, 256)
                    g.wait_ge(S.sFl, 256)
                    for par in range(2):
                        ctx = g.If_eq(regs["gpsimd"], 0) if par == 0 else g.Else()
                        with ctx:
                            for oc in range(2):
                                gg = OWN[par][oc]
                                for (src, dst) in ((xnT_h, d_xnh), (xnT_l, d_xnl)):
                                    sap = bass.AP(src, gg * 512,
                                                  [[32768, 128], [2048, 16], [1, 512]])
                                    dap = bass.AP(dst.tensor, oc * 512,
                                                  [[1024, 128], [131072, 16], [1, 512]])
                                    g.dma_start(out=dap, in_=sap).then_inc(S.dSxn, 16)

            if stages < 1:
                _finish(block, nc, d_out, S, [(S.dSxn, 64)], ones_row)
                return nc

            p1 = ExitStack()
            with p1:
                e1 = p1.enter_context
                pb = [e1(nc.psum_tensor(f"pb{i}", [128, 512], F32)) for i in range(8)]
                p1a = ExitStack()
                e1a = p1a.enter_context
                xnf = e1a(nc.sbuf_tensor("xnf", [128, 16 * 512], F32))
                wkb = [e1a(nc.sbuf_tensor(f"wkb{i}", [128, 1024], F32)) for i in range(3)]
                wvb = [e1a(nc.sbuf_tensor(f"wvb{i}", [128, 512], F32)) for i in range(3)]
                wgb = e1a(nc.sbuf_tensor("wgb", [128, 2048], F32))
                gstage = e1a(nc.sbuf_tensor("gstage", [128, 512], F32))
                ktile = e1a(nc.sbuf_tensor("ktile", [128, 512], F32))
                vtile = e1a(nc.sbuf_tensor("vtile", [128, 1024], F32))

                @block.sync
                def _(sync):
                    sync.wait_ge(S.sTh, 256)
                    sync.wait_ge(S.sTl, 256)
                    sync.wait_ge(S.sE, 16)
                    sync.wait_ge(S.sFh, 256)
                    sync.wait_ge(S.sFl, 256)
                    for g in range(4):
                        if "k" in P1_PARTS:
                            for kvh in range(8):
                                idx = g * 8 + kvh
                                for hf in range(2):
                                    nhk = 2 * idx + hf
                                    if nhk >= 3:
                                        sync.wait_ge(S.sPK, nhk - 2)
                                    sync.dma_start(
                                        out=bass.AP(wkb[nhk % 3], 0,
                                                    [[1024, 128], [128, 8], [1, 128]]),
                                        in_=bass.AP(d_wk.tensor,
                                                    kvh * 262144 + hf * 131072,
                                                    [[128, 128], [16384, 8], [1, 128]]),
                                    ).then_inc(dWkhs[nhk % 3], 16)
                        if g == 0 and "g" in P1_PARTS:
                            sync.wait_ge(S.sG, 3)
                            sync.dma_start(
                                out=bass.AP(wgb, 0, [[2048, 128], [128, 16], [1, 8]]),
                                in_=bass.AP(d_wg.tensor, 0, [[8, 128], [1024, 16], [1, 8]]),
                            ).then_inc(S.dWg, 16)
                        if "v" in P1_PARTS:
                            for cc in range(NCC):
                                n = 16 * g + cc
                                for hf in range(2):
                                    nhv = 2 * n + hf
                                    if nhv >= 3:
                                        sync.wait_ge(S.sPV, nhv - 2)
                                    sync.dma_start(
                                        out=wvb[nhv % 3][:],
                                        in_=d_wv[cc, :, hf * 512:(hf + 1) * 512]
                                    ).then_inc(dWvs[nhv % 3], 16)

                @block.vector
                def _(v):
                    for g in range(4):
                        if g == 0:
                            v.wait_ge(S.sFh, 256)
                            v.wait_ge(S.sFl, 256)
                        else:
                            if "k" in P1_PARTS:
                                v.wait_ge(S.sPK, 16 * g)
                            if "g" in P1_PARTS:
                                v.wait_ge(S.sPGt, g)
                            if "v" in P1_PARTS:
                                v.wait_ge(S.sPV, 32 * g)
                        for cc in range(NCC):
                            v.tensor_add(
                                xnf[:, cc * 512:(cc + 1) * 512],
                                xnT_h[:, cc * 2048 + g * 512: cc * 2048 + g * 512 + 512],
                                xnT_l[:, cc * 2048 + g * 512: cc * 2048 + g * 512 + 512],
                            ).then_inc(S.sXF, 1)
                        for tl in range(4 if "v" in P1_PARTS else 0):
                            iv = 4 * g + tl
                            v.wait_ge(S.sPV, 32 * (g + 1))
                            if iv >= 1:
                                v.wait_ge(S.dSv, 16 * iv)
                            v.tensor_copy(vtile[:, 0:512], pb[2 * tl][:, :])
                            v.tensor_copy(vtile[:, 512:1024],
                                          pb[2 * tl + 1][:, :]).then_inc(S.sVE, 2)

                @block.tensor
                def _(t):
                    for g in range(4):
                        t.wait_ge(S.sXF, 16 * (g + 1))
                        for kvh in range(8 if "k" in P1_PARTS else 0):
                            idx = g * 8 + kvh
                            if idx >= 2:
                                t.wait_ge(S.sKE, idx - 1)
                            for cc in range(NCC):
                                nhk = 2 * idx + cc // 8
                                if cc % 8 == 0:
                                    t.wait_ge(dWkhs[nhk % 3],
                                              16 * (nhk // 3 + 1))
                                ins = t.matmul(pb[idx % 2][:, :],
                                               wkb[nhk % 3][:, (cc % 8) * 128:
                                                            (cc % 8 + 1) * 128],
                                               xnf[:, cc * 512:(cc + 1) * 512],
                                               start=(cc == 0), stop=(cc == 15))
                                if cc % 8 == 7:
                                    ins.then_inc(S.sPK, 1)
                        if "g" in P1_PARTS:
                            t.wait_ge(S.dWg, 16)
                            t.wait_ge(S.sGE, g)
                            for cc in range(NCC):
                                ins = t.matmul(pb[2][:, :], wgb[:, cc * 128:(cc + 1) * 128],
                                               xnf[:, cc * 512:(cc + 1) * 512],
                                               start=(cc == 0), stop=(cc == 15))
                            ins.then_inc(S.sPGt, 1)
                        if "v" not in P1_PARTS:
                            continue
                        if "k" in P1_PARTS:
                            t.wait_ge(S.sKE, 8 * (g + 1))
                        if "g" in P1_PARTS:
                            t.wait_ge(S.sGE, g + 1)
                        t.wait_ge(S.sVE, 8 * g)
                        for cc in range(NCC):
                            n = 16 * g + cc
                            for hf in range(2):
                                nhv = 2 * n + hf
                                t.wait_ge(dWvs[nhv % 3], 16 * (nhv // 3 + 1))
                                for tl in range(4):
                                    lhs = xnf[:, cc * 512 + tl * 128:
                                              cc * 512 + (tl + 1) * 128]
                                    ins = t.matmul(pb[2 * tl + hf][:, :], lhs,
                                                   wvb[nhv % 3][:, :],
                                                   start=(cc == 0), stop=(cc == 15))
                                ins.then_inc(S.sPV, 1)

                @block.scalar
                def _(s):
                    for g in range(4):
                        for kvh in range(8 if "k" in P1_PARTS else 0):
                            idx = g * 8 + kvh
                            s.wait_ge(S.sPK, 2 * (idx + 1))
                            if idx >= 1:
                                s.wait_ge(S.dSk, 16 * idx)
                            s.activation(ktile[:, :], pb[idx % 2][:, :],
                                         AF.Copy).then_inc(S.sKE, 1)
                        if "g" in P1_PARTS:
                            s.wait_ge(S.sPGt, g + 1)
                            s.activation(gstage[:, :], pb[2][:, :], AF.Copy)
                            s.activation(gproj[:, g * 512:(g + 1) * 512], gstage[0:8, :],
                                         AF.Copy).then_inc(S.sGE, 1)

                @block.gpsimd
                def _(g_):
                    g_.memset(wgb[:, :], 0.0).then_inc(S.sG, 1)
                    for g in range(4):
                        for kvh in range(8 if "k" in P1_PARTS else 0):
                            idx = g * 8 + kvh
                            g_.wait_ge(S.sKE, idx + 1)
                            g_.dma_start(out=d_ksp[kvh, :, g * 512:(g + 1) * 512],
                                         in_=ktile[:, :]).then_inc(S.dSk, 16)
                        for tl in range(4 if "v" in P1_PARTS else 0):
                            iv = 4 * g + tl
                            g_.wait_ge(S.sVE, 2 * (iv + 1))
                            g_.dma_start(out=d_vsp[iv, :, :],
                                         in_=vtile[:, :]).then_inc(S.dSv, 16)

                p1a.close()

                q1 = ExitStack()
                with q1 if "q" in P1_PARTS else ExitStack():
                  if "q" in P1_PARTS:
                    eq = q1.enter_context
                    wqb = [eq(nc.sbuf_tensor(f"wqb{i}", [128, 2048], F32)) for i in range(2)]
                    wqh = eq(nc.sbuf_tensor("wqh", [128, 2048], BF16))
                    wql = eq(nc.sbuf_tensor("wql", [128, 2048], BF16))
                    qt2 = eq(nc.sbuf_tensor("qt2", [128, 1024], F32))

                    @block.sync
                    def _(sync):
                        if "v" in P1_PARTS:
                            sync.wait_ge(S.sPV, 128)
                        if "k" in P1_PARTS:
                            sync.wait_ge(S.dSk, 512)
                        if "v" in P1_PARTS:
                            sync.wait_ge(S.dSv, 256)
                        for qc in range(16):
                            if qc >= 2:
                                sync.wait_ge(S.sWH, qc - 1)
                                sync.wait_ge(S.sWL, qc - 1)
                            sync.dma_start(
                                out=bass.AP(wqb[qc % 2], 0, [[2048, 128], [128, 16], [1, 128]]),
                                in_=bass.AP(d_wq.tensor, qc * 262144,
                                            [[128, 128], [16384, 16], [1, 128]]),
                            ).then_inc(dWqs[qc % 2], 16)

                    @block.scalar
                    def _(s):
                        for qc in range(16):
                            s.wait_ge(dWqs[qc % 2], 16 * (qc // 2 + 1))
                            if qc >= 1:
                                s.wait_ge(S.sPQ, 2 * qc)
                            s.activation(wqh[:, :], wqb[qc % 2][:, :],
                                         AF.Copy).then_inc(S.sWH, 1)
                            for oc in range(2):
                                s.wait_ge(S.sPQ, 2 * qc + oc + 1)
                                if qc >= 1 and oc == 0:
                                    s.wait_ge(S.dSq, 16 * qc)
                                s.activation(qt2[:, oc * 512:(oc + 1) * 512],
                                             pb[oc][:, :], AF.Copy,
                                             scale=SCALE).then_inc(S.sQE, 1)

                    @block.vector
                    def _(v):
                        for qc in range(16):
                            v.wait_ge(S.sWH, qc + 1)
                            if qc >= 1:
                                v.wait_ge(S.sPQ, 2 * qc)
                            v.tensor_sub(wql[:, :], wqb[qc % 2][:, :],
                                         wqh[:, :]).then_inc(S.sWL, 1)

                    @block.tensor
                    def _(t):
                        t.wait_ge(S.sVE, 32)
                        t.wait_ge(S.sKE, 32)
                        for qc in range(16):
                            t.wait_ge(S.sWH, qc + 1)
                            t.wait_ge(S.sWL, qc + 1)
                            for par in range(2):
                                ctx = t.If_eq(regs["tensor"], 0) if par == 0 else t.Else()
                                with ctx:
                                    for oc in range(2):
                                        if qc >= 1:
                                            t.wait_ge(S.sQE, 2 * (qc - 1) + oc + 1)
                                        gg = OWN[par][oc]
                                        for cc in range(NCC):
                                            co = cc * 2048 + gg * 512
                                            t.matmul(pb[oc][:, :],
                                                     wqh[:, cc * 128:(cc + 1) * 128],
                                                     xnT_h[:, co:co + 512],
                                                     start=(cc == 0), stop=False)
                                            t.matmul(pb[oc][:, :],
                                                     wqh[:, cc * 128:(cc + 1) * 128],
                                                     xnT_l[:, co:co + 512],
                                                     start=False, stop=False)
                                            ins = t.matmul(pb[oc][:, :],
                                                           wql[:, cc * 128:(cc + 1) * 128],
                                                           xnT_h[:, co:co + 512],
                                                           start=False, stop=(cc == 15))
                                        ins.then_inc(S.sPQ, 1)

                    @block.gpsimd
                    def _(g_):
                        for qc in range(16):
                            g_.wait_ge(S.sQE, 2 * (qc + 1))
                            g_.dma_start(out=d_qsp[qc, :, :],
                                         in_=qt2[:, :]).then_inc(S.dSq, 16)

        if stages < 2:
            w = [(S.dSxn, 64)]
            if "k" in P1_PARTS:
                w.append((S.dSk, 512))
            if "v" in P1_PARTS:
                w.append((S.dSv, 256))
            if "q" in P1_PARTS:
                w.append((S.dSq, 256))
            _finish(block, nc, d_out, S, w, ones_row)
            return nc

        s2 = ExitStack()
        with s2:
            e2 = s2.enter_context
            cmask = e2(nc.sbuf_tensor("cmask", [128, 4 * 512], F32))
            allone = e2(nc.sbuf_tensor("allone", [128, 512], F32))
            eps16 = e2(nc.sbuf_tensor("eps16", [16, 2048], F32))
            b2tok = e2(nc.sbuf_tensor("b2tok", [128, 256], F32))
            sg2 = ExitStack()
            eg2 = sg2.enter_context
            ubuf = eg2(nc.sbuf_tensor("ubuf", [8, 2048], F32))
            gbuf = eg2(nc.sbuf_tensor("gbuf", [8, 2048], F32))
            acc = eg2(nc.sbuf_tensor("acc", [8, 2048], F32))
            tmp8 = eg2(nc.sbuf_tensor("tmp8", [8, 2048], F32))
            lgtokP = eg2(nc.sbuf_tensor("lgtokP", [128, 2048], F32))
            estage2 = eg2(nc.sbuf_tensor("estage2", [128, 512], F32))
            lstage = eg2(nc.sbuf_tensor("lstage", [128, 512], F32))
            logg16 = eg2(nc.sbuf_tensor("logg16", [16, 2048], F32))
            pgx = eg2(nc.psum_tensor("pgx", [128, 512], F32))
            pgc = eg2(nc.psum_tensor("pgc", [128, 512], F32))
            s2.callback(sg2.close)

            @block.gpsimd
            def _(g):
                g.wait_ge(S.sPQ, 32)
                g.wait_ge(S.dSq, 256)
                g.memset(allone[:, :], 1.0)
                g.memset(lgtokP[:, :], 0.0).then_inc(S.sG, 1)

            @block.vector
            def _(v):
                v.wait_ge(S.sG, 4)
                for r in range(4):
                    v.tensor_scalar(cmask[:, r * 512:(r + 1) * 512], iot[:, :],
                                    pplus[:, r:r + 1], None, ALU.is_ge).then_inc(S.sMK, 1)

            if stages < 3:
                _finish(block, nc, d_out, S,
                        [(S.dSxn, 64), (S.dSk, 512), (S.dSv, 256), (S.dSq, 256)],
                        ones_row)
                return nc

            @block.scalar
            def _(s):
                s.wait_ge(S.sGA, 1)
                s.activation(ubuf[:, :], gbuf[:, :], AF.Exp,
                             scale=-1.0).then_inc(S.sGA, 1)
                for g in range(4):
                    s.wait_ge(S.sGT, 16 + g + 1)
                    s.activation(estage2[:, :], pgc[:, :], AF.Exp,
                                 scale=-1.0).then_inc(S.sGA, 1)
                for tt in range(16):
                    s.wait_ge(S.sGT, 20 + tt + 1)
                    s.activation(b2tok[:, tt * 16:(tt + 1) * 16], pgx[:, 0:16],
                                 AF.Exp, scale=-0.5).then_inc(S.sGA, 1)

            @block.vector
            def _(v):
                v.wait_ge(S.sGE, 4)
                v.wait_ge(S.sPQ, 32)
                v.wait_ge(S.dSq, 256)
                v.tensor_scalar(gbuf[:, :], gproj[:, :], GO_HI, GO_LO,
                                ALU.add, ALU.add).then_inc(S.sGA, 1)
                v.wait_ge(S.sGA, 2)
                v.tensor_scalar(acc[:, :], ubuf[:, :], -1.0 / 6.0, 1.0 / 5.0,
                                ALU.mult, ALU.add)
                for k in (4, 3, 2):
                    v.tensor_tensor(tmp8[:, :], ubuf[:, :], acc[:, :], ALU.mult)
                    v.tensor_scalar(acc[:, :], tmp8[:, :], -1.0, 1.0 / k,
                                    ALU.mult, ALU.add)
                v.tensor_tensor(tmp8[:, :], ubuf[:, :], acc[:, :], ALU.mult)
                v.tensor_scalar(acc[:, :], tmp8[:, :], -1.0, 1.0, ALU.mult, ALU.add)
                v.tensor_tensor(tmp8[:, :], ubuf[:, :], acc[:, :], ALU.mult)
                v.tensor_scalar(acc[:, :], tmp8[:, :], -1.0, None,
                                ALU.mult).then_inc(S.sGA, 2)
                for tt in range(16):
                    v.wait_ge(S.sGT, tt + 1)
                    v.tensor_copy(
                        bass.AP(lgtokP, tt * 128, [[2048, 128], [2, 8]]),
                        pgx[:, 0:8]
                    )
                    v.tensor_copy(
                        bass.AP(lgtokP, tt * 128 + 1, [[2048, 128], [2, 8]]),
                        pgx[:, 0:8]
                    ).then_inc(S.sGA, 1)
                for g in range(4):
                    v.wait_ge(S.sGA, 21 + 3 * g)
                    v.tensor_scalar(eps16[:, g * 512:(g + 1) * 512], estage2[0:16, :],
                                    1e-6, None, ALU.mult)
                    v.wait_ge(S.sGT, 16 + g + 1)
                    v.tensor_copy(lstage[:, :], pgc[:, :])
                    v.tensor_copy(logg16[:, g * 512:(g + 1) * 512],
                                  lstage[0:16, :]).then_inc(S.sGA, 2)

            @block.tensor
            def _(t):
                t.wait_ge(S.sPQ, 32)
                for tt in range(16):
                    t.wait_ge(S.sGA, 4)
                    if tt >= 1:
                        t.wait_ge(S.sGA, 4 + tt)
                    t.transpose(pgx[:, 0:8], acc[:, tt * 128:(tt + 1) * 128],
                                id_f[0:8, 0:8]).then_inc(S.sGT, 1)
                for g in range(4):
                    t.wait_ge(S.sGA, 20)
                    if g >= 1:
                        t.wait_ge(S.sGA, 20 + 3 * g)
                    for st in range(4 * g + 4):
                        rhs = (allone[:, :] if st < 4 * g
                               else cmask[:, (st - 4 * g) * 512:(st - 4 * g + 1) * 512])
                        ins = t.matmul(pgc[:, :], lgtokP[:, st * 128:(st + 1) * 128], rhs,
                                       start=(st == 0), stop=(st == 4 * g + 3))
                    ins.then_inc(S.sGT, 1)
                for tt in range(16):
                    t.wait_ge(S.sGA, 32)
                    if tt >= 1:
                        t.wait_ge(S.sGA, 32 + tt)
                    t.transpose(pgx[:, 0:16], logg16[:, tt * 128:(tt + 1) * 128],
                                id_f[0:16, 0:16]).then_inc(S.sGT, 1)

            if DBG:
                @block.gpsimd
                def _(g):
                    g.wait_ge(S.sGA, 48)
                    g.dma_start(out=d_dbg[0:16, :], in_=eps16[:, :]).then_inc(S.dCst, 16)
                    g.dma_start(out=d_dbg[16:144, 0:256], in_=b2tok[:, :]).then_inc(S.dCst, 16)
                    g.dma_start(out=d_dbg[144:152, :], in_=gproj[:, :]).then_inc(S.dCst, 16)
                    g.dma_start(out=d_dbg[152:160, :], in_=acc[:, :]).then_inc(S.dCst, 16)

            sg2.close()

            kbuf = [e2(nc.sbuf_tensor(f"kbuf{i}", [128, 2048], F32)) for i in range(2)]
            vbuf = [e2(nc.sbuf_tensor(f"vbuf{i}", [128, 2048], F32)) for i in range(2)]
            qbuf = [e2(nc.sbuf_tensor(f"qbuf{i}", [128, 1024], F32)) for i in range(2)]
            stA = [e2(nc.sbuf_tensor(f"stA{i}", [128, 512], F32)) for i in range(2)]
            stB = [e2(nc.sbuf_tensor(f"stB{i}", [128, 512], F32)) for i in range(2)]
            epsrow = [e2(nc.sbuf_tensor(f"epsrow{i}", [1, 512], F32)) for i in range(2)]
            rb = e2(nc.sbuf_tensor("rb", [128, 512], F32))
            y_all = e2(nc.sbuf_tensor("y_all", [128, 16 * 1024], F32))
            pst = [e2(nc.psum_tensor(f"pst{i}", [128, 512], F32)) for i in range(2)]
            py = [e2(nc.psum_tensor(f"py{i}", [128, 512], F32)) for i in range(2)]
            pn = [e2(nc.psum_tensor(f"pn{i}", [128, 512], F32)) for i in range(2)]

            @block.gpsimd
            def _(g_):
                g_.wait_ge(S.dSk, 512)
                g_.wait_ge(S.dSv, 256)
                g_.wait_ge(S.dSq, 256)
                g_.wait_ge(S.sGA, 48)
                for kvh in range(KVH):
                    if kvh >= 2:
                        g_.wait_ge(S.sYN, 4 * (kvh - 1))
                    g_.dma_start(out=kbuf[kvh % 2][:],
                                 in_=d_ksp[kvh, :, :]).then_inc(dReKs[kvh % 2], 16)
                    vap = bass.AP(d_vsp.tensor, kvh * 128,
                                  [[1024, 128], [131072, 16], [1, 128]])
                    sap = bass.AP(vbuf[kvh % 2], 0, [[2048, 128], [128, 16], [1, 128]])
                    g_.dma_start(out=sap, in_=vap).then_inc(dReVs[kvh % 2], 16)
                    for j in range(2):
                        qh = 2 * kvh + j
                        if qh >= 2:
                            g_.wait_ge(S.sYN, 2 * (qh - 1))
                        g_.dma_start(out=qbuf[qh % 2][:],
                                     in_=d_qsp[qh, :, :]).then_inc(dReQs[qh % 2], 16)
                        for par in range(2):
                            ctx = g_.If_eq(regs["gpsimd"], 0) if par == 0 else g_.Else()
                            with ctx:
                                for oc in range(2):
                                    gg = OWN[par][oc]
                                    if qh >= 1:
                                        g_.wait_ge(S.sPD, 2 * (qh - 1) + oc + 1)
                                    g_.dma_start(
                                        out=epsrow[oc][:],
                                        in_=eps16[qh:qh + 1, gg * 512:(gg + 1) * 512]
                                    ).then_inc(dReEs[oc], 16)

            def ps_base(par, qh, oc):
                return qh * 20 + (BOF(par, 0) if oc == 1 else 0)

            @block.tensor
            def _(t):
                t.wait_ge(S.sGA, 48)
                for kvh in range(KVH):
                    for j in range(2):
                        qh = 2 * kvh + j
                        for par in range(2):
                            ctx = t.If_eq(regs["tensor"], 0) if par == 0 else t.Else()
                            with ctx:
                                if j == 0:
                                    t.wait_ge(dReKs[kvh % 2], 16 * (kvh // 2 + 1))
                                    t.wait_ge(dReVs[kvh % 2], 16 * (kvh // 2 + 1))
                                t.wait_ge(dReQs[qh % 2], 16 * (qh // 2 + 1))
                                for oc in range(2):
                                    B = BOF(par, oc)
                                    base = ps_base(par, qh, oc)
                                    ocn = 2 * qh + oc

                                    def avn(st):
                                        sel = stB if st >= B - 4 else stA
                                        if st >= B - 4:
                                            t.wait_ge(S.sMK, 4 + 4 * ocn + (st - (B - 4)) + 1)
                                        else:
                                            t.wait_ge(S.sSQ, base + st + 1)
                                        t.matmul(py[ocn % 2][:, :],
                                                 vbuf[kvh % 2][:, st * 128:(st + 1) * 128],
                                                 sel[st % 2][:, :],
                                                 start=(st == 0), stop=(st == B - 1))
                                        t.matmul(pn[ocn % 2][:, :], allone[:, 0:128],
                                                 sel[st % 2][:, :],
                                                 start=(st == 0), stop=False
                                                 ).then_inc(S.sPY, 1)

                                    if ocn >= 2:
                                        t.wait_ge(S.sYN, ocn - 1)
                                        t.wait_ge(S.sRC, ocn - 1)
                                    for st in range(B):
                                        if base + st >= 2:
                                            t.wait_ge(S.sSQ, base + st - 1)
                                        t.matmul(pst[st % 2][:, :],
                                                 kbuf[kvh % 2][:, st * 128:(st + 1) * 128],
                                                 qbuf[qh % 2][:, oc * 512:(oc + 1) * 512],
                                                 start=True, stop=True
                                                 ).then_inc(S.sPS, 1)
                                        if st >= 1:
                                            avn(st - 1)
                                    avn(B - 1)
                                    t.wait_ge(dReEs[oc], 16 * (qh + 1))
                                    t.matmul(pn[ocn % 2][:, :], ones_row[:, :],
                                             epsrow[oc][0:1, :],
                                             start=False, stop=True).then_inc(S.sPD, 1)

            @block.scalar
            def _(s):
                for kvh in range(KVH):
                    for j in range(2):
                        qh = 2 * kvh + j
                        for par in range(2):
                            ctx = s.If_eq(regs["scalar"], 0) if par == 0 else s.Else()
                            with ctx:
                                for oc in range(2):
                                    B = BOF(par, oc)
                                    base = ps_base(par, qh, oc)
                                    for st in range(B):
                                        s.wait_ge(S.sPS, base + st + 1)
                                        if base + st >= 2:
                                            s.wait_ge(S.sPY, base + st - 1)
                                        s.activation(
                                            stA[st % 2][:, :], pst[st % 2][:, :],
                                            AF.Square,
                                            scale=b2tok[:, st * 16 + qh: st * 16 + qh + 1],
                                        ).then_inc(S.sSQ, 1)

            @block.vector
            def _(v):
                for kvh in range(KVH):
                    for j in range(2):
                        qh = 2 * kvh + j
                        for par in range(2):
                            ctx = v.If_eq(regs["vector"], 0) if par == 0 else v.Else()
                            with ctx:
                                for oc in range(2):
                                    B = BOF(par, oc)
                                    base = ps_base(par, qh, oc)
                                    ocn = 2 * qh + oc
                                    for st in range(B - 4, B):
                                        r = st - (B - 4)
                                        v.wait_ge(S.sSQ, base + st + 1)
                                        if base + st >= 2:
                                            v.wait_ge(S.sPY, base + st - 1)
                                        v.tensor_tensor(
                                            stB[st % 2][:, :], stA[st % 2][:, :],
                                            cmask[:, r * 512:(r + 1) * 512],
                                            ALU.mult).then_inc(S.sMK, 1)
                                    v.wait_ge(S.sPD, ocn + 1)
                                    v.reciprocal(rb[:, :], pn[ocn % 2][:, :]
                                                 ).then_inc(S.sRC, 1)
                                    v.tensor_tensor(
                                        y_all[:, qh * 1024 + oc * 512: qh * 1024 + (oc + 1) * 512],
                                        py[ocn % 2][:, :], rb[:, :],
                                        ALU.mult).then_inc(S.sYN, 1)

            if stages < 4:
                if DBG:
                    @block.gpsimd
                    def _(g_):
                        g_.wait_ge(S.sYN, 32)
                        for qc in range(16):
                            g_.dma_start(out=d_qsp[qc, :, :],
                                         in_=y_all[:, qc * 1024:(qc + 1) * 1024]
                                         ).then_inc(S.dCst, 16)
                _finish(block, nc, d_out, S, [(S.sYN, 32)], ones_row)
                return nc

            wcb = [e2(nc.sbuf_tensor(f"wcb{i}", [128, 2048], F32)) for i in range(2)]
            stile = [e2(nc.sbuf_tensor(f"stile{i}", [128, 512], F32)) for i in range(2)]
            ps2 = [e2(nc.psum_tensor(f"ps2{i}", [128, 512], F32)) for i in range(2)]

            @block.sync
            def _(sync):
                sync.wait_ge(S.sYN, 32)
                for co in range(16):
                    if co >= 2:
                        sync.wait_ge(S.sP2, 2 * (co - 1))
                    sync.dma_start(
                        out=bass.AP(wcb[co % 2], 0, [[2048, 128], [128, 16], [1, 128]]),
                        in_=bass.AP(d_wc.tensor, co * 262144,
                                    [[128, 128], [16384, 16], [1, 128]]),
                    ).then_inc(dWcs[co % 2], 16)

            @block.tensor
            def _(t):
                t.wait_ge(S.sYN, 32)
                for co in range(16):
                    t.wait_ge(dWcs[co % 2], 16 * (co // 2 + 1))
                    for oc in range(2):
                        n2 = 2 * co + oc
                        if n2 >= 2:
                            t.wait_ge(S.sSE2, n2 - 1)
                        for qc in range(16):
                            ins = t.matmul(ps2[oc][:, :],
                                           wcb[co % 2][:, qc * 128:(qc + 1) * 128],
                                           y_all[:, qc * 1024 + oc * 512: qc * 1024 + (oc + 1) * 512],
                                           start=(qc == 0), stop=(qc == 15))
                        ins.then_inc(S.sP2, 1)

            @block.scalar
            def _(s):
                s.wait_ge(S.dCst, 64)
                for co in range(16):
                    for oc in range(2):
                        n2 = 2 * co + oc
                        s.wait_ge(S.sP2, n2 + 1)
                        if n2 >= 2:
                            s.wait_ge(dSss[n2 % 2], 16 * ((n2 - 2) // 2 + 1))
                        s.activation(stile[n2 % 2][:, :], ps2[oc][:, :], AF.Identity,
                                     bias=bc_s[:, co:co + 1]).then_inc(S.sSE2, 1)

            @block.gpsimd
            def _(g_):
                for co in range(16):
                    for oc in range(2):
                        n2 = 2 * co + oc
                        g_.wait_ge(S.sSE2, n2 + 1)
                        g_.dma_start(out=d_ssp[co, :, oc * 512:(oc + 1) * 512],
                                     in_=stile[n2 % 2][:, :]).then_inc(dSss[n2 % 2], 16)

        if stages < 5:
            _finish(block, nc, d_out, S, [(S.dSs0, 256), (S.dSs1, 256)], ones_row)
            return nc

        g0.close()
        s3 = ExitStack()
        with s3:
            e3 = s3.enter_context
            xno_h = e3(nc.sbuf_tensor("xno_h", [128, 16 * 512], BF16))
            xno_l = e3(nc.sbuf_tensor("xno_l", [128, 16 * 512], BF16))
            gwb = [e3(nc.sbuf_tensor(f"gwb{i}", [128, 2048], BF16)) for i in range(2)]
            uwb = [e3(nc.sbuf_tensor(f"uwb{i}", [128, 2048], BF16)) for i in range(2)]
            hbh = e3(nc.sbuf_tensor("hbh", [128, 64 * 512], BF16))
            hbl = e3(nc.sbuf_tensor("hbl", [128, 64 * 512], BF16))
            dwb = [e3(nc.sbuf_tensor(f"dwb{i}", [128, 2048], BF16)) for i in range(3)]
            obf = e3(nc.sbuf_tensor("obf", [128, 128], BF16))
            cs128 = e3(nc.sbuf_tensor("cs128", [128, 512], F32))
            gtmp = e3(nc.sbuf_tensor("gtmp", [128, 512], F32))
            sgl = e3(nc.sbuf_tensor("sgl", [128, 512], F32))
            uu = e3(nc.sbuf_tensor("uu", [128, 512], F32))
            go = e3(nc.sbuf_tensor("go", [128, 512], F32))
            st2 = e3(nc.sbuf_tensor("st2", [128, 512], F32))
            xt2 = e3(nc.sbuf_tensor("xt2", [128, 512], F32))
            ot2 = [e3(nc.sbuf_tensor(f"ot2{i}", [128, 512], F32)) for i in range(2)]
            pA = [e3(nc.psum_tensor(f"pA{i}", [128, 512], F32)) for i in range(2)]
            pB = [e3(nc.psum_tensor(f"pB{i}", [128, 512], F32)) for i in range(2)]
            pD = [e3(nc.psum_tensor(f"pD{i}", [128, 512], F32)) for i in range(2)]
            pC = [e3(nc.psum_tensor(f"pC{i}", [128, 512], F32)) for i in range(2)]
            dDws = [S.dDw, S.dDwB, S.dHb]

            @block.gpsimd
            def _(g_):
                g_.wait_ge(S.dSxn, 64)
                g_.wait_ge(S.sP2, 32)
                g_.wait_ge(S.dSs0, 256)
                g_.wait_ge(S.dSs1, 256)
                g_.memset(obf[:, :], 1.0).then_inc(S.dXn3, 1)
                for oc in range(2):
                    if oc == 1:
                        g_.wait_ge(S.sPF, NI)
                    for (dst, src) in ((xno_h, d_xnh), (xno_l, d_xnl)):
                        g_.dma_start(
                            out=bass.AP(dst, 0, [[8192, 128], [512, 16], [1, 512]]),
                            in_=bass.AP(src.tensor, oc * 512,
                                        [[1024, 128], [131072, 16], [1, 512]]),
                        ).then_inc(S.dXn3, 16)
                    for i in range(NI):
                        n = oc * NI + i
                        if n >= 2:
                            g_.wait_ge(S.sPF, n - 1)
                        g_.dma_start(
                            out=bass.AP(gwb[n % 2], 0, [[2048, 128], [128, 16], [1, 128]]),
                            in_=bass.AP(d_gw.tensor, i * 262144,
                                        [[128, 128], [16384, 16], [1, 128]]),
                        ).then_inc(dC3s[n % 2], 16)
                        g_.dma_start(
                            out=bass.AP(uwb[n % 2], 0, [[2048, 128], [128, 16], [1, 128]]),
                            in_=bass.AP(d_uw.tensor, i * 262144,
                                        [[128, 128], [16384, 16], [1, 128]]),
                        ).then_inc(dC3s[n % 2], 16)
                    if DBG:
                        g_.wait_ge(S.sHE, NI * (oc + 1))
                        g_.dma_start(
                            out=bass.AP(d_hh.tensor, oc * 4194304,
                                        [[512, 128], [65536, 64], [1, 512]]),
                            in_=bass.AP(hbh, 0, [[32768, 128], [512, 64], [1, 512]]),
                        ).then_inc(dShs[0], 16)
                        g_.dma_start(
                            out=bass.AP(d_hl.tensor, oc * 4194304,
                                        [[512, 128], [65536, 64], [1, 512]]),
                            in_=bass.AP(hbl, 0, [[32768, 128], [512, 64], [1, 512]]),
                        ).then_inc(dShs[1], 16)
                    for co in range(NT):
                        m = NT * oc + co
                        for qf in range(4):
                            nh = 4 * m + qf
                            if nh >= 3:
                                g_.wait_ge(S.sPDh, nh - 2)
                            g_.dma_start(
                                out=bass.AP(dwb[nh % 3], 0,
                                            [[2048, 128], [128, 16], [1, 128]]),
                                in_=bass.AP(d_dw.tensor,
                                            co * 1048576 + qf * 262144,
                                            [[128, 128], [16384, 16], [1, 128]]),
                            ).then_inc(dDws[nh % 3], 16)
                        if m >= 1:
                            g_.wait_ge(S.sOC, m)
                        g_.dma_start(out=st2[:, :],
                                     in_=d_ssp[co, :, oc * 512:(oc + 1) * 512]
                                     ).then_inc(dSxs[m % 2], 16)
                        g_.dma_start(
                            out=xt2[:, :],
                            in_=bass.AP(d_xT.tensor, co * 128 * 1024 + oc * 512,
                                        [[1024, 128], [1, 512]]),
                        ).then_inc(dSxs[m % 2], 16)

            @block.tensor
            def _(t):
                t.wait_ge(S.sSE2, 32)
                for oc in range(2):
                    t.wait_ge(S.dXn3, 1 + 32 * (oc + 1))
                    for cc in range(NCC):
                        t.matmul(pC[oc][:, :], obf[:, :],
                                 xno_h[:, cc * 512:(cc + 1) * 512],
                                 start=(cc == 0), stop=False)
                        ins = t.matmul(pC[oc][:, :], obf[:, :],
                                       xno_l[:, cc * 512:(cc + 1) * 512],
                                       start=False, stop=(cc == NCC - 1))
                    ins.then_inc(S.sDQ, 1)
                    for i in range(NI):
                        n = oc * NI + i
                        t.wait_ge(dC3s[n % 2], 32 * (n // 2 + 1))
                        if n >= 2:
                            t.wait_ge(S.sGU, n - 1)
                            t.wait_ge(S.sHE, n - 1)
                        for cc in range(NCC):
                            xo = cc * 512
                            t.matmul(pA[n % 2][:, :],
                                     gwb[n % 2][:, cc * 128:(cc + 1) * 128],
                                     xno_h[:, xo:xo + 512],
                                     start=(cc == 0), stop=False)
                            t.matmul(pA[n % 2][:, :],
                                     gwb[n % 2][:, cc * 128:(cc + 1) * 128],
                                     xno_l[:, xo:xo + 512], start=False,
                                     stop=(cc == NCC - 1))
                        for cc in range(NCC):
                            xo = cc * 512
                            t.matmul(pB[n % 2][:, :],
                                     uwb[n % 2][:, cc * 128:(cc + 1) * 128],
                                     xno_h[:, xo:xo + 512],
                                     start=(cc == 0), stop=False)
                            ins = t.matmul(pB[n % 2][:, :],
                                           uwb[n % 2][:, cc * 128:(cc + 1) * 128],
                                           xno_l[:, xo:xo + 512], start=False,
                                           stop=(cc == NCC - 1))
                        ins.then_inc(S.sPF, 1)
                    t.wait_ge(S.sHE, NI * (oc + 1))
                    for co in range(NT):
                        m = NT * oc + co
                        if m >= 2:
                            t.wait_ge(S.sS4, m - 1)
                        for i in range(NI):
                            qf, io = i // 16, i % 16
                            nh = 4 * m + qf
                            if io == 0:
                                t.wait_ge(S.sDQ4, nh + 1)
                            t.matmul(pD[m % 2][:, :],
                                     dwb[nh % 3][:, io * 128:(io + 1) * 128],
                                     hbh[:, i * 512:(i + 1) * 512],
                                     start=(i == 0), stop=False)
                            ins = t.matmul(pD[m % 2][:, :],
                                           dwb[nh % 3][:, io * 128:(io + 1) * 128],
                                           hbl[:, i * 512:(i + 1) * 512],
                                           start=False, stop=(i == NI - 1))
                            if io == 15:
                                ins.then_inc(S.sPDh, 1)

            def _combine(v, m):
                v.wait_ge(S.sS4, m + 1)
                v.wait_ge(dSxs[m % 2], 32 * (m // 2 + 1))
                if m >= 2:
                    v.wait_ge(dOs[m % 2], 16 * ((m - 2) // 2 + 1))
                v.tensor_tensor(ot2[m % 2][:, :], st2[:, :], go[:, :],
                                ALU.mult)
                v.tensor_add(ot2[m % 2][:, :], ot2[m % 2][:, :],
                             xt2[:, :]).then_inc(S.sOC, 1)

            @block.vector
            def _(v):
                v.wait_ge(S.dCst, 64)
                for oc in range(2):
                    v.wait_ge(S.sDQ, oc + 1)
                    v.tensor_scalar(cs128[:, :], pC[oc][:, :], -128.0, None,
                                    ALU.mult)
                    for i in range(NI):
                        n = oc * NI + i
                        v.wait_ge(S.sPF, n + 1)
                        if n >= 1:
                            v.wait_ge(S.sS3, n)
                        v.tensor_add(gtmp[:, :], pA[n % 2][:, :],
                                     cs128[:, :]).then_inc(S.sGU, 1)
                        v.tensor_add(uu[:, :], pB[n % 2][:, :], cs128[:, :])
                        v.tensor_scalar(uu[:, :], uu[:, :], su_s[:, i:i + 1],
                                        None, ALU.mult)
                        if oc == 1 and i == 0:
                            v.wait_ge(S.sPDh, 4 * NT)
                            if DBG:
                                v.wait_ge(dShs[0], 16)
                                v.wait_ge(dShs[1], 16)
                        v.wait_ge(S.sS3, n + 1)
                        v.tensor_tensor(uu[:, :], sgl[:, :], uu[:, :],
                                        ALU.mult)
                        v.tensor_copy(hbh[:, i * 512:(i + 1) * 512], uu[:, :])
                        v.tensor_sub(hbl[:, i * 512:(i + 1) * 512], uu[:, :],
                                     hbh[:, i * 512:(i + 1) * 512]
                                     ).then_inc(S.sHE, 1)
                    for co in range(NT):
                        m = NT * oc + co
                        for qf in range(4):
                            nh = 4 * m + qf
                            v.wait_ge(dDws[nh % 3], 16 * (nh // 3 + 1))
                            if nh >= 3:
                                v.wait_ge(S.sPDh, nh - 2)
                            v.tensor_scalar(dwb[nh % 3][:, :], dwb[nh % 3][:, :],
                                            -128.0, None, ALU.add
                                            ).then_inc(S.sDQ4, 1)
                        if co >= 1:
                            _combine(v, m - 1)
                    _combine(v, NT * oc + NT - 1)

            @block.scalar
            def _(s):
                s.wait_ge(S.dCst, 64)
                for oc in range(2):
                    for i in range(NI):
                        n = oc * NI + i
                        s.wait_ge(S.sGU, n + 1)
                        if n >= 1:
                            s.wait_ge(S.sHE, n)
                        s.activation(sgl[:, :], gtmp[:, :],
                                     AF.Silu, scale=sg_s[:, i:i + 1]
                                     ).then_inc(S.sS3, 1)
                    for co in range(NT):
                        m = NT * oc + co
                        s.wait_ge(S.sPDh, 4 * (m + 1))
                        if m >= 1:
                            s.wait_ge(S.sOC, m)
                        s.activation(go[:, :], pD[m % 2][:, :], AF.Silu,
                                     scale=sd_s[:, co:co + 1]).then_inc(S.sS4, 1)

            @block.sync
            def _(sync):
                for oc in range(2):
                    for co in range(NT):
                        m = NT * oc + co
                        sync.wait_ge(S.sOC, m + 1)
                        sync.dma_start(out=d_out[co * 128:(co + 1) * 128,
                                                 oc * 512:(oc + 1) * 512],
                                       in_=ot2[m % 2][:, :]).then_inc(dOs[m % 2], 16)
                sync.wait_ge(S.dO0, 256)
                sync.wait_ge(S.dO1, 256)


    return nc


def _finish(block, nc, d_out, S, waits, src):
    @block.sync
    def _(sync):
        for (sm, v) in waits:
            sync.wait_ge(sm, v)
        sync.dma_start(out=d_out[0:1, 0:128], in_=src[0:1, 0:128]).then_inc(S.dO0, 16)
        sync.wait_ge(S.dO0, 16)


_NC_CACHE = {}


def _get_nc():
    if "nc" not in _NC_CACHE:
        _NC_CACHE["nc"] = build_nc()
    return _NC_CACHE["nc"]


def make_inputs(core, x, Wq, Wk, Wv, Wg, Wc, bc, gw, gs, uw, us, dw, ds):
    b, par = core // 2, core % 2
    own_cols = np.concatenate([np.arange(g * 512, (g + 1) * 512) for g in OWN[par]])
    xb = np.ascontiguousarray(x[b])
    return {
        "rank": np.array([[par]], dtype=np.int32),
        "x_nat": xb,
        "xT_own": np.ascontiguousarray(xb.T[:, own_cols]),
        "wq_t": np.ascontiguousarray(Wq.reshape(16, 128, 16, 128).transpose(2, 0, 1, 3)),
        "wk_t": np.ascontiguousarray(Wk.reshape(16, 128, 8, 128).transpose(2, 0, 1, 3)),
        "wv_r": np.ascontiguousarray(Wv.reshape(16, 128, 1024)),
        "wg_r": np.ascontiguousarray(Wg.reshape(16, 128, 8)),
        "wc_t": np.ascontiguousarray(Wc.reshape(16, 128, 16, 128).transpose(2, 0, 1, 3)),
        "bc_sb": np.ascontiguousarray(bc.reshape(16, 128).T),
        "gwt": np.ascontiguousarray(
            gw.T.reshape(16, 128, 64, 128).transpose(2, 0, 1, 3)).astype(np.uint8),
        "uwt": np.ascontiguousarray(
            uw.T.reshape(16, 128, 64, 128).transpose(2, 0, 1, 3)).astype(np.uint8),
        "dwt": np.ascontiguousarray(
            dw.T.reshape(64, 128, 16, 128).transpose(2, 0, 1, 3)).astype(np.uint8),
        "sg_sb": np.ascontiguousarray(gs.reshape(64, 128).T),
        "su_sb": np.ascontiguousarray(us.reshape(64, 128).T),
        "sd_sb": np.ascontiguousarray(ds.reshape(16, 128).T),
    }


def kernel(L0_attn_hidden_states, L0_attn_Wq, L0_attn_Wk, L0_attn_Wv,
           L0_attn_Wg, L0_attn_Wc, L0_attn_bc, L0_ffn_gate_w,
           L0_ffn_gate_scale, L0_ffn_up_w, L0_ffn_up_scale, L0_ffn_down_w,
           L0_ffn_down_scale):
    x = np.asarray(L0_attn_hidden_states, dtype=np.float32)
    args = (x, np.asarray(L0_attn_Wq, np.float32), np.asarray(L0_attn_Wk, np.float32),
            np.asarray(L0_attn_Wv, np.float32), np.asarray(L0_attn_Wg, np.float32),
            np.asarray(L0_attn_Wc, np.float32), np.asarray(L0_attn_bc, np.float32),
            np.asarray(L0_ffn_gate_w, np.int32), np.asarray(L0_ffn_gate_scale, np.float32),
            np.asarray(L0_ffn_up_w, np.int32), np.asarray(L0_ffn_up_scale, np.float32),
            np.asarray(L0_ffn_down_w, np.int32), np.asarray(L0_ffn_down_scale, np.float32))
    in_maps = [make_inputs(c, *args) for c in range(8)]
    nc = _get_nc()
    res = run_bass_kernel_spmd(nc, in_maps, core_ids=list(range(8)))
    out = np.empty((4, T, CD), dtype=np.float32)
    for c in range(8):
        b, par = c // 2, c % 2
        o = res.results[c]["out"]
        for oc, g in enumerate(OWN[par]):
            out[b, g * 512:(g + 1) * 512, :] = o[:, oc * 512:(oc + 1) * 512].T
    return out
